# version 26
# baseline (speedup 1.0000x reference)
import numpy as np
import ml_dtypes
import concourse.bass as bass
import concourse.mybir as mybir

F32 = mybir.dt.float32
BF16 = mybir.dt.bfloat16
AF = mybir.ActivationFunctionType
ALU = mybir.AluOpType
AX = mybir.AxisListType
BF = ml_dtypes.bfloat16

C_SHIFT = 12.0
MASK_NEG = -200.0
EPS = 1e-5
TG = 2176
TK = 2816
OWN0, OWN1 = 64, 2112

ENGS = ('sync', 'gpsimd', 'vector', 'scalar', 'tensor')


def chunks(T, step=512):
    out = []
    t = 0
    while t < T:
        out.append((t, min(step, T - t)))
        t += step
    return out


class P:
    def __init__(self, name="prog"):
        self.nc = bass.Bass("TRN2", target_bir_lowering=False)
        self.em = {e: [] for e in ENGS}
        self.semh = {}
        self.cnt = {}
        self.consts_needed = set()

    @staticmethod
    def canon(name):
        return name.split('_', 1)[1] if '_' in name else name

    def S(self, name):
        name = self.canon(name)
        if name not in self.semh:
            self.semh[name] = self.nc.alloc_semaphore("sem_" + name)
            self.cnt[name] = 0
        return self.semh[name]

    def val(self, sem):
        self.S(sem)
        return self.cnt[self.canon(sem)]

    def op(self, eng, f, sem=None, n=1):
        if sem is not None:
            h = self.S(sem)
            sem = self.canon(sem)
            self.cnt[sem] += n
            nn = n

            def g(e):
                f(e).then_inc(h, nn)
            self.em[eng].append(g)
            return self.cnt[sem]
        else:
            self.em[eng].append(lambda e: f(e))
            return None

    def wait(self, eng, sem, val):
        if val <= 0:
            return
        h = self.S(sem)
        self.em[eng].append(lambda e: e.wait_ge(h, val))

    def stage_mark(self):
        pass

    def barrier(self, engines=ENGS, name=None):
        vals = {}
        for eng in engines:
            vals[eng] = self.op(eng, lambda e: e.nop(), sem=f"SB{eng}")
        for eng in engines:
            for o in engines:
                if o != eng:
                    self.wait(eng, f"SB{o}", vals[o])

    def sbuf(self, name, shape, dtype):
        if not hasattr(self, '_tcache'):
            self._tcache = {}
        if name in self._tcache:
            return self._tcache[name]
        t = self.nc.alloc_sbuf_tensor(name, list(shape), dtype)
        self._tcache[name] = t
        return t

    def psum(self, name, shape, dtype=F32):
        return self.nc.alloc_psum_tensor(name, list(shape), dtype)

    def dram(self, name, shape, dtype, kind):
        return self.nc.dram_tensor(name, list(shape), dtype, kind=kind)

    def const(self, v):
        self.consts_needed.add(float(v))
        return float(v)

    def finalize_consts(self):
        vals = sorted(self.consts_needed)
        ct = self.sbuf("c_consts", [1, max(1, len(vals) + 1)], F32)
        pre = []
        for i, v in enumerate(vals):
            ap = ct[0:1, i:i + 1]
            self.nc.const_aps.register(F32, v, ap)
            pre.append(lambda e, ap=ap, vv=v: e.memset(ap, vv))
        h = self.S('init')
        pre.append(lambda e, ct=ct: e.memset(
            ct[0:1, len(vals):len(vals) + 1], 0.0).then_inc(h, 1))
        self.em['vector'] = pre + self.em['vector']

    def build(self):
        with self.nc.Block() as block:
            for eng in ENGS:
                if not self.em[eng]:
                    continue
                fns = self.em[eng]

                def runner(e, fns=fns):
                    for f in fns:
                        f(e)
                getattr(block, eng)(runner)
        return self.nc


def init_gate(p):
    p.S('init')
    p.cnt['init'] = 1
    for eng in ENGS:
        if eng != 'vector':
            p.wait(eng, 'init', 1)


def core_geom(c):
    b, h = c // 2, c % 2
    s0 = 2048 * h
    return b, h, s0, s0 - 640, s0 - 64


L0_QCHUNKS = [(0, 512), (512, 512), (1024, 512), (1536, 512), (2048, 128)]
L0_NKK = [9, 9, 9, 9, 6]
L1_QCHUNKS = [(64 + 512 * i, 512) for i in range(4)]


def l0_mask_slot(qc, kk):
    if qc == 0 and kk <= 4:
        return 9 + kk
    if qc == 1 and kk == 0:
        return 14
    if qc == 4 and kk >= 4:
        return 15 + (kk - 4)
    return kk


def l1_mask_slot(qc, kk):
    if qc == 0 and kk == 0:
        return 5
    if qc == 3 and kk == 4:
        return 6
    return kk


def build_l0_masks(c):
    _, h, s0, gk0, g0 = core_geom(c)
    out = np.full((17, 128, 512), MASK_NEG, np.float32)
    for kk in range(9):
        jj = np.arange(128)[:, None] + kk * 128
        ii = np.arange(512)[None, :]
        band = (jj >= ii + 64) & (jj <= ii + 576)
        out[kk] = np.where(band, -C_SHIFT, MASK_NEG)
    for qc, kk in ([(0, k) for k in range(5)] + [(1, 0), (4, 4), (4, 5)]):
        qq, nq = L0_QCHUNKS[qc]
        jj = np.arange(128)[:, None] + 4 * qc * 128 + kk * 128
        ii = np.arange(nq)[None, :] + qq
        band = (jj >= ii + 64) & (jj <= ii + 576)
        valid = band & (gk0 + jj >= 0)
        qpos = g0 + ii
        phantom = (qpos < 0) | (qpos >= 4096)
        valid = np.where(phantom, jj == ii + 576, valid)
        out[l0_mask_slot(qc, kk), :, :nq] = np.where(valid, -C_SHIFT, MASK_NEG)
    return out.astype(BF)


def build_l1_masks(c):
    _, h, s0, gk0, g0 = core_geom(c)
    out = np.full((7, 128, 512), MASK_NEG, np.float32)
    for qc, (qq, nq) in enumerate(L1_QCHUNKS):
        for kk in range(5):
            jj = np.arange(128)[:, None] + (qq - 64) + kk * 128
            ii = np.arange(nq)[None, :] + qq
            valid = (np.abs(ii - jj) <= 64) & (g0 + jj >= 0) & (g0 + jj < 4096)
            out[l1_mask_slot(qc, kk), :, :nq] = np.where(valid, -C_SHIFT, MASK_NEG)
    return out.astype(BF)


def build_rope(c):
    _, _, _, _, g0 = core_geom(c)
    half = 32
    inv_freq = (1.0 / (10000.0 ** (np.arange(half, dtype=np.float64) / half))).astype(np.float32)
    pos = (g0 + np.arange(TG)).astype(np.float32)
    ang = pos[None, :] * inv_freq[:, None]
    cos, sin = np.cos(ang).astype(np.float32), np.sin(ang).astype(np.float32)
    r = np.arange(128)
    f = r % 32
    Cfull = cos[f]
    sign = np.where((r % 64) < 32, -1.0, 1.0).astype(np.float32)
    Sfull = sin[f] * sign[:, None]
    perm = np.zeros((128, 128), np.float32)
    for m in range(128):
        base = (m // 64) * 64
        perm[base + (m - base + 32) % 64, m] = 1.0
    return Cfull, Sfull, perm.astype(BF)


def slice_pad_T(x_bse, b, start, length):
    S = x_bse.shape[1]
    out = np.zeros((length, x_bse.shape[2]), np.float32)
    lo, hi = max(0, start), min(S, start + length)
    if hi > lo:
        out[lo - start:hi - start] = x_bse[b, lo:hi]
    return np.ascontiguousarray(out.T)


def host_prep(ins):
    sh = {}
    sh['wqkv0T'] = np.ascontiguousarray(ins['L0_attn_w_qkv'].T)
    sh['wout0T'] = np.ascontiguousarray(ins['L0_attn_w_out'].T)
    sh['wg0T'] = np.ascontiguousarray(ins['L0_ffn_gate_w'].T)
    w1T = np.concatenate([np.transpose(ins['L0_ffn_w1'], (0, 2, 1)),
                          ins['L0_ffn_ws1'].T[None]], 0)
    w3T = np.concatenate([np.transpose(ins['L0_ffn_w3'], (0, 2, 1)),
                          ins['L0_ffn_ws3'].T[None]], 0)
    w2T = np.concatenate([np.transpose(ins['L0_ffn_w2'], (0, 2, 1)),
                          ins['L0_ffn_ws2'].T[None]], 0)
    sh['w1T'] = np.ascontiguousarray(w1T)
    sh['w3T'] = np.ascontiguousarray(w3T)
    sh['w2T'] = np.ascontiguousarray(w2T)
    sh['wqkv1T'] = np.ascontiguousarray(ins['L1_attn_Wqkv'].T)
    sh['bqkv'] = np.ascontiguousarray(ins['L1_attn_bqkv'].reshape(48, 128).T)
    sh['wo1T'] = np.ascontiguousarray(ins['L1_attn_Wo'].T)
    sh['bo'] = np.ascontiguousarray(ins['L1_attn_bo'].reshape(16, 128).T)
    sk = ins['L1_attn_sinks']
    sh['sinksT'] = np.ascontiguousarray(np.transpose(sk, (0, 2, 1)))
    sh['sinks_aug'] = np.ascontiguousarray(
        np.concatenate([sk, np.ones((32, 16, 1), np.float32)], 2))
    sh['wd1T'] = np.ascontiguousarray(ins['L1_ffn_weight'].T)
    sh['wg1T'] = np.ascontiguousarray(ins['L1_ffn_gate_w'].T)
    sh['ApackT'] = np.ascontiguousarray(ins['L1_ffn_A_w'].reshape(128, 2048).T)
    sh['Bpack'] = np.ascontiguousarray(
        np.transpose(ins['L1_ffn_B_w'], (0, 2, 1)).reshape(128, 2048))
    sh['ident'] = np.eye(128, dtype=np.float32)
    per = []
    for c in range(8):
        b, h, s0, gk0, g0 = core_geom(c)
        d = {}
        d['xT'] = slice_pad_T(ins['L0_attn_x'], b, gk0, TK)
        d['masks0'] = build_l0_masks(c)
        d['masks1'] = build_l1_masks(c)
        Cf, Sf, perm = build_rope(c)
        d['ropeC'] = Cf
        d['ropeS'] = Sf
        d['perm'] = perm.astype(BF)
        per.append(d)
    return sh, per


class Ctx:
    pass


def common_alloc(p, ident_hbm):
    cx = Ctx()
    cx.ident = p.sbuf("cx_ident", [128, 128], BF16)
    cx.ones_col = p.sbuf("cx_ones_col", [128, 1], BF16)
    cx.ones_row = p.sbuf("cx_ones_row", [1, 128], F32)
    cx.ps = [p.psum(f"cx_ps{i}", [128, 512], F32) for i in range(7)]
    cx.psbf = p.psum("cx_psbf", [128, 512], BF16)
    identf = p.sbuf("cx_identf", [128, 128], F32)
    cx.identf = identf
    p.op('gpsimd', lambda e: e.dma_start(out=identf[:], in_=ident_hbm[:]),
         sem='ALI', n=16)
    p.wait('vector', 'ALI', 16)
    p.op('vector', lambda e: e.tensor_copy(cx.ident[:], identf[:]))
    p.op('vector', lambda e: e.memset(cx.ones_col[:], 1.0))
    p.op('vector', lambda e: e.memset(cx.ones_row[:], 1.0))
    cx.expbias = p.sbuf("cx_expbias", [128, 1], F32)
    p.op('vector', lambda e: e.memset(cx.expbias[:], -C_SHIFT))
    cx.neg1 = p.sbuf("cx_neg1", [128, 1], F32)
    v = p.op('vector', lambda e: e.memset(cx.neg1[:], -1.0), sem='cominit')
    p.wait('vector', 'cominit', v)
    return cx


def emit_ln(p, cx, src_hbm, T, lnx, pref, bank0=0, col0=0):
    nk = 16
    ps_sum = cx.ps[bank0][0:1, :]
    ps_sq = cx.ps[bank0 + 1][0:1, :]
    ps_mb = cx.ps[bank0 + 2]
    ps_rb = cx.ps[bank0 + 3]
    xin = p.sbuf("lnsc_xin", [128, 5 * 512], F32)
    xbf = p.sbuf("lnsc_xbf", [128, 2 * 512], BF16)
    sqbf = p.sbuf("lnsc_sqbf", [128, 2 * 512], BF16)
    rows = p.sbuf("lnsc_rows", [1, 2 * 512], F32)
    mrow = p.sbuf("lnsc_mrow", [128, 512], F32)
    rrow = p.sbuf("lnsc_rrow", [128, 512], F32)
    inv = p.const(1.0 / 2048.0)
    eps = p.const(EPS)
    neghalf = p.const(-0.5)
    SA, SC, SM, SV, SP = (f'{pref}_ac', f'{pref}_cv', f'{pref}_mm',
                          f'{pref}_ve', f'{pref}_pl')
    for ci, (t0, nt) in enumerate(chunks(T)):
        p.wait('tensor', SC, p.val(SC))
        if ci > 0:
            p.wait('gpsimd', SV, p.val(SV))
            p.wait('sync', SV, p.val(SV))
        sm0 = p.val(SM)
        for k in range(nk):
            sl = k % 4
            kp = k % 2
            q = 'gpsimd' if k % 2 == 0 else 'sync'
            if k >= 4:
                p.wait(q, SM, sm0 + 2 * (k - 4) + 2)
            alv = p.op(q, lambda e, k=k, sl=sl, t0=t0, nt=nt: e.dma_start(
                out=xin[:, sl * 512:sl * 512 + nt],
                in_=src_hbm[k * 128:(k + 1) * 128, col0 + t0:col0 + t0 + nt]),
                sem=f'{pref}_L{sl}', n=16)
            p.wait('scalar', f'{pref}_L{sl}', alv)
            p.wait('vector', f'{pref}_L{sl}', alv)
            if k >= 2:
                p.wait('scalar', SM, sm0 + 2 * (k - 2) + 2)
                p.wait('vector', SM, sm0 + 2 * (k - 2) + 2)
            p.op('scalar', lambda e, sl=sl, kp=kp, nt=nt: e.square(
                sqbf[:, kp * 512:kp * 512 + nt], xin[:, sl * 512:sl * 512 + nt]),
                sem=SA)
            p.op('vector', lambda e, sl=sl, kp=kp, nt=nt: e.tensor_copy(
                xbf[:, kp * 512:kp * 512 + nt], xin[:, sl * 512:sl * 512 + nt]),
                sem=SC)
            p.wait('tensor', SA, p.val(SA))
            p.wait('tensor', SC, p.val(SC))
            p.op('tensor', lambda e, k=k, kp=kp, nt=nt: e.matmul(
                ps_sum[:, 0:nt], cx.ones_col[:], xbf[:, kp * 512:kp * 512 + nt],
                start=(k == 0), stop=(k == nk - 1)))
            p.op('tensor', lambda e, k=k, kp=kp, nt=nt: e.matmul(
                ps_sq[:, 0:nt], cx.ones_col[:], sqbf[:, kp * 512:kp * 512 + nt],
                start=(k == 0), stop=(k == nk - 1)), sem=SM, n=2)
        p.wait('vector', SM, p.val(SM))
        v = p.op('vector', lambda e, nt=nt: e.tensor_scalar(
            rows[0:1, 0:nt], ps_sum[:, 0:nt], inv, None, op0=ALU.mult), sem=SC)
        v = p.op('vector', lambda e, nt=nt: e.tensor_scalar(
            rows[0:1, 512:512 + nt], ps_sq[:, 0:nt], inv, None, op0=ALU.mult), sem=SC)
        p.wait('vector', SC, v)
        v = p.op('vector', lambda e, nt=nt: e.tensor_tensor(
            mrow[0:1, 0:nt], rows[0:1, 0:nt], rows[0:1, 0:nt],
            op=ALU.mult), sem=SC)
        p.wait('vector', SC, v)
        v = p.op('vector', lambda e, nt=nt: e.tensor_tensor(
            rows[0:1, 512:512 + nt], rows[0:1, 512:512 + nt],
            mrow[0:1, 0:nt], op=ALU.subtract), sem=SC)
        p.wait('scalar', SC, p.val(SC))
        sa1 = p.op('scalar', lambda e, nt=nt: e.activation(
            mrow[0:1, 0:nt], rows[0:1, 512:512 + nt], AF.Ln,
            bias=eps), sem=SA)
        p.wait('scalar', SA, sa1)
        p.op('scalar', lambda e, nt=nt: e.activation(
            rows[0:1, 512:512 + nt], mrow[0:1, 0:nt], AF.Exp,
            scale=neghalf), sem=SA)
        p.wait('vector', SA, p.val(SA))
        p.op('vector', lambda e: e.nop(), sem=SC)
        p.wait('tensor', SC, p.val(SC))
        p.wait('tensor', SP, p.val(SP))
        p.op('tensor', lambda e, nt=nt: e.matmul(
            ps_mb[:, 0:nt], cx.ones_row[:, :], rows[0:1, 0:nt], start=True, stop=True))
        p.op('tensor', lambda e, nt=nt: e.matmul(
            ps_rb[:, 0:nt], cx.ones_row[:, :], rows[0:1, 512:512 + nt],
            start=True, stop=True), sem=SM)
        p.wait('vector', SM, p.val(SM))
        p.wait('vector', SP, p.val(SP))
        p.op('vector', lambda e, nt=nt: e.tensor_copy(mrow[:, 0:nt], ps_mb[:, 0:nt]))
        v = p.op('vector', lambda e, nt=nt: e.tensor_copy(rrow[:, 0:nt], ps_rb[:, 0:nt]),
             sem=SC)
        p.wait('vector', SC, v)
        scv = p.val(SC)
        sv0 = p.val(SV)
        p.wait('gpsimd', SC, scv)
        p.wait('gpsimd', SV, sv0)
        p.wait('sync', SC, scv)
        p.wait('sync', SV, sv0)
        lvs = []
        for k in range(4):
            q = 'gpsimd' if k % 2 == 0 else 'sync'
            lvs.append(p.op(q, lambda e, k=k, t0=t0, nt=nt: e.dma_start(
                out=xin[:, (k % 4) * 512:(k % 4) * 512 + nt],
                in_=src_hbm[k * 128:(k + 1) * 128, col0 + t0:col0 + t0 + nt]),
                sem=f'{pref}_L{k % 4}', n=16))
        for k in range(nk):
            sl = k % 4
            p.wait('gpsimd', f'{pref}_L{sl}', lvs[k])
            pl = p.op('gpsimd', lambda e, sl=sl, nt=nt: e.tensor_sub(
                xin[:, sl * 512:sl * 512 + nt], xin[:, sl * 512:sl * 512 + nt],
                mrow[:, 0:nt]), sem=SP)
            p.wait('vector', SP, pl)
            p.op('vector', lambda e, k=k, sl=sl, nt=nt, t0=t0: e.tensor_tensor(
                lnx[:, k * T + t0:k * T + t0 + nt], xin[:, sl * 512:sl * 512 + nt],
                rrow[:, 0:nt], op=ALU.mult), sem=SV)
            if k + 4 < nk:
                q = 'gpsimd' if k % 2 == 0 else 'sync'
                p.wait(q, SV, sv0 + k + 1)
                lvs.append(p.op(q, lambda e, k=k, t0=t0, nt=nt: e.dma_start(
                    out=xin[:, ((k + 4) % 4) * 512:((k + 4) % 4) * 512 + nt],
                    in_=src_hbm[(k + 4) * 128:(k + 5) * 128,
                                col0 + t0:col0 + t0 + nt]),
                    sem=f'{pref}_L{(k + 4) % 4}', n=16))
    return p.val(SV)


def load_wcol(p, pref, wsrc, w32ap, wbfap, ncols, nk=16, queue='sync'):
    nslab = (nk + 3) // 4
    for s in range(min(nslab, (nk + 3) // 4)):
        k0 = s * 4
        kn = min(4, nk - k0)
        p.op(queue, lambda e, k0=k0, kn=kn, s=s: e.dma_start(
            out=w32ap[:, k0 * ncols:(k0 + kn) * ncols]
                .rearrange("p (k f) -> p k f", k=kn),
            in_=wsrc[k0 * 128:(k0 + kn) * 128, :]
                .rearrange("(k p) f -> p k f", p=128)), sem='WL', n=16)
    wl = p.val('WL')
    p.wait('scalar', 'WL', wl)
    p.op('scalar', lambda e: e.copy(wbfap[:, 0:nk * ncols], w32ap[:, 0:nk * ncols]),
         sem=f'{pref}_cast')
    return wl, p.val(f'{pref}_cast')


def emit_l0_attn(p, cx, lnx, wqkv, masks_hbm, attnT_hbm, pref='a0', nheads=16, phases='ABCD'):
    msk = p.sbuf(f'{pref}_msk', [128, 17 * 512], BF16)
    p.op('gpsimd', lambda e: e.dma_start(
        out=msk.rearrange("p (s f) -> p s f", s=17),
        in_=masks_hbm.rearrange("s p f -> p s f")), sem='ALM', n=16)
    mskv = p.val('ALM')
    qh = p.sbuf(f'{pref}_qh', [128, TG], BF16)
    kh = p.sbuf(f'{pref}_kh', [128, TK], BF16)
    vh = p.sbuf(f'{pref}_vh', [128, TK], BF16)
    v_tm = p.sbuf(f'{pref}_vtm', [128, 22 * 128], BF16)
    w32 = p.sbuf(f'{pref}_w32', [128, 4 * 512], F32)
    wbf = p.sbuf(f'{pref}_wbf', [128, 2 * 2048], BF16)
    slab_q = [0]
    qc_vals = []
    sexp = p.sbuf(f'{pref}_sexp', [128, 4 * 512], F32)
    pT = p.sbuf(f'{pref}_pT', [128, 4 * 512], BF16)
    rrow1 = p.sbuf(f'{pref}_rrow1', [1, 512], F32)
    lnt = p.sbuf(f'{pref}_lnt', [128, 512], F32)
    qscl = p.sbuf(f'{pref}_qscl', [128, 1], F32)
    v = p.op('vector', lambda e: e.memset(qscl[:], float(1.0 / np.sqrt(128.0))),
             sem=f'{pref}_cm')
    p.wait('vector', f'{pref}_cm', v)
    rb = p.sbuf(f'{pref}_rb', [128, 512], F32)
    att_out = p.sbuf(f'{pref}_att', [128, 2 * TG], BF16)
    SCAST, SMMD, SEVQ = f'{pref}_cast', f'{pref}_mmd', f'{pref}_evq'
    SMS, SCM, SAE, SMP, SCR, SMB, SST = (f'{pref}_ms', f'{pref}_cm', f'{pref}_ae',
                                         f'{pref}_mp', f'{pref}_cr', f'{pref}_mb', 'ST')
    p.wait('vector', 'ALM', mskv)
    cast_done = []
    mmdone = []
    st_after_head = []
    kki = 0
    msv, cmv, aev, mpv = {}, {}, {}, {}
    evq = 0
    tr_copy_done = []
    for h in range(nheads):
        vev = []
        for idx, part in enumerate((1, 2, 0)):
            pseq = h * 3 + idx
            par = pseq % 2
            col = part * 2048 + h * 128
            first_cast = True
            for s in range(4):
                sl = slab_q[0] % 4
                if slab_q[0] >= 4:
                    p.wait('sync', SCAST, qc_vals[slab_q[0] - 4])
                dv = p.op('sync', lambda e, s=s, sl=sl, col=col: e.dma_start(
                    out=w32[:, sl * 512:(sl + 1) * 512]
                        .rearrange("p (k f) -> p k f", k=4),
                    in_=wqkv[s * 512:(s + 1) * 512, col:col + 128]
                        .rearrange("(k p) f -> p k f", p=128)),
                    sem=f'{pref}_WS{sl}', n=16)
                p.wait('scalar', f'{pref}_WS{sl}', dv)
                if first_cast and pseq >= 2:
                    p.wait('scalar', SMMD, mmdone[pseq - 2])
                first_cast = False
                qc_vals.append(p.op('scalar', lambda e, sl=sl, s=s, par=par: e.copy(
                    wbf[:, par * 2048 + s * 512:par * 2048 + (s + 1) * 512],
                    w32[:, sl * 512:(sl + 1) * 512]), sem=SCAST))
                slab_q[0] += 1
            cast_done.append(p.val(SCAST))
            p.wait('tensor', SCAST, cast_done[pseq])
            T = TG if part == 0 else TK
            roff = 576 if part == 0 else 0
            dst = (qh, kh, vh)[part]
            nchq = len(chunks(T))
            for ci, (t0, nt) in enumerate(chunks(T)):
                bk = ci % 2
                p.wait('tensor', SEVQ, evq if (ci == 0 and idx == 0) else evq - 1)
                for k in range(16):
                    p.op('tensor', lambda e, k=k, par=par, bk=bk, t0=t0, nt=nt,
                         roff=roff, T=T: e.matmul(
                             cx.ps[bk][:, 0:nt],
                             wbf[:, par * 2048 + k * 128:par * 2048 + (k + 1) * 128],
                             lnx[:, k * TK + roff + t0:k * TK + roff + t0 + nt],
                             start=(k == 0), stop=(k == 15)),
                         sem=(SMMD if k == 15 else None))
                mmv = p.val(SMMD)
                p.wait('vector', SMMD, mmv)
                if part == 0:
                    evq = p.op('vector', lambda e, bk=bk, t0=t0, nt=nt, dst=dst:
                               e.tensor_scalar(dst[:, t0:t0 + nt], cx.ps[bk][:, 0:nt],
                                               qscl[:, 0:1], None, op0=ALU.mult),
                               sem=SEVQ)
                else:
                    evq = p.op('vector', lambda e, bk=bk, t0=t0, nt=nt, dst=dst:
                               e.tensor_copy(dst[:, t0:t0 + nt], cx.ps[bk][:, 0:nt]),
                               sem=SEVQ)
                    if part == 2:
                        vev.append(evq)
                if part == 0:
                    kt_tgt = (22 * (ci + 1) + nchq - 1) // nchq
                    while len(tr_copy_done) < h * 22 + min(22, kt_tgt):
                        kt = len(tr_copy_done) - h * 22
                        gt = len(tr_copy_done)
                        tslot = gt % 2
                        tbank = cx.psbf[:, 0:128] if tslot == 0 else \
                            cx.ps[6].bitcast(BF16)[:, 0:128]
                        p.wait('tensor', SEVQ, vev[((kt + 1) * 128 - 1) // 512])
                        if gt >= 2:
                            p.wait('tensor', SCM, tr_copy_done[gt - 2])
                        p.op('tensor', lambda e, kt=kt, tbank=tbank: e.matmul(
                            tbank, vh[:, kt * 128:(kt + 1) * 128],
                            cx.ident[:], start=True, stop=True,
                            is_transpose=True), sem=SMS)
                        p.wait('vector', SMS, p.val(SMS))
                        tr_copy_done.append(p.op(
                            'vector', lambda e, kt=kt, tbank=tbank: e.tensor_copy(
                                v_tm[:, kt * 128:(kt + 1) * 128].bitcast(F32),
                                tbank.bitcast(F32)), sem=SCM))
            mmdone.append(p.val(SMMD))
        vtm_done = p.val(SCM)
        if 'C' not in phases:
            continue
        p.wait('tensor', SCM, vtm_done)
        tiles = []
        for qc, (qq, nq) in enumerate(L0_QCHUNKS):
            for kk in range(L0_NKK[qc]):
                tiles.append((qc, kk, L0_NKK[qc], qq, nq))
        LA = 3
        ntl = len(tiles)
        kki0 = kki
        for it in range(ntl + LA):
            if it < ntl:
                qc, kk, nkk, qq, nq = tiles[it]
                kg = kki0 + it
                kp = kg % 4
                kbase = (4 * qc + kk) * 128
                slot = l0_mask_slot(qc, kk)
                if kg >= 4:
                    p.wait('tensor', SCM, cmv[kg - 4])
                msv[kg] = p.op('tensor', lambda e, kp=kp, kbase=kbase, qq=qq,
                               nq=nq: e.matmul(
                    cx.ps[kp][:, 0:nq], kh[:, kbase:kbase + 128],
                    qh[:, qq:qq + nq], start=True, stop=True), sem=SMS)
                p.wait('vector', SMS, msv[kg])
                if kg >= 4:
                    p.wait('vector', SAE, aev[kg - 4])
                cmv[kg] = p.op('vector', lambda e, kp=kp, slot=slot, nq=nq:
                               e.tensor_tensor(
                    sexp[:, kp * 512:kp * 512 + nq], cx.ps[kp][:, 0:nq],
                    msk[:, slot * 512:slot * 512 + nq], op=ALU.add), sem=SCM)
                p.wait('scalar', SCM, cmv[kg])
                if kg >= 4:
                    p.wait('scalar', SMP, mpv[kg - 4])
                aev[kg] = p.op('scalar', lambda e, kp=kp, nq=nq: e.activation(
                    pT[:, kp * 512:kp * 512 + nq], sexp[:, kp * 512:kp * 512 + nq],
                    AF.Exp), sem=SAE)
            jt = it - LA
            if jt < 0:
                continue
            qc, kk, nkk, qq, nq = tiles[jt]
            kg = kki0 + jt
            kp = kg % 4
            kbase = (4 * qc + kk) * 128
            p.wait('tensor', SAE, aev[kg])
            if kk == 0 and qc > 0:
                p.wait('tensor', SEVQ, evq)
            p.op('tensor', lambda e, kp=kp, kbase=kbase, nq=nq, kk=kk, nkk=nkk:
                 e.matmul(
                     cx.ps[4][:, 0:nq], v_tm[:, kbase:kbase + 128],
                     pT[:, kp * 512:kp * 512 + nq],
                     start=(kk == 0), stop=(kk == nkk - 1)))
            p.op('tensor', lambda e, kp=kp, nq=nq, kk=kk, nkk=nkk: e.matmul(
                cx.ps[5][0:1, 0:nq], cx.ones_col[:],
                pT[:, kp * 512:kp * 512 + nq],
                start=(kk == 0), stop=(kk == nkk - 1)),
                sem=SMP, n=2)
            mpv[kg] = p.val(SMP)
            if kk == nkk - 1:
                p.wait('vector', SMP, mpv[kg])
                v = p.op('vector', lambda e, nq=nq: e.tensor_copy(
                    rrow1[0:1, 0:nq], cx.ps[5][0:1, 0:nq]), sem=SCR)
                p.wait('tensor', SCR, v)
                p.op('tensor', lambda e, nq=nq: e.matmul(
                    cx.ps[6][:, 0:nq], cx.ones_row[:, :], rrow1[0:1, 0:nq],
                    start=True, stop=True), sem=SMB)
                p.wait('scalar', SMB, p.val(SMB))
                av = p.op('scalar', lambda e, nq=nq: e.activation(
                    lnt[:, 0:nq], cx.ps[6][:, 0:nq], AF.Ln), sem=SAE)
                p.wait('scalar', SAE, av)
                av = p.op('scalar', lambda e, nq=nq: e.activation(
                    rb[:, 0:nq], lnt[:, 0:nq], AF.Exp, scale=cx.neg1[:, 0:1]),
                    sem=SAE)
                p.wait('vector', SAE, av)
                if h >= 2:
                    p.wait('vector', f'{pref}_S{h % 2}', st_after_head[h - 2])
                evq = p.op('vector', lambda e, nq=nq, qq=qq, h=h: e.tensor_tensor(
                    att_out[:, (h % 2) * TG + qq:(h % 2) * TG + qq + nq],
                    cx.ps[4][:, 0:nq], rb[:, 0:nq], op=ALU.mult), sem=SEVQ)
        kki = kki0 + ntl
        if 'D' not in phases:
            continue
        p.wait('gpsimd', SEVQ, evq)
        p.op('gpsimd', lambda e, h=h: e.dma_start(
            out=attnT_hbm[h * 128:(h + 1) * 128, :],
            in_=att_out[:, (h % 2) * TG:(h % 2) * TG + TG]),
            sem=f'{pref}_S{h % 2}', n=16)
        st_after_head.append(p.val(f'{pref}_S{h % 2}'))
    return (p.val(f'{pref}_S0'), p.val(f'{pref}_S1'))


def load_resident(p, hbm, T, nk, dst, sem):
    for k in range(nk):
        v = p.op('gpsimd', lambda e, k=k: e.dma_start(
            out=dst[:, k * T:(k + 1) * T], in_=hbm[k * 128:(k + 1) * 128, :]),
            sem=sem, n=16)
    return v


def emit_dense2(p, cx, pref, wsrcs, X_ap, Tch, nk, evac, extra_mm=None,
                bank0=0):
    nch = len(Tch)
    nslab = (nk + 3) // 4
    w32 = p.sbuf(f"{pref}_w32", [128, 2 * nk * 128], F32)
    wbf = p.sbuf(f"{pref}_wbf", [128, 2 * nk * 128], BF16)
    SCAST, SMMD, SEV = f'{pref}_cast', f'{pref}_mmd', f'{pref}_ev'
    sev_map = {}
    cast_vals = []
    mmd_vals = []
    for oi, wsrc in enumerate(wsrcs):
        par = oi % 2
        if oi >= 2:
            p.wait('sync', SCAST, cast_vals[oi - 2])
        for s in range(nslab):
            k0 = s * 4
            kn = min(4, nk - k0)
            p.op('sync', lambda e, k0=k0, kn=kn, par=par, wsrc=wsrc: e.dma_start(
                out=w32[:, par * nk * 128 + k0 * 128:par * nk * 128 + (k0 + kn) * 128]
                    .rearrange("p (k f) -> p k f", k=kn),
                in_=wsrc[k0 * 128:(k0 + kn) * 128, :]
                    .rearrange("(k p) f -> p k f", p=128)), sem=f'{pref}_W{par}', n=16)
        p.wait('scalar', f'{pref}_W{par}', p.val(f'{pref}_W{par}'))
        if oi >= 2:
            p.wait('scalar', SMMD, mmd_vals[oi - 2])
        cast_vals.append(p.op('scalar', lambda e, par=par: e.copy(
            wbf[:, par * nk * 128:(par + 1) * nk * 128],
            w32[:, par * nk * 128:(par + 1) * nk * 128]), sem=SCAST))
        p.wait('tensor', SCAST, cast_vals[oi])
        for ci, (t0, nt) in enumerate(Tch):
            if oi >= 1:
                p.wait('tensor', SEV, sev_map[(oi - 1, ci)])
            for k in range(nk):
                last = (k == nk - 1) and (extra_mm is None)
                p.op('tensor', lambda e, k=k, par=par, ci=ci, t0=t0, nt=nt,
                     last=last: e.matmul(
                         cx.ps[bank0 + ci][:, 0:nt],
                         wbf[:, par * nk * 128 + k * 128:par * nk * 128 + (k + 1) * 128],
                         X_ap(k, t0, nt), start=(k == 0), stop=last),
                     sem=(SMMD if last else None))
            if extra_mm is not None:
                extra_mm(p, oi, ci, cx.ps[bank0 + ci], t0, nt, SMMD)
            fv = evac(p, oi, cx.ps[bank0 + ci], t0, nt, ci)
            sev_map[(oi, ci)] = fv if fv else p.val(SEV)
        mmd_vals.append(p.val(SMMD))
    return p.val(SEV)


def emit_ffn(p, cx, pref, x1_hbm, w1_hbm, w3_hbm, w2_hbm, wg_hbm, sel_hbm,
             x2_hbm, stats_in=None, stats_out=None):
    TH = 1088
    lnx1 = p.sbuf(f'{pref}_lnx', [128, 16 * TH], BF16)
    combT = p.sbuf(f'{pref}_comb', [8, TH], F32)
    cbuf = p.sbuf(f'{pref}_cbuf', [128, TH], F32)
    g_sb = p.sbuf(f'{pref}_g', [128, 8 * TH], BF16)
    yacc = p.sbuf(f'{pref}_y', [128, 16 * TH], F32)
    sil = p.sbuf(f'{pref}_sil', [128, 512], F32)
    tmp = p.sbuf(f'{pref}_tmp', [128, 512], F32)
    selb = p.sbuf(f'{pref}_selb', [8, 8 * 128], F32)
    hw32 = p.sbuf(f'{pref}_hw32', [128, 4 * 512], F32)
    hwbf = p.sbuf(f'{pref}_hwbf', [128, 2 * 2 * 2048 + 2 * 1024], BF16)
    slab_i = [0]
    xr = p.sbuf(f'{pref}_xr', [128, 3 * 512], F32)
    x2o = p.sbuf(f'{pref}_x2o', [128, 3 * 512], F32)
    p.op('gpsimd', lambda e: e.dma_start(
        out=selb.rearrange("p (e f) -> p e f", e=8),
        in_=sel_hbm.rearrange("e p f -> p e f")), sem=f'{pref}_SEL', n=16)
    p.wait('tensor', f'{pref}_SEL', 16)
    SMMD, SEV, SAC, SCV, SCB = (f'{pref}_mmd', f'{pref}_ev', f'{pref}_ac',
                                f'{pref}_cv', f'{pref}_cbb')
    Tch = chunks(TH)
    nch = len(Tch)
    ost = {0: [], 1: [], 2: []}
    gi = [0]
    hev = {}
    wev = {}
    hc_vals = []
    for half in range(2):
        col0 = half * TH
        if half == 1:
            for s in (0, 1, 2):
                if ost[s]:
                    p.wait('gpsimd', f'{pref}_O{s}', ost[s][-1])
            p.barrier(('gpsimd', 'vector', 'scalar', 'tensor', 'sync'))
        if stats_in is not None:
            lnve = emit_ln_apply(p, cx, x1_hbm, TH, lnx1, f'{pref}n{half}',
                                 stats_in, TG, col0=col0)
        else:
            lnve = emit_ln(p, cx, x1_hbm, TH, lnx1, f'{pref}n{half}', col0=col0)
        for eng in ('tensor', 'vector', 'scalar'):
            p.wait(eng, f'{pref}n{half}_ve', lnve)
        cb = emit_gate(p, cx, f'{pref}g{half}', lnx1, TH, wg_hbm, combT,
                       l1_norm=False)
        for eng in ('tensor', 'vector'):
            p.wait(eng, f'{pref}g{half}_cb', cb)
        for e in range(9):
            if e < 8:
                for ci, (t0, nt) in enumerate(Tch):
                    vb = p.wait('tensor', SCV, p.val(SCV))
                    p.op('tensor', lambda e2, e=e, t0=t0, nt=nt: e2.matmul(
                        cx.ps[6][:, 0:nt], selb[0:8, e * 128:(e + 1) * 128],
                        combT[0:8, t0:t0 + nt], start=True, stop=True),
                        sem=f'{pref}_mb')
                    p.wait('vector', f'{pref}_mb', p.val(f'{pref}_mb'))
                    vv = p.op('vector', lambda e2, t0=t0, nt=nt: e2.tensor_copy(
                        cbuf[:, t0:t0 + nt], cx.ps[6][:, 0:nt]), sem=SCV)
                    p.wait('vector', SCV, vv)
            p.wait('tensor', f'{pref}_yev', p.val(f'{pref}_yev'))
            for ot in range(8):
                pi = (half * 9 + e) * 8 + ot
                par = pi % 2
                po = par * 2 * 2048
                first_cast = True
                for wi, wh in enumerate((w1_hbm, w3_hbm)):
                    for s in range(4):
                        sl = slab_i[0] % 4
                        if slab_i[0] >= 4:
                            p.wait('sync', f'{pref}_hcast', hc_vals[slab_i[0] - 4])
                        dv = p.op('sync', lambda e2, s=s, sl=sl, e=e, ot=ot, wh=wh:
                                  e2.dma_start(
                                      out=hw32[:, sl * 512:(sl + 1) * 512]
                                      .rearrange("p (k f) -> p k f", k=4),
                                      in_=wh[e, s * 512:(s + 1) * 512,
                                             ot * 128:(ot + 1) * 128]
                                      .rearrange("(k p) f -> p k f", p=128)),
                                  sem=f'{pref}_HS{sl}', n=16)
                        p.wait('scalar', f'{pref}_HS{sl}', dv)
                        if first_cast and pi >= 2:
                            p.wait('scalar', SMMD, p.val(SMMD) - nch)
                        first_cast = False
                        hc_vals.append(p.op('scalar', lambda e2, sl=sl, wi=wi, s=s,
                                            po=po: e2.copy(
                            hwbf[:, po + wi * 2048 + s * 512:
                                 po + wi * 2048 + (s + 1) * 512],
                            hw32[:, sl * 512:(sl + 1) * 512]), sem=f'{pref}_hcast'))
                        slab_i[0] += 1
                p.wait('tensor', f'{pref}_hcast', p.val(f'{pref}_hcast'))
                for ci, (t0, nt) in enumerate(Tch):
                    if (pi - 1, ci) in hev:
                        p.wait('tensor', SEV, hev[(pi - 1, ci)])
                    for k in range(16):
                        p.op('tensor', lambda e2, k=k, po=po, ci=ci, t0=t0, nt=nt:
                             e2.matmul(
                                 cx.ps[ci][:, 0:nt],
                                 hwbf[:, po + k * 128:po + (k + 1) * 128],
                                 lnx1[:, k * TH + t0:k * TH + t0 + nt],
                                 start=(k == 0), stop=(k == 15)))
                    for k in range(16):
                        p.op('tensor', lambda e2, k=k, po=po, ci=ci, t0=t0, nt=nt:
                             e2.matmul(
                                 cx.ps[3 + ci][:, 0:nt],
                                 hwbf[:, po + 2048 + k * 128:po + 2048 + (k + 1) * 128],
                                 lnx1[:, k * TH + t0:k * TH + t0 + nt],
                                 start=(k == 0), stop=(k == 15)),
                             sem=(SMMD if k == 15 else None))
                    mmv = p.val(SMMD)
                    p.wait('scalar', SMMD, mmv)
                    p.wait('scalar', SCV, p.val(SCV))
                    av = p.op('scalar', lambda e2, ci=ci, nt=nt: e2.activation(
                        sil[:, 0:nt], cx.ps[ci][:, 0:nt], AF.Silu), sem=SAC)
                    p.wait('vector', SAC, av)
                    p.wait('vector', SEV, p.val(SEV))
                    vv = p.op('vector', lambda e2, ci=ci, nt=nt: e2.tensor_tensor(
                        tmp[:, 0:nt], cx.ps[3 + ci][:, 0:nt],
                        sil[:, 0:nt], op=ALU.mult), sem=SCV)
                    p.wait('vector', SCV, vv)
                    if e < 8:
                        p.op('vector', lambda e2, ot=ot, t0=t0, nt=nt:
                             e2.tensor_tensor(
                                 g_sb[:, ot * TH + t0:ot * TH + t0 + nt],
                                 tmp[:, 0:nt], cbuf[:, t0:t0 + nt],
                                 op=ALU.mult), sem=SEV)
                    else:
                        p.op('vector', lambda e2, ot=ot, t0=t0, nt=nt:
                             e2.tensor_copy(
                                 g_sb[:, ot * TH + t0:ot * TH + t0 + nt],
                                 tmp[:, 0:nt]), sem=SEV)
                    hev[(pi, ci)] = p.val(SEV)
                    gi[0] += 1
            p.wait('tensor', SEV, p.val(SEV))
            for o2 in range(16):
                pi2 = (half * 9 + e) * 16 + o2
                par = pi2 % 2
                po = par * 1024
                first_cast = True
                for s in range(2):
                    sl = slab_i[0] % 4
                    if slab_i[0] >= 4:
                        p.wait('sync', f'{pref}_hcast', hc_vals[slab_i[0] - 4])
                    dv = p.op('sync', lambda e2, s=s, sl=sl, e=e, o2=o2: e2.dma_start(
                        out=hw32[:, sl * 512:(sl + 1) * 512]
                        .rearrange("p (k f) -> p k f", k=4),
                        in_=w2_hbm[e, s * 512:(s + 1) * 512, o2 * 128:(o2 + 1) * 128]
                        .rearrange("(k p) f -> p k f", p=128)),
                        sem=f'{pref}_HS{sl}', n=16)
                    p.wait('scalar', f'{pref}_HS{sl}', dv)
                    if first_cast and pi2 >= 2:
                        p.wait('scalar', f'{pref}_yev', p.val(f'{pref}_yev') - nch)
                    first_cast = False
                    hc_vals.append(p.op('scalar', lambda e2, sl=sl, s=s, po=po:
                                        e2.copy(
                        hwbf[:, 8192 + po + s * 512:8192 + po + (s + 1) * 512],
                        hw32[:, sl * 512:(sl + 1) * 512]), sem=f'{pref}_hcast'))
                    slab_i[0] += 1
                p.wait('tensor', f'{pref}_hcast', p.val(f'{pref}_hcast'))
                for ci, (t0, nt) in enumerate(Tch):
                    if (pi2 - 1, ci) in wev:
                        p.wait('tensor', f'{pref}_yev', wev[(pi2 - 1, ci)])
                    for k in range(8):
                        p.op('tensor', lambda e2, k=k, po=po, ci=ci, t0=t0, nt=nt:
                             e2.matmul(
                                 cx.ps[ci][:, 0:nt],
                                 hwbf[:, 8192 + po + k * 128:
                                      8192 + po + (k + 1) * 128],
                                 g_sb[:, k * TH + t0:k * TH + t0 + nt],
                                 start=(k == 0), stop=(k == 7)),
                             sem=(f'{pref}_wmm' if k == 7 else None))
                    p.wait('vector', f'{pref}_wmm', p.val(f'{pref}_wmm'))
                    first = (e == 0)
                    if first:
                        p.op('vector', lambda e2, o2=o2, t0=t0, nt=nt, ci=ci:
                             e2.tensor_copy(
                                 yacc[:, o2 * TH + t0:o2 * TH + t0 + nt],
                                 cx.ps[ci][:, 0:nt]), sem=f'{pref}_yev')
                        wev[(pi2, ci)] = p.val(f'{pref}_yev')
                    else:
                        p.op('vector', lambda e2, o2=o2, t0=t0, nt=nt, ci=ci:
                             e2.tensor_tensor(
                                 yacc[:, o2 * TH + t0:o2 * TH + t0 + nt],
                                 cx.ps[ci][:, 0:nt],
                                 yacc[:, o2 * TH + t0:o2 * TH + t0 + nt],
                                 op=ALU.add), sem=f'{pref}_yev')
                    wev[(pi2, ci)] = p.val(f'{pref}_yev')
        yevf = p.val(f'{pref}_yev')
        xb2 = p.sbuf('lnsc_xbf', [128, 2 * 512], BF16)
        qb2 = p.sbuf('lnsc_sqbf', [128, 2 * 512], BF16)
        srw = p.sbuf('lnsc_srows', [1, 2 * 512], F32)
        for ci, (t0, nt) in enumerate(Tch):
            for o in range(16):
                g2 = gi[0]
                sl = g2 % 3
                par2 = g2 % 3
                ps2 = g2 % 2
                if g2 >= 3:
                    p.wait('gpsimd', SEV, p.val(SEV) - 2)
                rv = p.op('gpsimd', lambda e2, o=o, t0=t0, nt=nt, sl=sl, col0=col0:
                          e2.dma_start(
                              out=xr[:, sl * 512:sl * 512 + nt],
                              in_=x1_hbm[o * 128:(o + 1) * 128,
                                         col0 + t0:col0 + t0 + nt]),
                          sem=f'{pref}_R{sl}', n=16)
                p.wait('vector', f'{pref}_R{sl}', rv)
                pi2e8 = (half * 9 + 8) * 16 + o
                p.wait('vector', f'{pref}_yev', wev[(pi2e8, ci)])
                if ost[par2]:
                    p.wait('vector', f'{pref}_O{par2}', ost[par2][-1])
                if stats_out is not None:
                    p.wait('vector', f'{pref}_sa2', p.val(f'{pref}_sa2'))
                av = p.op('vector', lambda e2, o=o, t0=t0, nt=nt, sl=sl, par2=par2:
                     e2.tensor_tensor(
                         x2o[:, par2 * 512:par2 * 512 + nt],
                         yacc[:, o * TH + t0:o * TH + t0 + nt],
                         xr[:, sl * 512:sl * 512 + nt], op=ALU.add), sem=SEV)
                p.wait('gpsimd', SEV, p.val(SEV))
                ov = p.op('gpsimd', lambda e2, o=o, t0=t0, nt=nt, par2=par2,
                          col0=col0: e2.dma_start(
                              out=x2_hbm[o * 128:(o + 1) * 128,
                                         col0 + t0:col0 + t0 + nt],
                              in_=x2o[:, par2 * 512:par2 * 512 + nt]),
                          sem=f'{pref}_O{par2}', n=16)
                ost[par2].append(ov)
                if stats_out is not None:
                    p.wait('vector', SEV, av)
                    p.wait('vector', f'{pref}_sm2', max(0, p.val(f'{pref}_sm2') - 1))
                    cv = p.op('vector', lambda e2, nt=nt, par2=par2, ps2=ps2:
                              e2.tensor_copy(
                        xb2[:, ps2 * 512:ps2 * 512 + nt],
                        x2o[:, par2 * 512:par2 * 512 + nt]), sem=f'{pref}_sc2')
                    p.wait('scalar', f'{pref}_sc2', cv)
                    p.wait('scalar', f'{pref}_sm2', max(0, p.val(f'{pref}_sm2') - 1))
                    aq = p.op('scalar', lambda e2, nt=nt, par2=par2, ps2=ps2:
                              e2.square(
                        qb2[:, ps2 * 512:ps2 * 512 + nt],
                        x2o[:, par2 * 512:par2 * 512 + nt]), sem=f'{pref}_sa2')
                    p.wait('tensor', f'{pref}_sc2', cv)
                    p.wait('tensor', f'{pref}_sa2', aq)
                    if o == 0:
                        p.wait('tensor', f'{pref}_sv2', p.val(f'{pref}_sv2'))
                        p.wait('tensor', f'{pref}_yev', wev[(pi2e8, ci)])
                    p.op('tensor', lambda e2, nt=nt, ps2=ps2, o=o: e2.matmul(
                        cx.ps[3][0:1, 0:nt], cx.ones_col[:],
                        xb2[:, ps2 * 512:ps2 * 512 + nt],
                        start=(o == 0), stop=(o == 15), skip_group_check=True))
                    p.op('tensor', lambda e2, nt=nt, ps2=ps2, o=o: e2.matmul(
                        cx.ps[4][0:1, 0:nt], cx.ones_col[:],
                        qb2[:, ps2 * 512:ps2 * 512 + nt],
                        start=(o == 0), stop=(o == 15), skip_group_check=True),
                        sem=f'{pref}_sm2')
                gi[0] += 1
            if stats_out is not None:
                p.wait('vector', f'{pref}_sm2', p.val(f'{pref}_sm2'))
                if ci > 0:
                    p.wait('vector', f'{pref}_sS', p.val(f'{pref}_sS'))
                p.op('vector', lambda e2, nt=nt: e2.tensor_copy(
                    srw[0:1, 0:nt], cx.ps[3][0:1, 0:nt]))
                sv2 = p.op('vector', lambda e2, nt=nt: e2.tensor_copy(
                    srw[0:1, 512:512 + nt], cx.ps[4][0:1, 0:nt]),
                    sem=f'{pref}_sv2')
                p.wait('gpsimd', f'{pref}_sv2', sv2)
                p.op('gpsimd', lambda e2, col0=col0, t0=t0, nt=nt: e2.dma_start(
                    out=stats_out[0:1, col0 + t0:col0 + t0 + nt],
                    in_=srw[0:1, 0:nt]), sem=f'{pref}_sS', n=16)
                p.op('gpsimd', lambda e2, col0=col0, t0=t0, nt=nt: e2.dma_start(
                    out=stats_out[1:2, col0 + t0:col0 + t0 + nt],
                    in_=srw[0:1, 512:512 + nt]), sem=f'{pref}_sS', n=16)
        if stats_out is not None:
            p.wait('gpsimd', f'{pref}_sS', p.val(f'{pref}_sS'))
            p.wait('tensor', f'{pref}_sv2', p.val(f'{pref}_sv2'))
    return ost[0][-1], ost[1][-1], ost[2][-1]


def emit_outproj_residual(p, cx, pref, attn_res, wT_hbm, x_hbm, xcol0, xout_hbm,
                          T, bias_sb=None, stats_hbm=None):
    Tch = chunks(T)
    nch = len(Tch)
    xr = p.sbuf(f'{pref}_xr', [128, 4 * 512], F32)
    x1o = p.sbuf(f'{pref}_x1o', [128, 4 * 512], F32)
    SEV, SMMD = f'{pref}_ev', f'{pref}_mmd'
    gi = [0]
    olast = {0: [], 1: [], 2: [], 3: []}
    sev_g = []
    st = stats_init(p, pref, T) if stats_hbm is not None else None

    def X_ap(k, t0, nt):
        return attn_res[:, k * T + t0:k * T + t0 + nt]

    def evac(p2, oi, bank, t0, nt, ci):
        g = gi[0]
        sl = g % 4
        par = g % 4
        if g >= 4:
            p.wait('gpsimd', SEV, sev_g[g - 4])
        v = p.op('gpsimd', lambda e, oi=oi, t0=t0, nt=nt, sl=sl: e.dma_start(
            out=xr[:, sl * 512:sl * 512 + nt],
            in_=x_hbm[oi * 128:(oi + 1) * 128, xcol0 + t0:xcol0 + t0 + nt]),
            sem=f'{pref}_R{sl}', n=16)
        p.wait('vector', f'{pref}_R{sl}', v)
        p.wait('vector', SMMD, p.val(SMMD))
        if olast[par]:
            p.wait('vector', f'{pref}_O{par}', olast[par][-1])
        if bias_sb is not None:
            vv = p.op('vector', lambda e, oi=oi, nt=nt, sl=sl, bank=bank: e.tensor_scalar(
                xr[:, sl * 512:sl * 512 + nt], xr[:, sl * 512:sl * 512 + nt],
                bias_sb[:, oi:oi + 1], None, op0=ALU.add), sem=f'{pref}_bs')
            p.wait('vector', f'{pref}_bs', vv)
        fv = p.op('vector', lambda e, nt=nt, sl=sl, par=par, bank=bank: e.tensor_tensor(
            x1o[:, par * 512:par * 512 + nt], bank[:, 0:nt],
            xr[:, sl * 512:sl * 512 + nt], op=ALU.add), sem=SEV)
        sev_g.append(p.val(SEV))
        p.wait('gpsimd', SEV, p.val(SEV))
        ov = p.op('gpsimd', lambda e, oi=oi, t0=t0, nt=nt, par=par: e.dma_start(
            out=xout_hbm[oi * 128:(oi + 1) * 128, t0:t0 + nt],
            in_=x1o[:, par * 512:par * 512 + nt]), sem=f'{pref}_O{par}', n=16)
        olast[par].append(ov)
        if st is not None:
            p.wait('vector', SEV, fv)
            stats_acc(p, cx, pref, st, x1o[:, par * 512:par * 512 + nt],
                      t0, nt, first=(oi == 0))
        gi[0] += 1
        return fv

    wsrcs = [wT_hbm[:, oi * 128:(oi + 1) * 128] for oi in range(16)]
    emit_dense2(p, cx, pref, wsrcs, X_ap, Tch, 16, evac)
    if st is not None:
        stats_store(p, pref, st, stats_hbm)
    for s in range(4):
        if olast[s]:
            p.wait('gpsimd', f'{pref}_O{s}', olast[s][-1])
    return (0, 0)


def emit_gate(p, cx, pref, lnx, T, wg_hbm, combT, l1_norm=False,
              bank_g=5, bank_t0=0, bank_t1=1):
    wg32 = p.sbuf('gtsc_wg32', [128, 128], F32)
    wgbf = p.sbuf('gtsc_wgbf', [128, 128], BF16)
    lg = p.sbuf('gtsc_lg', [8, 512], F32)
    tl = p.sbuf('gtsc_tl', [128, 2 * 8], F32)
    prob = p.sbuf('gtsc_prob', [128, 8], F32)
    t2 = p.sbuf('gtsc_t2', [128, 8], F32)
    cmp2 = p.sbuf('gtsc_cmp2', [128, 8], F32)
    cm = p.sbuf('gtsc_cm', [128, 8], F32)
    r1 = p.sbuf('gtsc_r1', [128, 8], F32)
    kk2 = p.sbuf('gtsc_k2', [128, 2], F32)
    negbig = kk2[:, 0:1]
    c16 = kk2[:, 1:2]
    SMS, SCB, SAC = f'{pref}_ms', f'{pref}_cb', f'{pref}_ac'
    v = p.op('vector', lambda e: e.memset(negbig[:, :], -1e30))
    v = p.op('vector', lambda e: e.memset(c16[:, :], 16.0), sem=SCB)
    p.wait('vector', SCB, v)
    wgv = p.op('sync', lambda e: e.dma_start(
        out=wg32.rearrange("p (k e) -> p k e", k=16),
        in_=wg_hbm.rearrange("(k p) e -> p k e", p=128)), sem=f'{pref}_Wg', n=16)
    p.wait('scalar', f'{pref}_Wg', wgv)
    cv = p.op('scalar', lambda e: e.copy(wgbf[:], wg32[:]), sem=f'{pref}_cast')
    p.wait('tensor', f'{pref}_cast', cv)

    def dve(f):
        vv = p.op('vector', f, sem=SCB)
        p.wait('vector', SCB, vv)

    for ci, (t0, nt) in enumerate(chunks(T)):
        p.wait('tensor', SCB, p.val(SCB))
        for k in range(16):
            p.op('tensor', lambda e, k=k, t0=t0, nt=nt: e.matmul(
                cx.ps[bank_g][0:8, 0:nt], wgbf[:, k * 8:(k + 1) * 8],
                lnx[:, k * T + t0:k * T + t0 + nt], start=(k == 0), stop=(k == 15)),
                sem=(SMS if k == 15 else None))
        p.wait('vector', SMS, p.val(SMS))
        dve(lambda e, nt=nt: e.tensor_copy(lg[:, 0:nt], cx.ps[bank_g][0:8, 0:nt]))
        for si in range((nt + 127) // 128):
            ns = min(128, nt - si * 128)
            par = si % 2
            bt = bank_t0 if par == 0 else bank_t1
            p.wait('tensor', SCB, p.val(SCB))
            p.op('tensor', lambda e, si=si, ns=ns, bt=bt: e.matmul(
                cx.ps[bt][0:ns, 0:8], lg[0:8, si * 128:si * 128 + ns],
                cx.identf[0:8, 0:8], start=True, stop=True, is_transpose=True),
                sem=SMS)
            p.wait('vector', SMS, p.val(SMS))
            dve(lambda e, ns=ns, par=par, bt=bt: e.tensor_copy(
                tl[0:ns, par * 8:par * 8 + 8], cx.ps[bt][0:ns, 0:8]))
            dve(lambda e, ns=ns, par=par: e.reduce_max(
                r1[0:ns, 1:2], tl[0:ns, par * 8:par * 8 + 8], axis=AX.X, negate=True))
            p.wait('scalar', SCB, p.val(SCB))
            av = p.op('scalar', lambda e, ns=ns, par=par: e.activation(
                t2[0:ns, 0:8], tl[0:ns, par * 8:par * 8 + 8], AF.Exp,
                bias=r1[0:ns, 1:2]), sem=SAC)
            p.wait('vector', SAC, av)
            dve(lambda e, ns=ns: e.reduce_sum(r1[0:ns, 2:3], t2[0:ns, 0:8], axis=AX.X))
            dve(lambda e, ns=ns: e.reciprocal(r1[0:ns, 3:4], r1[0:ns, 2:3]))
            dve(lambda e, ns=ns: e.tensor_scalar(
                prob[0:ns, :], t2[0:ns, 0:8], r1[0:ns, 3:4], None, op0=ALU.mult))
            dve(lambda e, ns=ns: e.reduce_max(r1[0:ns, 4:5], prob[0:ns, :], axis=AX.X))
            dve(lambda e, ns=ns: e.tensor_scalar(
                cmp2[0:ns, :], prob[0:ns, :], r1[0:ns, 4:5], None, op0=ALU.is_ge))
            dve(lambda e, ns=ns: e.scalar_tensor_tensor(
                t2[0:ns, 0:8], cmp2[0:ns, :], negbig[0:ns, :], prob[0:ns, :],
                op0=ALU.mult, op1=ALU.add))
            dve(lambda e, ns=ns: e.reduce_max(r1[0:ns, 5:6], t2[0:ns, 0:8], axis=AX.X))
            dve(lambda e, ns=ns: e.tensor_scalar(
                cmp2[0:ns, :], prob[0:ns, :], r1[0:ns, 5:6], None, op0=ALU.is_ge))
            dve(lambda e, ns=ns: e.tensor_tensor(
                cm[0:ns, :], prob[0:ns, :], cmp2[0:ns, :], op=ALU.mult))
            if l1_norm:
                dve(lambda e, ns=ns: e.tensor_tensor(
                    r1[0:ns, 6:7], r1[0:ns, 4:5], r1[0:ns, 5:6], op=ALU.add))
                dve(lambda e, ns=ns: e.reciprocal(r1[0:ns, 7:8], r1[0:ns, 6:7]))
                dve(lambda e, ns=ns: e.tensor_scalar(
                    cm[0:ns, :], cm[0:ns, :], r1[0:ns, 7:8], None, op0=ALU.mult))
                dve(lambda e, ns=ns: e.tensor_scalar(
                    cm[0:ns, :], cm[0:ns, :], c16[0:ns, :], None, op0=ALU.mult))
            p.wait('tensor', SCB, p.val(SCB))
            p.op('tensor', lambda e, ns=ns, bt=bt: e.matmul(
                cx.ps[bt][0:8, 8:8 + ns], cm[0:ns, 0:8],
                cx.identf[0:ns, 0:ns], start=True, stop=True, is_transpose=True),
                sem=SMS)
            p.wait('vector', SMS, p.val(SMS))
            dve(lambda e, si=si, t0=t0, ns=ns, bt=bt: e.tensor_copy(
                combT[0:8, t0 + si * 128:t0 + si * 128 + ns],
                cx.ps[bt][0:8, 8:8 + ns]))
    return p.val(SCB)


def emit_l1qkv(p, cx, pref, x2_hbm, wqkv_hbm, bqkv_hbm, ropeC_hbm, ropeS_hbm,
               perm_hbm, qkv_hbm, stats_in=None):
    lnx2 = p.sbuf(f'{pref}_lnx', [128, 16 * TG], BF16)
    bia = p.sbuf(f'{pref}_bia', [128, 48], F32)
    rC = p.sbuf(f'{pref}_rC', [128, TG], F32)
    rS = p.sbuf(f'{pref}_rS', [128, TG], F32)
    prm = p.sbuf(f'{pref}_prm', [128, 128], BF16)
    qpre = p.sbuf(f'{pref}_qpre', [128, 2 * 512], BF16)
    qscl1 = p.sbuf(f'{pref}_qscl1', [128, 1], F32)
    vq = p.op('vector', lambda e: e.memset(qscl1[:], 0.125), sem=f'{pref}_ev')
    p.wait('vector', f'{pref}_ev', vq)
    rt1 = p.sbuf(f'{pref}_rt1', [128, 512], F32)
    rt2 = p.sbuf(f'{pref}_rt2', [128, 512], F32)
    outb = p.sbuf(f'{pref}_outb', [128, 2 * 512], BF16)
    for nm, t, hb in (('bia', bia, None), ('rC', rC, ropeC_hbm),
                      ('rS', rS, ropeS_hbm), ('prm', prm, perm_hbm)):
        pass
    p.op('gpsimd', lambda e: e.dma_start(out=bia[:], in_=bqkv_hbm[:]),
         sem=f'{pref}_B', n=16)
    p.op('gpsimd', lambda e: e.dma_start(out=rC[:], in_=ropeC_hbm[:]),
         sem=f'{pref}_B', n=16)
    p.op('gpsimd', lambda e: e.dma_start(out=rS[:], in_=ropeS_hbm[:]),
         sem=f'{pref}_B', n=16)
    p.op('gpsimd', lambda e: e.dma_start(out=prm[:], in_=perm_hbm[:]),
         sem=f'{pref}_B', n=16)
    bv = p.val(f'{pref}_B')
    if stats_in is not None:
        lnve = emit_ln_apply(p, cx, x2_hbm, TG, lnx2, f'{pref}n', stats_in, TG)
    else:
        lnve = emit_ln(p, cx, x2_hbm, TG, lnx2, f'{pref}n')
    for eng in ('tensor', 'vector', 'scalar'):
        p.wait(eng, f'{pref}n_ve', lnve)
    p.wait('vector', f'{pref}_B', bv)
    p.wait('tensor', f'{pref}_B', bv)
    Tch = chunks(TG)
    nch = len(Tch)
    SEV, SMMD = f'{pref}_ev', f'{pref}_mmd'
    ost = {0: [], 1: []}
    gq = [0]

    def X_ap(k, t0, nt):
        return lnx2[:, k * TG + t0:k * TG + t0 + nt]

    def evac(p2, oi, bank, t0, nt, ci):
        g = gq[0]
        par = g % 2
        rb = 5 + (g % 2)
        p.wait('vector', SMMD, p.val(SMMD))
        if ost[par]:
            p.wait('vector', f'{pref}_O{par}', ost[par][-1])
        if oi < 16:
            vv = p.op('vector', lambda e, oi=oi, nt=nt, par=par, bank=bank:
                      e.tensor_scalar(qpre[:, par * 512:par * 512 + nt],
                                      bank[:, 0:nt], bia[:, oi:oi + 1],
                                      qscl1[:, 0:1], op0=ALU.add, op1=ALU.mult),
                      sem=SEV)
        else:
            vv = p.op('vector', lambda e, oi=oi, nt=nt, par=par, bank=bank:
                      e.tensor_scalar(qpre[:, par * 512:par * 512 + nt],
                                      bank[:, 0:nt], bia[:, oi:oi + 1], None,
                                      op0=ALU.add), sem=SEV)
        p.wait('vector', SEV, vv)
        freev = vv
        if oi < 32:
            p.wait('tensor', SEV, vv)
            mv = p.op('tensor', lambda e, nt=nt, par=par, rb=rb: e.matmul(
                cx.ps[rb][:, 0:nt], prm[:], qpre[:, par * 512:par * 512 + nt],
                start=True, stop=True), sem=f'{pref}_pm')
            vv = p.op('vector', lambda e, t0=t0, nt=nt, par=par: e.tensor_tensor(
                rt1[:, 0:nt], qpre[:, par * 512:par * 512 + nt], rC[:, t0:t0 + nt],
                op=ALU.mult), sem=SEV)
            p.wait('vector', f'{pref}_pm', mv)
            p.wait('vector', SEV, vv)
            vv = p.op('vector', lambda e, t0=t0, nt=nt, rb=rb: e.tensor_tensor(
                rt2[:, 0:nt], cx.ps[rb][:, 0:nt], rS[:, t0:t0 + nt],
                op=ALU.mult), sem=SEV)
            p.wait('vector', SEV, vv)
            vv = p.op('vector', lambda e, nt=nt, par=par: e.tensor_tensor(
                outb[:, par * 512:par * 512 + nt], rt1[:, 0:nt], rt2[:, 0:nt],
                op=ALU.add), sem=SEV)
            p.wait('gpsimd', SEV, vv)
            src = outb
        else:
            p.wait('gpsimd', SEV, vv)
            src = qpre
        ov = p.op('gpsimd', lambda e, oi=oi, t0=t0, nt=nt, par=par, src=src:
                  e.dma_start(out=qkv_hbm[oi * 128:(oi + 1) * 128, t0:t0 + nt],
                              in_=src[:, par * 512:par * 512 + nt]),
                  sem=f'{pref}_O{par}', n=16)
        ost[par].append(ov)
        gq[0] += 1
        return freev

    wsrcs = [wqkv_hbm[:, oi * 128:(oi + 1) * 128] for oi in range(48)]
    emit_dense2(p, cx, pref, wsrcs, X_ap, Tch, 16, evac)
    return ost[0][-1], ost[1][-1]


def emit_l1attn(p, cx, pref, qkv_hbm, masks_hbm, sinksT_hbm, sinksaug_hbm,
                attn1_hbm):
    msk = p.sbuf(f'{pref}_msk', [128, 7 * 512], BF16)
    q2 = p.sbuf(f'{pref}_q2', [128, 2 * TG], BF16)
    k2 = p.sbuf(f'{pref}_k2', [128, 2 * TG], BF16)
    v2 = p.sbuf(f'{pref}_v2', [128, 2 * TG], BF16)
    vtm2 = p.sbuf(f'{pref}_vtm2', [128, 34 * 66], BF16)
    skT32 = p.sbuf(f'{pref}_skT32', [128, 512], F32)
    skT = p.sbuf(f'{pref}_skT', [128, 512], BF16)
    ska32 = p.sbuf(f'{pref}_ska32', [16, 32 * 65], F32)
    skab = p.sbuf(f'{pref}_skab', [48, 16 * 65], BF16)
    psnk = p.sbuf(f'{pref}_psnk', [48, 2 * 512], BF16)
    sexp = p.sbuf(f'{pref}_sexp', [128, 4 * 512], F32)
    pT = p.sbuf(f'{pref}_pT', [128, 4 * 512], BF16)
    rr2 = p.sbuf(f'{pref}_rr2', [1, 2 * 512], F32)
    lnt5 = p.sbuf(f'{pref}_lnt5', [128, 512], F32)
    rb = p.sbuf(f'{pref}_rb', [128, 512], F32)
    att1 = p.sbuf(f'{pref}_att1', [128, 2 * 2048], BF16)
    SMS, SCM, SAE, SMP, SCR, SMB = (f'{pref}_ms', f'{pref}_cm', f'{pref}_ae',
                                    f'{pref}_mp', f'{pref}_cr', f'{pref}_mb')
    p.op('gpsimd', lambda e: e.dma_start(
        out=msk.rearrange("p (s f) -> p s f", s=7),
        in_=masks_hbm.rearrange("s p f -> p s f")), sem=f'{pref}_M', n=16)
    p.op('gpsimd', lambda e: e.dma_start(
        out=skT32[0:64, :].rearrange("p (h s) -> p h s", h=32),
        in_=sinksT_hbm.rearrange("h p s -> p h s")), sem=f'{pref}_M', n=16)
    p.op('gpsimd', lambda e: e.dma_start(
        out=skT32[64:128, :].rearrange("p (h s) -> p h s", h=32),
        in_=sinksT_hbm.rearrange("h p s -> p h s")), sem=f'{pref}_M', n=16)
    p.op('gpsimd', lambda e: e.dma_start(
        out=ska32.rearrange("p (h f) -> p h f", h=32),
        in_=sinksaug_hbm.rearrange("h p f -> p h f")), sem=f'{pref}_M', n=16)
    mv = p.val(f'{pref}_M')
    p.wait('vector', f'{pref}_M', mv)
    v = p.op('vector', lambda e: e.memset(vtm2[:], 1.0), sem=SCM)
    v = p.op('vector', lambda e: e.tensor_copy(skT[:], skT32[:]), sem=SCM)
    for j in range(16):
        for b in range(2):
            v = p.op('vector', lambda e, j=j, b=b: e.tensor_copy(
                skab[b * 32:b * 32 + 16, j * 65:(j + 1) * 65],
                ska32[0:16, (2 * j + b) * 65:(2 * j + b + 1) * 65]), sem=SCM)
    p.wait('vector', SCM, v)
    p.wait('tensor', SCM, v)
    st_pairs = {0: [], 1: []}
    pe_done = []
    kki = 0
    sms_s = {}
    scm_done = []
    sae_done = []
    smp_done = []
    sae_ln = None
    smp_pv = None
    evq_last = None
    for hp in range(16):
        par = hp % 2
        if hp >= 2:
            p.wait('gpsimd', SMP, pe_done[hp - 2])
        lv = 0
        for nm, dst, r0 in (('q', q2, 0), ('k', k2, 2048), ('v', v2, 4096)):
            lv = p.op('gpsimd', lambda e, dst=dst, r0=r0, par=par, hp=hp:
                      e.dma_start(out=dst[:, par * TG:(par + 1) * TG],
                                  in_=qkv_hbm[r0 + hp * 128:r0 + (hp + 1) * 128, :]),
                      sem=f'{pref}_L{par}', n=16)
        p.wait('tensor', f'{pref}_L{par}', lv)
        scm0 = p.val(SCM)
        for kt in range(17):
            bk = kt % 2
            if kt >= 2:
                p.wait('tensor', SCM, scm0 + 2 * (kt - 2) + 2)
            p.op('tensor', lambda e, kt=kt, bk=bk, par=par: e.matmul(
                cx.ps[bk].bitcast(BF16)[:, 0:128],
                v2[:, par * TG + kt * 128:par * TG + (kt + 1) * 128],
                cx.ident[:], start=True, stop=True, is_transpose=True), sem=SMS)
            p.wait('vector', SMS, p.val(SMS))
            for hi in range(2):
                p.op('vector', lambda e, kt=kt, hi=hi, bk=bk: e.tensor_copy(
                    vtm2[:, (kt * 2 + hi) * 66:(kt * 2 + hi) * 66 + 64].bitcast(F32),
                    cx.ps[bk].bitcast(BF16)[:, hi * 64:(hi + 1) * 64].bitcast(F32)),
                    sem=SCM)
        vtm_v = p.val(SCM)
        p.wait('tensor', SCM, vtm_v)
        h0 = hp * 2
        for qc in range(4):
            qq = 64 + 512 * qc
            nq = 512
            qbase = kki

            def emit_spair(tstep, kk, par=par, qq=qq, nq=nq):
                kp0 = (tstep % 2) * 2
                kb = 512 * qc + kk * 128
                if tstep >= 2:
                    p.wait('tensor', SCM, scm_done[tstep - 2])
                p.op('tensor', lambda e, kb=kb, qq=qq, nq=nq, kp0=kp0, par=par:
                     e.matmul(cx.ps[kp0][:, 0:nq],
                              k2[0:64, par * TG + kb:par * TG + kb + 128],
                              q2[0:64, par * TG + qq:par * TG + qq + nq],
                              start=True, stop=True))
                v = p.op('tensor', lambda e, kb=kb, qq=qq, nq=nq, kp0=kp0,
                         par=par:
                         e.matmul(cx.ps[kp0 + 1][:, 0:nq],
                                  k2[64:128, par * TG + kb:par * TG + kb + 128],
                                  q2[64:128, par * TG + qq:par * TG + qq + nq],
                                  start=True, stop=True), sem=SMS)
                sms_s[tstep] = v

            emit_spair(qbase, 0)
            emit_spair(qbase + 1, 1)
            if sae_ln is not None:
                p.wait('tensor', SAE, sae_ln)
            p.op('tensor', lambda e, h0=h0, qq=qq, nq=nq, par=par: e.matmul(
                cx.ps[6][0:16, 0:nq],
                skT[0:64, h0 * 16:(h0 + 1) * 16],
                q2[0:64, par * TG + qq:par * TG + qq + nq],
                start=True, stop=True))
            p.op('tensor', lambda e, h0=h0, qq=qq, nq=nq, par=par: e.matmul(
                cx.ps[6][32:48, 0:nq],
                skT[64:128, (h0 + 1) * 16:(h0 + 2) * 16],
                q2[64:128, par * TG + qq:par * TG + qq + nq],
                start=True, stop=True), sem=SMS)
            p.wait('scalar', SMS, p.val(SMS))
            if smp_pv is not None:
                p.wait('scalar', SMP, smp_pv)
            av = p.op('scalar', lambda e, nq=nq, par=par: e.activation(
                psnk[0:48, par * 512:par * 512 + nq], cx.ps[6][0:48, 0:nq],
                AF.Exp, bias=cx.expbias[0:48, 0:1]), sem=SAE)
            p.wait('tensor', SAE, av)
            if evq_last is not None:
                p.wait('tensor', f'{pref}_evq', evq_last)
            p.op('tensor', lambda e, hp=hp, nq=nq, par=par: e.matmul(
                cx.ps[4][0:65, 0:nq], skab[0:16, hp * 65:(hp + 1) * 65],
                psnk[0:16, par * 512:par * 512 + nq], start=True, stop=False))
            p.op('tensor', lambda e, hp=hp, nq=nq, par=par: e.matmul(
                cx.ps[5][0:65, 0:nq], skab[32:48, hp * 65:(hp + 1) * 65],
                psnk[32:48, par * 512:par * 512 + nq], start=True, stop=False),
                sem=SMP)
            smp_pv = p.val(SMP)
            for kk in range(5):
                tstep = qbase + kk
                kp0 = (tstep % 2) * 2
                slot = l1_mask_slot(qc, kk)
                p.wait('vector', SMS, sms_s[tstep])
                if tstep >= 2:
                    p.wait('vector', SAE, sae_done[tstep - 2])
                p.op('vector', lambda e, kp0=kp0, slot=slot, nq=nq: e.tensor_tensor(
                    sexp[:, kp0 * 512:kp0 * 512 + nq], cx.ps[kp0][:, 0:nq],
                    msk[:, slot * 512:slot * 512 + nq], op=ALU.add))
                p.op('vector', lambda e, kp0=kp0, slot=slot, nq=nq: e.tensor_tensor(
                    sexp[:, (kp0 + 1) * 512:(kp0 + 1) * 512 + nq],
                    cx.ps[kp0 + 1][:, 0:nq],
                    msk[:, slot * 512:slot * 512 + nq], op=ALU.add), sem=SCM)
                scm_done.append(p.val(SCM))
                p.wait('scalar', SCM, scm_done[-1])
                if tstep >= 2:
                    p.wait('scalar', SMP, smp_done[tstep - 2])
                p.op('scalar', lambda e, kp0=kp0, nq=nq: e.activation(
                    pT[:, kp0 * 512:kp0 * 512 + 2 * nq],
                    sexp[:, kp0 * 512:kp0 * 512 + 2 * nq], AF.Exp), sem=SAE)
                sae_done.append(p.val(SAE))
                if kk < 3:
                    emit_spair(tstep + 2, kk + 2)
                p.wait('tensor', SAE, sae_done[tstep])
                p.op('tensor', lambda e, kt4=4 * qc + kk, kp0=kp0, nq=nq, kk=kk:
                     e.matmul(cx.ps[4][0:65, 0:nq],
                              vtm2[:, (kt4 * 2) * 66:(kt4 * 2) * 66 + 65],
                              pT[:, kp0 * 512:kp0 * 512 + nq],
                              start=False, stop=(kk == 4)))
                p.op('tensor', lambda e, kt4=4 * qc + kk, kp0=kp0, nq=nq, kk=kk:
                     e.matmul(cx.ps[5][0:65, 0:nq],
                              vtm2[:, (kt4 * 2 + 1) * 66:(kt4 * 2 + 1) * 66 + 65],
                              pT[:, (kp0 + 1) * 512:(kp0 + 1) * 512 + nq],
                              start=False, stop=(kk == 4)), sem=SMP)
                smp_done.append(p.val(SMP))
            kki = qbase + 5
            p.wait('vector', SMP, p.val(SMP))
            p.op('vector', lambda e, nq=nq: e.tensor_copy(
                rr2[0:1, 0:nq], cx.ps[4][64:65, 0:nq]))
            vv = p.op('vector', lambda e, nq=nq: e.tensor_copy(
                rr2[0:1, 512:512 + nq], cx.ps[5][64:65, 0:nq]), sem=SCR)
            p.wait('tensor', SCR, vv)
            p.op('tensor', lambda e, nq=nq: e.matmul(
                cx.ps[6][0:64, 0:nq], cx.ones_row[0:1, 0:64], rr2[0:1, 0:nq],
                start=True, stop=True))
            p.op('tensor', lambda e, nq=nq: e.matmul(
                cx.ps[6][64:128, 0:nq], cx.ones_row[0:1, 0:64],
                rr2[0:1, 512:512 + nq], start=True, stop=True), sem=SMB)
            p.wait('scalar', SMB, p.val(SMB))
            av = p.op('scalar', lambda e, nq=nq: e.activation(
                lnt5[:, 0:nq], cx.ps[6][:, 0:nq], AF.Ln), sem=SAE)
            sae_ln = av
            p.wait('scalar', SAE, av)
            av = p.op('scalar', lambda e, nq=nq: e.activation(
                rb[:, 0:nq], lnt5[:, 0:nq], AF.Exp,
                scale=cx.neg1[:, 0:1]), sem=SAE)
            p.wait('vector', SAE, av)
            if hp >= 2 and st_pairs[par]:
                p.wait('vector', f'{pref}_S{par}', st_pairs[par][-1])
            p.op('vector', lambda e, qq=qq, nq=nq, par=par: e.tensor_tensor(
                att1[0:64, par * 2048 + qq - 64:par * 2048 + qq - 64 + nq],
                cx.ps[4][0:64, 0:nq], rb[0:64, 0:nq], op=ALU.mult))
            evv = p.op('vector', lambda e, qq=qq, nq=nq, par=par: e.tensor_tensor(
                att1[64:128, par * 2048 + qq - 64:par * 2048 + qq - 64 + nq],
                cx.ps[5][0:64, 0:nq], rb[64:128, 0:nq], op=ALU.mult),
                sem=f'{pref}_evq')
            evq_last = evv
        pe_done.append(p.val(SMP))
        p.wait('gpsimd', f'{pref}_evq', p.val(f'{pref}_evq'))
        sv = p.op('gpsimd', lambda e, hp=hp, par=par: e.dma_start(
            out=attn1_hbm[hp * 128:(hp + 1) * 128, :],
            in_=att1[:, par * 2048:(par + 1) * 2048]), sem=f'{pref}_S{par}', n=16)
        st_pairs[par].append(sv)
    return st_pairs[0][-1], st_pairs[1][-1]


def emit_l1ffn(p, cx, pref, x3_hbm, wd_hbm, wg_hbm, Apack_hbm, Bpack_hbm,
               sel16_hbm, out_hbm, stats_in=None):
    TO = 2048
    lnx3 = p.sbuf(f'{pref}_lnx', [128, 16 * TO], BF16)
    combT = p.sbuf(f'{pref}_comb', [8, TO], F32)
    ts = p.sbuf(f'{pref}_ts', [128, TO], BF16)
    cwe = p.sbuf(f'{pref}_cwe', [128, 512], F32)
    aw32 = p.sbuf(f'{pref}_aw32', [128, 2048], F32)
    awbf = p.sbuf(f'{pref}_awbf', [128, 2048], BF16)
    bp32 = p.sbuf(f'{pref}_bp32', [128, 2048], F32)
    bpbf = p.sbuf(f'{pref}_bpbf', [128, 2048], BF16)
    s16 = p.sbuf(f'{pref}_s16', [8, 128], F32)
    SEV, SMMD = f'{pref}_ev', f'{pref}_mmd'
    p.op('sync', lambda e: e.dma_start(
        out=aw32.rearrange("p (k f) -> p k f", k=16),
        in_=Apack_hbm.rearrange("(k p) f -> p k f", p=128)), sem=f'{pref}_A', n=16)
    p.op('sync', lambda e: e.dma_start(out=bp32[:], in_=Bpack_hbm[:]),
         sem=f'{pref}_A', n=16)
    p.op('sync', lambda e: e.dma_start(out=s16[:], in_=sel16_hbm[:]),
         sem=f'{pref}_A', n=16)
    av = p.val(f'{pref}_A')
    p.wait('scalar', f'{pref}_A', av)
    cv = p.op('scalar', lambda e: e.copy(awbf[:], aw32[:]), sem=f'{pref}_acast')
    cv = p.op('scalar', lambda e: e.copy(bpbf[:], bp32[:]), sem=f'{pref}_acast')
    if stats_in is not None:
        lnve = emit_ln_apply(p, cx, x3_hbm, TO, lnx3, f'{pref}n', stats_in, TO)
    else:
        lnve = emit_ln(p, cx, x3_hbm, TO, lnx3, f'{pref}n')
    for eng in ('tensor', 'vector', 'scalar'):
        p.wait(eng, f'{pref}n_ve', lnve)
    cb = emit_gate(p, cx, f'{pref}g', lnx3, TO, wg_hbm, combT, l1_norm=True)
    for eng in ('tensor', 'vector'):
        p.wait(eng, f'{pref}g_cb', cb)
    p.wait('tensor', f'{pref}_acast', cv)
    Tch = chunks(TO)
    nch = len(Tch)
    for ci, (t0, nt) in enumerate(Tch):
        bk = ci % 2
        p.wait('tensor', f'{pref}_tsv', p.val(f'{pref}_tsv'))
        for k in range(16):
            p.op('tensor', lambda e, k=k, t0=t0, nt=nt, bk=bk: e.matmul(
                cx.ps[bk][:, 0:nt], awbf[:, k * 128:(k + 1) * 128],
                lnx3[:, k * TO + t0:k * TO + t0 + nt],
                start=(k == 0), stop=(k == 15)),
                sem=(f'{pref}_tmm' if k == 15 else None))
        p.op('tensor', lambda e, t0=t0, nt=nt: e.matmul(
            cx.ps[5][:, 0:nt], s16[0:8, :], combT[0:8, t0:t0 + nt],
            start=True, stop=True), sem=f'{pref}_tmm')
        p.wait('vector', f'{pref}_tmm', p.val(f'{pref}_tmm'))
        vv = p.op('vector', lambda e, nt=nt: e.tensor_copy(
            cwe[:, 0:nt], cx.ps[5][:, 0:nt]), sem=f'{pref}_tsv')
        p.wait('vector', f'{pref}_tsv', vv)
        p.op('vector', lambda e, t0=t0, nt=nt, bk=bk: e.tensor_tensor(
            ts[:, t0:t0 + nt], cx.ps[bk][:, 0:nt], cwe[:, 0:nt], op=ALU.mult),
            sem=f'{pref}_tsv')
    tsv = p.val(f'{pref}_tsv')
    p.wait('tensor', f'{pref}_tsv', tsv)

    xr = p.sbuf(f'{pref}_xr', [128, 4 * 512], F32)
    xo = p.sbuf(f'{pref}_xo', [128, 4 * 512], F32)
    ost = {0: [], 1: [], 2: [], 3: []}
    gi = [0]
    sev_g7 = []

    def X_ap(k, t0, nt):
        return lnx3[:, k * TO + t0:k * TO + t0 + nt]

    def extra(p2, oi, ci, bank, t0, nt, semname):
        p.op('tensor', lambda e, oi=oi, t0=t0, nt=nt, bank=bank: e.matmul(
            bank[:, 0:nt], bpbf[:, oi * 128:(oi + 1) * 128], ts[:, t0:t0 + nt],
            start=False, stop=True), sem=semname)

    def evac(p2, oi, bank, t0, nt, ci):
        g = gi[0]
        sl = g % 4
        par = g % 4
        if g >= 4:
            p.wait('gpsimd', SEV, sev_g7[g - 4])
        rv = p.op('gpsimd', lambda e, oi=oi, t0=t0, nt=nt, sl=sl: e.dma_start(
            out=xr[:, sl * 512:sl * 512 + nt],
            in_=x3_hbm[oi * 128:(oi + 1) * 128, t0:t0 + nt]), sem=f'{pref}_R{sl}', n=16)
        p.wait('vector', f'{pref}_R{sl}', rv)
        p.wait('vector', SMMD, p.val(SMMD))
        if ost[par]:
            p.wait('vector', f'{pref}_O{par}', ost[par][-1])
        fv = p.op('vector', lambda e, nt=nt, sl=sl, par=par, bank=bank: e.tensor_tensor(
            xo[:, par * 512:par * 512 + nt], bank[:, 0:nt],
            xr[:, sl * 512:sl * 512 + nt], op=ALU.add), sem=SEV)
        sev_g7.append(p.val(SEV))
        p.wait('gpsimd', SEV, p.val(SEV))
        ov = p.op('gpsimd', lambda e, oi=oi, t0=t0, nt=nt, par=par: e.dma_start(
            out=out_hbm[oi * 128:(oi + 1) * 128, t0:t0 + nt],
            in_=xo[:, par * 512:par * 512 + nt]), sem=f'{pref}_O{par}', n=16)
        ost[par].append(ov)
        gi[0] += 1
        return fv

    wsrcs = [wd_hbm[:, oi * 128:(oi + 1) * 128] for oi in range(16)]
    emit_dense2(p, cx, pref, wsrcs, X_ap, Tch, 16, evac, extra_mm=extra)
    for s in range(4):
        if ost[s]:
            p.wait('gpsimd', f'{pref}_O{s}', ost[s][-1])
    return (0, 0)


def stats_init(p, pref, T):
    st = {}
    st['acc'] = p.sbuf(f'{pref}_stacc', [1, 2 * 2176], F32)
    st['xb'] = p.sbuf('lnsc_xbf', [128, 2 * 512], BF16)
    st['qb'] = p.sbuf('lnsc_sqbf', [128, 2 * 512], BF16)
    st['T'] = T
    st['n'] = 0
    return st


def stats_acc(p, cx, pref, st, tile_ap, t0, nt, first):
    SC, SA, SM, SP = (f'{pref}_stc', f'{pref}_sta', f'{pref}_stm', f'{pref}_stp')
    v = p.op('vector', lambda e: e.tensor_copy(st['xb'][:, 0:nt], tile_ap), sem=SC)
    p.wait('scalar', SC, v)
    a = p.op('scalar', lambda e: e.square(st['qb'][:, 0:nt], tile_ap), sem=SA)
    p.wait('tensor', SC, p.val(SC))
    p.wait('tensor', SA, p.val(SA))
    p.wait('tensor', SP, p.val(SP))
    p.op('tensor', lambda e: e.matmul(cx.ps[5][0:1, 0:nt], cx.ones_col[:],
                                      st['xb'][:, 0:nt], start=True, stop=True))
    m = p.op('tensor', lambda e: e.matmul(cx.ps[6][0:1, 0:nt], cx.ones_col[:],
                                          st['qb'][:, 0:nt], start=True, stop=True),
             sem=SM)
    p.wait('vector', SM, m)
    T = st['T']
    if first:
        p.op('vector', lambda e: e.tensor_copy(
            st['acc'][0:1, t0:t0 + nt], cx.ps[5][0:1, 0:nt]))
        p.op('vector', lambda e: e.tensor_copy(
            st['acc'][0:1, T + t0:T + t0 + nt], cx.ps[6][0:1, 0:nt]), sem=SP)
    else:
        p.op('vector', lambda e: e.tensor_tensor(
            st['acc'][0:1, t0:t0 + nt], cx.ps[5][0:1, 0:nt],
            st['acc'][0:1, t0:t0 + nt], op=ALU.add))
        p.op('vector', lambda e: e.tensor_tensor(
            st['acc'][0:1, T + t0:T + t0 + nt], cx.ps[6][0:1, 0:nt],
            st['acc'][0:1, T + t0:T + t0 + nt], op=ALU.add), sem=SP)


def stats_store(p, pref, st, stats_hbm):
    T = st['T']
    p.wait('gpsimd', f'{pref}_stp', p.val(f'{pref}_stp'))
    p.op('gpsimd', lambda e: e.dma_start(out=stats_hbm[0:1, :],
                                         in_=st['acc'][0:1, 0:T]),
         sem=f'{pref}_stS', n=16)
    v = p.op('gpsimd', lambda e: e.dma_start(out=stats_hbm[1:2, :],
                                             in_=st['acc'][0:1, T:2 * T]),
         sem=f'{pref}_stS', n=16)
    p.wait('gpsimd', f'{pref}_stS', v)
    return v


def emit_ln_apply(p, cx, src_hbm, T, lnx, pref, stats_hbm, Tst, col0=0,
                  bank0=0):
    nk = 16
    ps_mb = cx.ps[bank0 + 2]
    ps_rb = cx.ps[bank0 + 3]
    xin = p.sbuf("lnsc_xin", [128, 5 * 512], F32)
    rows = p.sbuf("lnsc_rows", [1, 2 * 512], F32)
    srows = p.sbuf("lnsc_srows", [1, 2 * 512], F32)
    mrow = p.sbuf("lnsc_mrow", [128, 512], F32)
    rrow = p.sbuf("lnsc_rrow", [128, 512], F32)
    inv = p.const(1.0 / 2048.0)
    eps = p.const(EPS)
    neghalf = p.const(-0.5)
    SA, SC, SM, SV, SP, SL = (f'{pref}_ac', f'{pref}_cv', f'{pref}_mm',
                              f'{pref}_ve', f'{pref}_pl', f'{pref}_sl')
    ve_done = []
    pl_done = 0
    for ci, (t0, nt) in enumerate(chunks(T)):
        c0 = col0 + t0
        p.wait('gpsimd', SC, p.val(SC))
        p.op('gpsimd', lambda e, c0=c0, nt=nt: e.dma_start(
            out=srows[0:1, 0:nt], in_=stats_hbm[0:1, c0:c0 + nt]), sem=SL, n=16)
        lv = p.op('gpsimd', lambda e, c0=c0, nt=nt: e.dma_start(
            out=srows[0:1, 512:512 + nt], in_=stats_hbm[1:2, c0:c0 + nt]),
            sem=SL, n=16)
        p.wait('vector', SL, lv)
        v = p.op('vector', lambda e, nt=nt: e.tensor_scalar(
            rows[0:1, 0:nt], srows[0:1, 0:nt], inv, None, op0=ALU.mult),
            sem=SC)
        v = p.op('vector', lambda e, nt=nt: e.tensor_scalar(
            rows[0:1, 512:512 + nt], srows[0:1, 512:512 + nt], inv,
            None, op0=ALU.mult), sem=SC)
        p.wait('vector', SC, v)
        v = p.op('vector', lambda e, nt=nt: e.tensor_tensor(
            mrow[0:1, 0:nt], rows[0:1, 0:nt], rows[0:1, 0:nt],
            op=ALU.mult), sem=SC)
        p.wait('vector', SC, v)
        v = p.op('vector', lambda e, nt=nt: e.tensor_tensor(
            rows[0:1, 512:512 + nt], rows[0:1, 512:512 + nt],
            mrow[0:1, 0:nt], op=ALU.subtract), sem=SC)
        p.wait('scalar', SC, p.val(SC))
        sa1 = p.op('scalar', lambda e, nt=nt: e.activation(
            mrow[0:1, 0:nt], rows[0:1, 512:512 + nt], AF.Ln,
            bias=eps), sem=SA)
        p.wait('scalar', SA, sa1)
        p.op('scalar', lambda e, nt=nt: e.activation(
            rows[0:1, 512:512 + nt], mrow[0:1, 0:nt], AF.Exp,
            scale=neghalf), sem=SA)
        p.wait('tensor', SC, p.val(SC))
        p.wait('tensor', SA, p.val(SA))
        p.wait('tensor', SP, p.val(SP))
        p.wait('tensor', SV, ve_done[-1] if ve_done else 0)
        p.op('tensor', lambda e, nt=nt: e.matmul(
            ps_mb[:, 0:nt], cx.ones_row[:, :], rows[0:1, 0:nt], start=True, stop=True))
        p.op('tensor', lambda e, nt=nt: e.matmul(
            ps_rb[:, 0:nt], cx.ones_row[:, :], rows[0:1, 512:512 + nt],
            start=True, stop=True), sem=SM)
        p.wait('vector', SM, p.val(SM))
        p.wait('vector', SP, pl_done)
        p.op('vector', lambda e, nt=nt: e.tensor_copy(mrow[:, 0:nt], ps_mb[:, 0:nt]))
        v = p.op('vector', lambda e, nt=nt: e.tensor_copy(rrow[:, 0:nt], ps_rb[:, 0:nt]),
                 sem=SC)
        p.wait('vector', SC, v)
        scv = p.val(SC)
        sv0 = p.val(SV)
        p.wait('gpsimd', SC, scv)
        p.wait('gpsimd', SV, sv0)
        lvs = []
        for k in range(5):
            lvs.append(p.op('gpsimd', lambda e, k=k, c0=c0, nt=nt: e.dma_start(
                out=xin[:, (k % 5) * 512:(k % 5) * 512 + nt],
                in_=src_hbm[k * 128:(k + 1) * 128, c0:c0 + nt]),
                sem=f'{pref}_L{k % 5}', n=16))
        for k in range(nk):
            sl = k % 5
            p.wait('gpsimd', f'{pref}_L{sl}', lvs[k])
            pl = p.op('gpsimd', lambda e, sl=sl, nt=nt: e.tensor_sub(
                xin[:, sl * 512:sl * 512 + nt], xin[:, sl * 512:sl * 512 + nt],
                mrow[:, 0:nt]), sem=SP)
            p.wait('vector', SP, pl)
            p.op('vector', lambda e, k=k, sl=sl, nt=nt, t0=t0: e.tensor_tensor(
                lnx[:, k * T + t0:k * T + t0 + nt], xin[:, sl * 512:sl * 512 + nt],
                rrow[:, 0:nt], op=ALU.mult), sem=SV)
            if k + 5 < nk:
                p.wait('gpsimd', SV, sv0 + k + 1)
                lvs.append(p.op('gpsimd', lambda e, k=k, c0=c0, nt=nt: e.dma_start(
                    out=xin[:, ((k + 5) % 5) * 512:((k + 5) % 5) * 512 + nt],
                    in_=src_hbm[(k + 5) * 128:(k + 6) * 128, c0:c0 + nt]),
                    sem=f'{pref}_L{(k + 5) % 5}', n=16))
        pl_done = p.val(SP)
        ve_done.append(p.val(SV))
    return ve_done[-1]


from concourse.bass_utils import run_bass_kernel_spmd


def _sel_consts():
    sel8 = np.zeros((8, 8, 128), np.float32)
    for e in range(8):
        sel8[e, e, :] = 1.0
    sel16 = np.zeros((8, 128), np.float32)
    for m in range(128):
        sel16[m // 16, m] = 1.0
    return sel8, sel16


def _build_s1():
    p = P()
    xT = p.dram('xT', [2048, TK], F32, kind='ExternalInput')
    ident = p.dram('ident', [128, 128], F32, kind='ExternalInput')
    wqkv = p.dram('wqkv0T', [2048, 6144], F32, kind='ExternalInput')
    masks0 = p.dram('masks0', [17, 128, 512], BF16, kind='ExternalInput')
    attnT = p.dram('attnT0', [2048, TG], BF16, kind='ExternalOutput')
    init_gate(p)
    cx = common_alloc(p, ident)
    lnx = p.sbuf('lnx', [128, 16 * TK], BF16)
    ve = emit_ln(p, cx, xT, TK, lnx, 'ln0')
    p.wait('tensor', 'ln0_ve', ve)
    p.wait('vector', 'ln0_ve', ve)
    st = emit_l0_attn(p, cx, lnx, wqkv, masks0, attnT, 'a0')
    p.wait('gpsimd', 'a0_S0', st[0])
    p.wait('gpsimd', 'a0_S1', st[1])
    p.finalize_consts()
    return p.build()


def _build_s2():
    p = P()
    xT = p.dram('xT', [2048, TK], F32, kind='ExternalInput')
    ident = p.dram('ident', [128, 128], F32, kind='ExternalInput')
    wout = p.dram('wout0T', [2048, 2048], F32, kind='ExternalInput')
    attnT = p.dram('attnT0', [2048, TG], BF16, kind='ExternalInput')
    x1T = p.dram('x1T', [2048, TG], F32, kind='ExternalOutput')
    init_gate(p)
    cx = common_alloc(p, ident)
    ares = p.sbuf('ares', [128, 16 * TG], BF16)
    v = load_resident(p, attnT, TG, 16, ares, 'ALR')
    p.wait('tensor', 'ALR', v)
    o0, o1 = emit_outproj_residual(p, cx, 's2', ares, wout, xT, 576, x1T, TG)
    p.wait('gpsimd', 's2_O0', o0)
    p.wait('gpsimd', 's2_O1', o1)
    p.finalize_consts()
    return p.build()


def _build_s3():
    p = P()
    ident = p.dram('ident', [128, 128], F32, kind='ExternalInput')
    x1T = p.dram('x1T', [2048, TG], F32, kind='ExternalInput')
    w1 = p.dram('w1T', [9, 2048, 1024], F32, kind='ExternalInput')
    w3 = p.dram('w3T', [9, 2048, 1024], F32, kind='ExternalInput')
    w2 = p.dram('w2T', [9, 1024, 2048], F32, kind='ExternalInput')
    wg = p.dram('wg0T', [2048, 8], F32, kind='ExternalInput')
    selh = p.dram('sel8', [8, 8, 128], F32, kind='ExternalInput')
    x2T = p.dram('x2T', [2048, TG], F32, kind='ExternalOutput')
    init_gate(p)
    cx = common_alloc(p, ident)
    o0, o1 = emit_ffn(p, cx, 'f', x1T, w1, w3, w2, wg, selh, x2T)
    p.wait('gpsimd', 'f_O0', o0)
    p.wait('gpsimd', 'f_O1', o1)
    p.finalize_consts()
    return p.build()


def _build_s4():
    p = P()
    ident = p.dram('ident', [128, 128], F32, kind='ExternalInput')
    x2T = p.dram('x2T', [2048, TG], F32, kind='ExternalInput')
    wq = p.dram('wqkv1T', [2048, 6144], F32, kind='ExternalInput')
    bq = p.dram('bqkv', [128, 48], F32, kind='ExternalInput')
    rC = p.dram('ropeC', [128, TG], F32, kind='ExternalInput')
    rS = p.dram('ropeS', [128, TG], F32, kind='ExternalInput')
    pm = p.dram('perm', [128, 128], BF16, kind='ExternalInput')
    qk = p.dram('qkvT1', [6144, TG], BF16, kind='ExternalOutput')
    init_gate(p)
    cx = common_alloc(p, ident)
    o0, o1 = emit_l1qkv(p, cx, 's4', x2T, wq, bq, rC, rS, pm, qk)
    p.wait('gpsimd', 's4_O0', o0)
    p.wait('gpsimd', 's4_O1', o1)
    p.finalize_consts()
    return p.build()


def _build_s5():
    p = P()
    ident = p.dram('ident', [128, 128], F32, kind='ExternalInput')
    qk = p.dram('qkvT1', [6144, TG], BF16, kind='ExternalInput')
    mk1 = p.dram('masks1', [7, 128, 512], BF16, kind='ExternalInput')
    skT = p.dram('sinksT', [32, 64, 16], F32, kind='ExternalInput')
    ska = p.dram('sinks_aug', [32, 16, 65], F32, kind='ExternalInput')
    at1 = p.dram('attn1T', [2048, 2048], BF16, kind='ExternalOutput')
    init_gate(p)
    cx = common_alloc(p, ident)
    s0, s1 = emit_l1attn(p, cx, 's5', qk, mk1, skT, ska, at1)
    p.wait('gpsimd', 's5_S0', s0)
    p.wait('gpsimd', 's5_S1', s1)
    p.finalize_consts()
    return p.build()


def _build_s6():
    p = P()
    ident = p.dram('ident', [128, 128], F32, kind='ExternalInput')
    at1 = p.dram('attn1T', [2048, 2048], BF16, kind='ExternalInput')
    wo = p.dram('wo1T', [2048, 2048], F32, kind='ExternalInput')
    bo = p.dram('bo', [128, 16], F32, kind='ExternalInput')
    x2T = p.dram('x2T', [2048, TG], F32, kind='ExternalInput')
    x3T = p.dram('x3T', [2048, 2048], F32, kind='ExternalOutput')
    init_gate(p)
    cx = common_alloc(p, ident)
    ares = p.sbuf('ares', [128, 16 * 2048], BF16)
    v = load_resident(p, at1, 2048, 16, ares, 'ALR')
    p.wait('tensor', 'ALR', v)
    bo_sb = p.sbuf('bo_sb', [128, 16], F32)
    p.op('gpsimd', lambda e: e.dma_start(out=bo_sb[:], in_=bo[:]), sem='BO', n=16)
    p.wait('vector', 'BO', 16)
    o0, o1 = emit_outproj_residual(p, cx, 's6', ares, wo, x2T, 64, x3T, 2048,
                                   bias_sb=bo_sb)
    p.wait('gpsimd', 's6_O0', o0)
    p.wait('gpsimd', 's6_O1', o1)
    p.finalize_consts()
    return p.build()


def _build_s7():
    p = P()
    ident = p.dram('ident', [128, 128], F32, kind='ExternalInput')
    x3T = p.dram('x3T', [2048, 2048], F32, kind='ExternalInput')
    wd = p.dram('wd1T', [2048, 2048], F32, kind='ExternalInput')
    wg = p.dram('wg1T', [2048, 8], F32, kind='ExternalInput')
    Ap = p.dram('ApackT', [2048, 128], F32, kind='ExternalInput')
    Bp = p.dram('Bpack', [128, 2048], F32, kind='ExternalInput')
    s16 = p.dram('sel16', [8, 128], F32, kind='ExternalInput')
    out = p.dram('outT', [2048, 2048], F32, kind='ExternalOutput')
    init_gate(p)
    cx = common_alloc(p, ident)
    o0, o1 = emit_l1ffn(p, cx, 's7', x3T, wd, wg, Ap, Bp, s16, out)
    p.wait('gpsimd', 's7_O0', o0)
    p.wait('gpsimd', 's7_O1', o1)
    p.finalize_consts()
    return p.build()


def kernel_unfused(**inputs):
    ins = {k: np.asarray(v) for k, v in inputs.items()}
    sh, per = host_prep(ins)
    sel8, sel16 = _sel_consts()
    N = 8
    cores = list(range(N))

    def run(nc, mk):
        res = run_bass_kernel_spmd(nc, [mk(c) for c in cores], core_ids=cores)
        return res.results

    r = run(_build_s1(), lambda c: {
        'xT': per[c]['xT'], 'ident': sh['ident'], 'wqkv0T': sh['wqkv0T'],
        'masks0': per[c]['masks0']})
    attnT = [np.asarray(r[c]['attnT0']) for c in cores]
    r = run(_build_s2(), lambda c: {
        'xT': per[c]['xT'], 'ident': sh['ident'], 'wout0T': sh['wout0T'],
        'attnT0': attnT[c]})
    x1T = [np.asarray(r[c]['x1T']) for c in cores]
    r = run(_build_s3(), lambda c: {
        'ident': sh['ident'], 'x1T': x1T[c], 'w1T': sh['w1T'], 'w3T': sh['w3T'],
        'w2T': sh['w2T'], 'wg0T': sh['wg0T'], 'sel8': sel8})
    x2T = [np.asarray(r[c]['x2T']) for c in cores]
    r = run(_build_s4(), lambda c: {
        'ident': sh['ident'], 'x2T': x2T[c], 'wqkv1T': sh['wqkv1T'],
        'bqkv': sh['bqkv'], 'ropeC': per[c]['ropeC'], 'ropeS': per[c]['ropeS'],
        'perm': per[c]['perm']})
    qkvT1 = [np.asarray(r[c]['qkvT1']) for c in cores]
    r = run(_build_s5(), lambda c: {
        'ident': sh['ident'], 'qkvT1': qkvT1[c], 'masks1': per[c]['masks1'],
        'sinksT': sh['sinksT'], 'sinks_aug': sh['sinks_aug']})
    attn1T = [np.asarray(r[c]['attn1T']) for c in cores]
    r = run(_build_s6(), lambda c: {
        'ident': sh['ident'], 'attn1T': attn1T[c], 'wo1T': sh['wo1T'],
        'bo': sh['bo'], 'x2T': x2T[c]})
    x3T = [np.asarray(r[c]['x3T']) for c in cores]
    r = run(_build_s7(), lambda c: {
        'ident': sh['ident'], 'x3T': x3T[c], 'wd1T': sh['wd1T'],
        'wg1T': sh['wg1T'], 'ApackT': sh['ApackT'], 'Bpack': sh['Bpack'],
        'sel16': sel16})
    out = np.zeros((4, 4096, 2048), np.float32)
    for c in cores:
        b, h, s0, _, _ = core_geom(c)
        out[b, s0:s0 + 2048] = np.asarray(r[c]['outT'], np.float32).T
    return out



ARENA_F32 = 44928


class _Dummy:
    def __init__(self, name, shape, dtype):
        self.name, self.shape, self.dtype = name, list(shape), dtype

    def __getitem__(self, *a):
        return self

    def rearrange(self, *a, **k):
        return self

    def bitcast(self, *a):
        return self

    def then_inc(self, *a):
        return self


class DryP(P):

    def __init__(self):
        self.em = {e: [] for e in ENGS}
        self.semh = {}
        self.cnt = {}
        self.consts_needed = set()
        self.sbuf_log = []
        self._tcache = {}

    def S(self, name):
        name = self.canon(name)
        if name not in self.semh:
            self.semh[name] = name
            self.cnt[name] = 0
        return self.semh[name]

    def op(self, eng, f, sem=None, n=1):
        if sem is not None:
            self.S(sem)
            sem = self.canon(sem)
            self.cnt[sem] += n
            return self.cnt[sem]
        return None

    def wait(self, eng, sem, val):
        pass

    def barrier(self, engines=ENGS, name=None):
        pass

    def sbuf(self, name, shape, dtype):
        if name in self._tcache:
            return self._tcache[name]
        t = _Dummy(name, shape, dtype)
        self._tcache[name] = t
        self.sbuf_log.append((name, list(shape), dtype))
        return t

    def stage_mark(self):
        self.sbuf_log.append(('__STAGE__', None, None))

    def psum(self, name, shape, dtype=F32):
        return _Dummy(name, shape, dtype)

    def dram(self, name, shape, dtype, kind):
        return _Dummy(name, shape, dtype)


class ArenaP(P):

    def __init__(self, layout, shared):
        super().__init__()
        self.layout = layout
        self.shared = shared
        self.arena = None

    def make_arena(self):
        self.arena = self.nc.alloc_sbuf_tensor("ARENA", [128, ARENA_F32], F32)

    def sbuf(self, name, shape, dtype):
        if name in self.layout:
            off, sh, dt = self.layout[name]
            assert dt == dtype and list(sh) == list(shape), name
            if dtype == F32:
                return self.arena[0:shape[0], off:off + shape[1]]
            elif dtype == BF16:
                return self.arena.bitcast(BF16)[0:shape[0], 2 * off:2 * off + shape[1]]
            else:
                raise ValueError(dtype)
        return super().sbuf(name, shape, dtype)


def _elems_f32(shape, dtype):
    n = shape[1] * (2 if dtype == F32 else 1)
    return (n + 1) // 2 * 2


def _stage_layouts(stage_logs):
    layout = {}
    for log in stage_logs:
        off = 0
        for name, shape, dtype in log:
            nf32 = shape[1] if dtype == F32 else (shape[1] + 1) // 2
            nf32 = (nf32 + 3) // 4 * 4
            layout[name] = (off, shape, dtype)
            off += nf32
        assert off <= ARENA_F32, f"stage overflow {off}"
    return layout


def emit_all_stages(p, t, upto=7):
    cx = common_alloc(p, t['ident'])
    lnx = p.sbuf('lnx', [128, 16 * TK], BF16)
    ve = emit_ln(p, cx, t['xT'], TK, lnx, 'ln0')
    p.wait('tensor', 'ln0_ve', ve)
    p.wait('vector', 'ln0_ve', ve)
    import os as _os
    _nh = int(_os.environ.get('S1NH', '16'))
    _ph = _os.environ.get('S1PH', 'ABCD')
    st = emit_l0_attn(p, cx, lnx, t['wqkv0T'], t['masks0'], t['attnT0'], 'a0',
                      nheads=_nh, phases=_ph)
    if st[0]:
        p.wait('gpsimd', 'a0_S0', st[0])
    if st[1]:
        p.wait('gpsimd', 'a0_S1', st[1])
    p.barrier()
    p.stage_mark()
    if upto < 2:
        return
    ares = p.sbuf('ares', [128, 16 * TG], BF16)
    v = load_resident(p, t['attnT0'], TG, 16, ares, 'ALR')
    p.wait('tensor', 'ALR', v)
    o0, o1 = emit_outproj_residual(p, cx, 's2', ares, t['wout0T'], t['xT'], 576,
                                   t['x1T'], TG, stats_hbm=t['stat1'])
    p.wait('gpsimd', 's2_O0', o0)
    p.wait('gpsimd', 's2_O1', o1)
    p.barrier()
    p.stage_mark()
    if upto < 3:
        return
    o0, o1, o2 = emit_ffn(p, cx, 'f', t['x1T'], t['w1T'], t['w3T'], t['w2T'],
                          t['wg0T'], t['sel8'], t['x2T'], stats_in=t['stat1'],
                          stats_out=t['stat2'])
    p.wait('gpsimd', 'f_O0', o0)
    p.wait('gpsimd', 'f_O1', o1)
    p.wait('gpsimd', 'f_O2', o2)
    p.barrier()
    p.stage_mark()
    if upto < 4:
        return
    o0, o1 = emit_l1qkv(p, cx, 's4', t['x2T'], t['wqkv1T'], t['bqkv'],
                        t['ropeC'], t['ropeS'], t['perm'], t['qkvT1'],
                        stats_in=t['stat2'])
    p.wait('gpsimd', 's4_O0', o0)
    p.wait('gpsimd', 's4_O1', o1)
    p.barrier()
    p.stage_mark()
    if upto < 5:
        return
    s0, s1 = emit_l1attn(p, cx, 's5', t['qkvT1'], t['masks1'], t['sinksT'],
                         t['sinks_aug'], t['attn1T'])
    p.wait('gpsimd', 's5_S0', s0)
    p.wait('gpsimd', 's5_S1', s1)
    p.barrier()
    p.stage_mark()
    if upto < 6:
        return
    ares6 = p.sbuf('ares6', [128, 16 * 2048], BF16)
    v = load_resident(p, t['attn1T'], 2048, 16, ares6, 'ALR6')
    p.wait('tensor', 'ALR6', v)
    bo_sb = p.sbuf('bo_sb', [128, 16], F32)
    p.op('gpsimd', lambda e: e.dma_start(out=bo_sb[:], in_=t['bo'][:]),
         sem='BO', n=16)
    p.wait('vector', 'BO', 16)
    o0, o1 = emit_outproj_residual(p, cx, 's6', ares6, t['wo1T'], t['x2T'], 64,
                                   t['x3T'], 2048, bias_sb=bo_sb,
                                   stats_hbm=t['stat3'])
    p.wait('gpsimd', 's6_O0', o0)
    p.wait('gpsimd', 's6_O1', o1)
    p.barrier()
    p.stage_mark()
    if upto < 7:
        return
    o0, o1 = emit_l1ffn(p, cx, 's7', t['x3T'], t['wd1T'], t['wg1T'],
                        t['ApackT'], t['Bpack'], t['sel16'], t['outT'],
                        stats_in=t['stat3'])
    p.wait('gpsimd', 's7_O0', o0)
    p.wait('gpsimd', 's7_O1', o1)




def _declare_drams(p, ext_scratch=False):
    t = {}
    SK = 'ExternalOutput' if ext_scratch else 'Internal'
    t['ident'] = p.dram('ident', [128, 128], F32, kind='ExternalInput')
    t['xT'] = p.dram('xT', [2048, TK], F32, kind='ExternalInput')
    t['wqkv0T'] = p.dram('wqkv0T', [2048, 6144], F32, kind='ExternalInput')
    t['masks0'] = p.dram('masks0', [17, 128, 512], BF16, kind='ExternalInput')
    t['wout0T'] = p.dram('wout0T', [2048, 2048], F32, kind='ExternalInput')
    t['w1T'] = p.dram('w1T', [9, 2048, 1024], F32, kind='ExternalInput')
    t['w3T'] = p.dram('w3T', [9, 2048, 1024], F32, kind='ExternalInput')
    t['w2T'] = p.dram('w2T', [9, 1024, 2048], F32, kind='ExternalInput')
    t['wg0T'] = p.dram('wg0T', [2048, 8], F32, kind='ExternalInput')
    t['sel8'] = p.dram('sel8', [8, 8, 128], F32, kind='ExternalInput')
    t['wqkv1T'] = p.dram('wqkv1T', [2048, 6144], F32, kind='ExternalInput')
    t['bqkv'] = p.dram('bqkv', [128, 48], F32, kind='ExternalInput')
    t['ropeC'] = p.dram('ropeC', [128, TG], F32, kind='ExternalInput')
    t['ropeS'] = p.dram('ropeS', [128, TG], F32, kind='ExternalInput')
    t['perm'] = p.dram('perm', [128, 128], BF16, kind='ExternalInput')
    t['masks1'] = p.dram('masks1', [7, 128, 512], BF16, kind='ExternalInput')
    t['sinksT'] = p.dram('sinksT', [32, 64, 16], F32, kind='ExternalInput')
    t['sinks_aug'] = p.dram('sinks_aug', [32, 16, 65], F32, kind='ExternalInput')
    t['wo1T'] = p.dram('wo1T', [2048, 2048], F32, kind='ExternalInput')
    t['bo'] = p.dram('bo', [128, 16], F32, kind='ExternalInput')
    t['wd1T'] = p.dram('wd1T', [2048, 2048], F32, kind='ExternalInput')
    t['wg1T'] = p.dram('wg1T', [2048, 8], F32, kind='ExternalInput')
    t['ApackT'] = p.dram('ApackT', [2048, 128], F32, kind='ExternalInput')
    t['Bpack'] = p.dram('Bpack', [128, 2048], F32, kind='ExternalInput')
    t['sel16'] = p.dram('sel16', [8, 128], F32, kind='ExternalInput')
    t['stat1'] = p.dram('stat1', [2, TG], F32, kind='Internal')
    t['stat2'] = p.dram('stat2', [2, TG], F32, kind='Internal')
    t['stat3'] = p.dram('stat3', [2, 2048], F32, kind='Internal')
    t['attnT0'] = p.dram('attnT0', [2048, TG], BF16, kind=SK)
    t['x1T'] = p.dram('x1T', [2048, TG], F32, kind=SK)
    t['x2T'] = p.dram('x2T', [2048, TG], F32, kind=SK)
    t['qkvT1'] = p.dram('qkvT1', [6144, TG], BF16, kind=SK)
    t['attn1T'] = p.dram('attn1T', [2048, 2048], BF16, kind=SK)
    t['x3T'] = p.dram('x3T', [2048, 2048], F32, kind=SK)
    t['outT'] = p.dram('outT', [2048, 2048], F32, kind='ExternalOutput')
    return t


def build_fused(upto=7, ext_scratch=False):
    dp = DryP()
    td = _declare_drams(dp)
    emit_all_stages(dp, td, upto)
    logs = [[]]
    shared = set()
    for name, shape, dtype in dp.sbuf_log:
        if name == '__STAGE__':
            logs.append([])
            continue
        if name.startswith('cx_') or name.startswith('lnsc_') or \
           name.startswith('gtsc_') or name == 'c_consts':
            shared.add(name)
            continue
        logs[-1].append((name, shape, dtype))
    layout = _stage_layouts(logs)
    p = ArenaP(layout, shared)
    init_gate(p)
    p.make_arena()
    t = _declare_drams(p, ext_scratch)
    emit_all_stages(p, t, upto)
    p.finalize_consts()
    return p


def kernel_fused(**inputs):
    ins = {k: np.asarray(v) for k, v in inputs.items()}
    sh, per = host_prep(ins)
    sel8, sel16 = _sel_consts()
    p = build_fused()
    nc = p.build()
    cores = list(range(8))
    in_maps = []
    for c in cores:
        m = {'ident': sh['ident'], 'xT': per[c]['xT'], 'wqkv0T': sh['wqkv0T'],
             'masks0': per[c]['masks0'], 'wout0T': sh['wout0T'],
             'w1T': sh['w1T'], 'w3T': sh['w3T'], 'w2T': sh['w2T'],
             'wg0T': sh['wg0T'], 'sel8': sel8, 'wqkv1T': sh['wqkv1T'],
             'bqkv': sh['bqkv'], 'ropeC': per[c]['ropeC'],
             'ropeS': per[c]['ropeS'], 'perm': per[c]['perm'],
             'masks1': per[c]['masks1'], 'sinksT': sh['sinksT'],
             'sinks_aug': sh['sinks_aug'], 'wo1T': sh['wo1T'], 'bo': sh['bo'],
             'wd1T': sh['wd1T'], 'wg1T': sh['wg1T'], 'ApackT': sh['ApackT'],
             'Bpack': sh['Bpack'], 'sel16': sel16}
        in_maps.append(m)
    res = run_bass_kernel_spmd(nc, in_maps, core_ids=cores)
    out = np.zeros((4, 4096, 2048), np.float32)
    for c in cores:
        b, h, s0, _, _ = core_geom(c)
        out[b, s0:s0 + 2048] = np.asarray(res.results[c]['outT'], np.float32).T
    return out


kernel = kernel_fused


# version 27
# speedup vs baseline: 1.0457x; 1.0457x over previous
import numpy as np
import ml_dtypes
import concourse.bass as bass
import concourse.mybir as mybir

F32 = mybir.dt.float32
BF16 = mybir.dt.bfloat16
AF = mybir.ActivationFunctionType
ALU = mybir.AluOpType
AX = mybir.AxisListType
BF = ml_dtypes.bfloat16

C_SHIFT = 12.0
MASK_NEG = -200.0
EPS = 1e-5
TG = 2176
TK = 2816
OWN0, OWN1 = 64, 2112

ENGS = ('sync', 'gpsimd', 'vector', 'scalar', 'tensor')


def chunks(T, step=512):
    out = []
    t = 0
    while t < T:
        out.append((t, min(step, T - t)))
        t += step
    return out


class P:
    def __init__(self, name="prog"):
        self.nc = bass.Bass("TRN2", target_bir_lowering=False)
        self.em = {e: [] for e in ENGS}
        self.semh = {}
        self.cnt = {}
        self.consts_needed = set()

    @staticmethod
    def canon(name):
        return name.split('_', 1)[1] if '_' in name else name

    def S(self, name):
        name = self.canon(name)
        if name not in self.semh:
            self.semh[name] = self.nc.alloc_semaphore("sem_" + name)
            self.cnt[name] = 0
        return self.semh[name]

    def val(self, sem):
        self.S(sem)
        return self.cnt[self.canon(sem)]

    def op(self, eng, f, sem=None, n=1):
        if sem is not None:
            h = self.S(sem)
            sem = self.canon(sem)
            self.cnt[sem] += n
            nn = n

            def g(e):
                f(e).then_inc(h, nn)
            self.em[eng].append(g)
            return self.cnt[sem]
        else:
            self.em[eng].append(lambda e: f(e))
            return None

    def wait(self, eng, sem, val):
        if val <= 0:
            return
        h = self.S(sem)
        self.em[eng].append(lambda e: e.wait_ge(h, val))

    def stage_mark(self):
        pass

    def barrier(self, engines=ENGS, name=None):
        vals = {}
        for eng in engines:
            vals[eng] = self.op(eng, lambda e: e.nop(), sem=f"SB{eng}")
        for eng in engines:
            for o in engines:
                if o != eng:
                    self.wait(eng, f"SB{o}", vals[o])

    def sbuf(self, name, shape, dtype):
        if not hasattr(self, '_tcache'):
            self._tcache = {}
        if name in self._tcache:
            return self._tcache[name]
        t = self.nc.alloc_sbuf_tensor(name, list(shape), dtype)
        self._tcache[name] = t
        return t

    def psum(self, name, shape, dtype=F32):
        return self.nc.alloc_psum_tensor(name, list(shape), dtype)

    def dram(self, name, shape, dtype, kind):
        return self.nc.dram_tensor(name, list(shape), dtype, kind=kind)

    def const(self, v):
        self.consts_needed.add(float(v))
        return float(v)

    def finalize_consts(self):
        vals = sorted(self.consts_needed)
        ct = self.sbuf("c_consts", [1, max(1, len(vals) + 1)], F32)
        pre = []
        for i, v in enumerate(vals):
            ap = ct[0:1, i:i + 1]
            self.nc.const_aps.register(F32, v, ap)
            pre.append(lambda e, ap=ap, vv=v: e.memset(ap, vv))
        h = self.S('init')
        pre.append(lambda e, ct=ct: e.memset(
            ct[0:1, len(vals):len(vals) + 1], 0.0).then_inc(h, 1))
        self.em['vector'] = pre + self.em['vector']

    def build(self):
        with self.nc.Block() as block:
            for eng in ENGS:
                if not self.em[eng]:
                    continue
                fns = self.em[eng]

                def runner(e, fns=fns):
                    for f in fns:
                        f(e)
                getattr(block, eng)(runner)
        return self.nc


def init_gate(p):
    p.S('init')
    p.cnt['init'] = 1
    for eng in ENGS:
        if eng != 'vector':
            p.wait(eng, 'init', 1)


def core_geom(c):
    b, h = c // 2, c % 2
    s0 = 2048 * h
    return b, h, s0, s0 - 640, s0 - 64


L0_QCHUNKS = [(0, 512), (512, 512), (1024, 512), (1536, 512), (2048, 128)]
L0_NKK = [9, 9, 9, 9, 6]
L1_QCHUNKS = [(64 + 512 * i, 512) for i in range(4)]


def l0_mask_slot(qc, kk):
    if qc == 0 and kk <= 4:
        return 9 + kk
    if qc == 1 and kk == 0:
        return 14
    if qc == 4 and kk >= 4:
        return 15 + (kk - 4)
    return kk


def l1_mask_slot(qc, kk):
    if qc == 0 and kk == 0:
        return 5
    if qc == 3 and kk == 4:
        return 6
    return kk


def build_l0_masks(c):
    _, h, s0, gk0, g0 = core_geom(c)
    out = np.full((17, 128, 512), MASK_NEG, np.float32)
    for kk in range(9):
        jj = np.arange(128)[:, None] + kk * 128
        ii = np.arange(512)[None, :]
        band = (jj >= ii + 64) & (jj <= ii + 576)
        out[kk] = np.where(band, -C_SHIFT, MASK_NEG)
    for qc, kk in ([(0, k) for k in range(5)] + [(1, 0), (4, 4), (4, 5)]):
        qq, nq = L0_QCHUNKS[qc]
        jj = np.arange(128)[:, None] + 4 * qc * 128 + kk * 128
        ii = np.arange(nq)[None, :] + qq
        band = (jj >= ii + 64) & (jj <= ii + 576)
        valid = band & (gk0 + jj >= 0)
        qpos = g0 + ii
        phantom = (qpos < 0) | (qpos >= 4096)
        valid = np.where(phantom, jj == ii + 576, valid)
        out[l0_mask_slot(qc, kk), :, :nq] = np.where(valid, -C_SHIFT, MASK_NEG)
    return out.astype(BF)


def build_l1_masks(c):
    _, h, s0, gk0, g0 = core_geom(c)
    out = np.full((7, 128, 512), MASK_NEG, np.float32)
    for qc, (qq, nq) in enumerate(L1_QCHUNKS):
        for kk in range(5):
            jj = np.arange(128)[:, None] + (qq - 64) + kk * 128
            ii = np.arange(nq)[None, :] + qq
            valid = (np.abs(ii - jj) <= 64) & (g0 + jj >= 0) & (g0 + jj < 4096)
            out[l1_mask_slot(qc, kk), :, :nq] = np.where(valid, -C_SHIFT, MASK_NEG)
    return out.astype(BF)


def build_rope(c):
    _, _, _, _, g0 = core_geom(c)
    half = 32
    inv_freq = (1.0 / (10000.0 ** (np.arange(half, dtype=np.float64) / half))).astype(np.float32)
    pos = (g0 + np.arange(TG)).astype(np.float32)
    ang = pos[None, :] * inv_freq[:, None]
    cos, sin = np.cos(ang).astype(np.float32), np.sin(ang).astype(np.float32)
    r = np.arange(128)
    f = r % 32
    Cfull = cos[f]
    sign = np.where((r % 64) < 32, -1.0, 1.0).astype(np.float32)
    Sfull = sin[f] * sign[:, None]
    perm = np.zeros((128, 128), np.float32)
    for m in range(128):
        base = (m // 64) * 64
        perm[base + (m - base + 32) % 64, m] = 1.0
    return Cfull, Sfull, perm.astype(BF)


def slice_pad_T(x_bse, b, start, length):
    S = x_bse.shape[1]
    out = np.zeros((length, x_bse.shape[2]), np.float32)
    lo, hi = max(0, start), min(S, start + length)
    if hi > lo:
        out[lo - start:hi - start] = x_bse[b, lo:hi]
    return np.ascontiguousarray(out.T)


def host_prep(ins):
    sh = {}
    sh['wqkv0T'] = np.ascontiguousarray(ins['L0_attn_w_qkv'].T)
    sh['wout0T'] = np.ascontiguousarray(ins['L0_attn_w_out'].T)
    sh['wg0T'] = np.ascontiguousarray(ins['L0_ffn_gate_w'].T)
    w1T = np.concatenate([np.transpose(ins['L0_ffn_w1'], (0, 2, 1)),
                          ins['L0_ffn_ws1'].T[None]], 0)
    w3T = np.concatenate([np.transpose(ins['L0_ffn_w3'], (0, 2, 1)),
                          ins['L0_ffn_ws3'].T[None]], 0)
    w2T = np.concatenate([np.transpose(ins['L0_ffn_w2'], (0, 2, 1)),
                          ins['L0_ffn_ws2'].T[None]], 0)
    sh['w1T'] = np.ascontiguousarray(w1T)
    sh['w3T'] = np.ascontiguousarray(w3T)
    sh['w2T'] = np.ascontiguousarray(w2T)
    sh['wqkv1T'] = np.ascontiguousarray(ins['L1_attn_Wqkv'].T)
    sh['bqkv'] = np.ascontiguousarray(ins['L1_attn_bqkv'].reshape(48, 128).T)
    sh['wo1T'] = np.ascontiguousarray(ins['L1_attn_Wo'].T)
    sh['bo'] = np.ascontiguousarray(ins['L1_attn_bo'].reshape(16, 128).T)
    sk = ins['L1_attn_sinks']
    sh['sinksT'] = np.ascontiguousarray(np.transpose(sk, (0, 2, 1)))
    sh['sinks_aug'] = np.ascontiguousarray(
        np.concatenate([sk, np.ones((32, 16, 1), np.float32)], 2))
    sh['wd1T'] = np.ascontiguousarray(ins['L1_ffn_weight'].T)
    sh['wg1T'] = np.ascontiguousarray(ins['L1_ffn_gate_w'].T)
    sh['ApackT'] = np.ascontiguousarray(ins['L1_ffn_A_w'].reshape(128, 2048).T)
    sh['Bpack'] = np.ascontiguousarray(
        np.transpose(ins['L1_ffn_B_w'], (0, 2, 1)).reshape(128, 2048))
    sh['ident'] = np.eye(128, dtype=np.float32)
    per = []
    for c in range(8):
        b, h, s0, gk0, g0 = core_geom(c)
        d = {}
        d['xT'] = slice_pad_T(ins['L0_attn_x'], b, gk0, TK)
        d['masks0'] = build_l0_masks(c)
        d['masks1'] = build_l1_masks(c)
        Cf, Sf, perm = build_rope(c)
        d['ropeC'] = Cf
        d['ropeS'] = Sf
        d['perm'] = perm.astype(BF)
        per.append(d)
    return sh, per


class Ctx:
    pass


def common_alloc(p, ident_hbm):
    cx = Ctx()
    cx.ident = p.sbuf("cx_ident", [128, 128], BF16)
    cx.ones_col = p.sbuf("cx_ones_col", [128, 1], BF16)
    cx.ones_row = p.sbuf("cx_ones_row", [1, 128], F32)
    cx.ps = [p.psum(f"cx_ps{i}", [128, 512], F32) for i in range(7)]
    cx.psbf = p.psum("cx_psbf", [128, 512], BF16)
    identf = p.sbuf("cx_identf", [128, 128], F32)
    cx.identf = identf
    p.op('gpsimd', lambda e: e.dma_start(out=identf[:], in_=ident_hbm[:]),
         sem='ALI', n=16)
    p.wait('vector', 'ALI', 16)
    p.op('vector', lambda e: e.tensor_copy(cx.ident[:], identf[:]))
    p.op('vector', lambda e: e.memset(cx.ones_col[:], 1.0))
    p.op('vector', lambda e: e.memset(cx.ones_row[:], 1.0))
    cx.expbias = p.sbuf("cx_expbias", [128, 1], F32)
    p.op('vector', lambda e: e.memset(cx.expbias[:], -C_SHIFT))
    cx.neg1 = p.sbuf("cx_neg1", [128, 1], F32)
    v = p.op('vector', lambda e: e.memset(cx.neg1[:], -1.0), sem='cominit')
    p.wait('vector', 'cominit', v)
    return cx


def emit_ln(p, cx, src_hbm, T, lnx, pref, bank0=0, col0=0):
    nk = 16
    ps_sum = cx.ps[bank0][0:1, :]
    ps_sq = cx.ps[bank0 + 1][0:1, :]
    ps_mb = cx.ps[bank0 + 2]
    ps_rb = cx.ps[bank0 + 3]
    xin = p.sbuf("lnsc_xin", [128, 5 * 512], F32)
    xbf = p.sbuf("lnsc_xbf", [128, 2 * 512], BF16)
    sqbf = p.sbuf("lnsc_sqbf", [128, 2 * 512], BF16)
    rows = p.sbuf("lnsc_rows", [1, 2 * 512], F32)
    mrow = p.sbuf("lnsc_mrow", [128, 512], F32)
    rrow = p.sbuf("lnsc_rrow", [128, 512], F32)
    inv = p.const(1.0 / 2048.0)
    eps = p.const(EPS)
    neghalf = p.const(-0.5)
    SA, SC, SM, SV, SP = (f'{pref}_ac', f'{pref}_cv', f'{pref}_mm',
                          f'{pref}_ve', f'{pref}_pl')
    for ci, (t0, nt) in enumerate(chunks(T)):
        p.wait('tensor', SC, p.val(SC))
        if ci > 0:
            p.wait('gpsimd', SV, p.val(SV))
            p.wait('sync', SV, p.val(SV))
        sm0 = p.val(SM)
        for k in range(nk):
            sl = k % 4
            kp = k % 2
            q = 'gpsimd' if k % 2 == 0 else 'sync'
            if k >= 4:
                p.wait(q, SM, sm0 + 2 * (k - 4) + 2)
            alv = p.op(q, lambda e, k=k, sl=sl, t0=t0, nt=nt: e.dma_start(
                out=xin[:, sl * 512:sl * 512 + nt],
                in_=src_hbm[k * 128:(k + 1) * 128, col0 + t0:col0 + t0 + nt]),
                sem=f'{pref}_L{sl}', n=16)
            p.wait('scalar', f'{pref}_L{sl}', alv)
            p.wait('vector', f'{pref}_L{sl}', alv)
            if k >= 2:
                p.wait('scalar', SM, sm0 + 2 * (k - 2) + 2)
                p.wait('vector', SM, sm0 + 2 * (k - 2) + 2)
            p.op('scalar', lambda e, sl=sl, kp=kp, nt=nt: e.square(
                sqbf[:, kp * 512:kp * 512 + nt], xin[:, sl * 512:sl * 512 + nt]),
                sem=SA)
            p.op('vector', lambda e, sl=sl, kp=kp, nt=nt: e.tensor_copy(
                xbf[:, kp * 512:kp * 512 + nt], xin[:, sl * 512:sl * 512 + nt]),
                sem=SC)
            p.wait('tensor', SA, p.val(SA))
            p.wait('tensor', SC, p.val(SC))
            p.op('tensor', lambda e, k=k, kp=kp, nt=nt: e.matmul(
                ps_sum[:, 0:nt], cx.ones_col[:], xbf[:, kp * 512:kp * 512 + nt],
                start=(k == 0), stop=(k == nk - 1)))
            p.op('tensor', lambda e, k=k, kp=kp, nt=nt: e.matmul(
                ps_sq[:, 0:nt], cx.ones_col[:], sqbf[:, kp * 512:kp * 512 + nt],
                start=(k == 0), stop=(k == nk - 1)), sem=SM, n=2)
        p.wait('vector', SM, p.val(SM))
        v = p.op('vector', lambda e, nt=nt: e.tensor_scalar(
            rows[0:1, 0:nt], ps_sum[:, 0:nt], inv, None, op0=ALU.mult), sem=SC)
        v = p.op('vector', lambda e, nt=nt: e.tensor_scalar(
            rows[0:1, 512:512 + nt], ps_sq[:, 0:nt], inv, None, op0=ALU.mult), sem=SC)
        p.wait('vector', SC, v)
        v = p.op('vector', lambda e, nt=nt: e.tensor_tensor(
            mrow[0:1, 0:nt], rows[0:1, 0:nt], rows[0:1, 0:nt],
            op=ALU.mult), sem=SC)
        p.wait('vector', SC, v)
        v = p.op('vector', lambda e, nt=nt: e.tensor_tensor(
            rows[0:1, 512:512 + nt], rows[0:1, 512:512 + nt],
            mrow[0:1, 0:nt], op=ALU.subtract), sem=SC)
        p.wait('scalar', SC, p.val(SC))
        sa1 = p.op('scalar', lambda e, nt=nt: e.activation(
            mrow[0:1, 0:nt], rows[0:1, 512:512 + nt], AF.Ln,
            bias=eps), sem=SA)
        p.wait('scalar', SA, sa1)
        p.op('scalar', lambda e, nt=nt: e.activation(
            rows[0:1, 512:512 + nt], mrow[0:1, 0:nt], AF.Exp,
            scale=neghalf), sem=SA)
        p.wait('vector', SA, p.val(SA))
        p.op('vector', lambda e: e.nop(), sem=SC)
        p.wait('tensor', SC, p.val(SC))
        p.wait('tensor', SP, p.val(SP))
        p.op('tensor', lambda e, nt=nt: e.matmul(
            ps_mb[:, 0:nt], cx.ones_row[:, :], rows[0:1, 0:nt], start=True, stop=True))
        p.op('tensor', lambda e, nt=nt: e.matmul(
            ps_rb[:, 0:nt], cx.ones_row[:, :], rows[0:1, 512:512 + nt],
            start=True, stop=True), sem=SM)
        p.wait('vector', SM, p.val(SM))
        p.wait('vector', SP, p.val(SP))
        p.op('vector', lambda e, nt=nt: e.tensor_copy(mrow[:, 0:nt], ps_mb[:, 0:nt]))
        v = p.op('vector', lambda e, nt=nt: e.tensor_copy(rrow[:, 0:nt], ps_rb[:, 0:nt]),
             sem=SC)
        p.wait('vector', SC, v)
        scv = p.val(SC)
        sv0 = p.val(SV)
        p.wait('gpsimd', SC, scv)
        p.wait('gpsimd', SV, sv0)
        p.wait('sync', SC, scv)
        p.wait('sync', SV, sv0)
        lvs = []
        for k in range(4):
            q = 'gpsimd' if k % 2 == 0 else 'sync'
            lvs.append(p.op(q, lambda e, k=k, t0=t0, nt=nt: e.dma_start(
                out=xin[:, (k % 4) * 512:(k % 4) * 512 + nt],
                in_=src_hbm[k * 128:(k + 1) * 128, col0 + t0:col0 + t0 + nt]),
                sem=f'{pref}_L{k % 4}', n=16))
        for k in range(nk):
            sl = k % 4
            p.wait('gpsimd', f'{pref}_L{sl}', lvs[k])
            pl = p.op('gpsimd', lambda e, sl=sl, nt=nt: e.tensor_sub(
                xin[:, sl * 512:sl * 512 + nt], xin[:, sl * 512:sl * 512 + nt],
                mrow[:, 0:nt]), sem=SP)
            p.wait('vector', SP, pl)
            p.op('vector', lambda e, k=k, sl=sl, nt=nt, t0=t0: e.tensor_tensor(
                lnx[:, k * T + t0:k * T + t0 + nt], xin[:, sl * 512:sl * 512 + nt],
                rrow[:, 0:nt], op=ALU.mult), sem=SV)
            if k + 4 < nk:
                q = 'gpsimd' if k % 2 == 0 else 'sync'
                p.wait(q, SV, sv0 + k + 1)
                lvs.append(p.op(q, lambda e, k=k, t0=t0, nt=nt: e.dma_start(
                    out=xin[:, ((k + 4) % 4) * 512:((k + 4) % 4) * 512 + nt],
                    in_=src_hbm[(k + 4) * 128:(k + 5) * 128,
                                col0 + t0:col0 + t0 + nt]),
                    sem=f'{pref}_L{(k + 4) % 4}', n=16))
    return p.val(SV)


def load_wcol(p, pref, wsrc, w32ap, wbfap, ncols, nk=16, queue='sync'):
    nslab = (nk + 3) // 4
    for s in range(min(nslab, (nk + 3) // 4)):
        k0 = s * 4
        kn = min(4, nk - k0)
        p.op(queue, lambda e, k0=k0, kn=kn, s=s: e.dma_start(
            out=w32ap[:, k0 * ncols:(k0 + kn) * ncols]
                .rearrange("p (k f) -> p k f", k=kn),
            in_=wsrc[k0 * 128:(k0 + kn) * 128, :]
                .rearrange("(k p) f -> p k f", p=128)), sem='WL', n=16)
    wl = p.val('WL')
    p.wait('scalar', 'WL', wl)
    p.op('scalar', lambda e: e.copy(wbfap[:, 0:nk * ncols], w32ap[:, 0:nk * ncols]),
         sem=f'{pref}_cast')
    return wl, p.val(f'{pref}_cast')


def emit_l0_attn(p, cx, lnx, wqkv, masks_hbm, attnT_hbm, pref='a0', nheads=16, phases='ABCD'):
    msk = p.sbuf(f'{pref}_msk', [128, 17 * 512], BF16)
    p.op('gpsimd', lambda e: e.dma_start(
        out=msk.rearrange("p (s f) -> p s f", s=17),
        in_=masks_hbm.rearrange("s p f -> p s f")), sem='ALM', n=16)
    mskv = p.val('ALM')
    qh = p.sbuf(f'{pref}_qh', [128, TG], BF16)
    kh = p.sbuf(f'{pref}_kh', [128, TK], BF16)
    vh = p.sbuf(f'{pref}_vh', [128, TK], BF16)
    v_tm = p.sbuf(f'{pref}_vtm', [128, 22 * 128], BF16)
    w32 = p.sbuf(f'{pref}_w32', [128, 4 * 512], F32)
    wbf = p.sbuf(f'{pref}_wbf', [128, 2 * 2048], BF16)
    slab_q = [0]
    qc_vals = []
    sexp = p.sbuf(f'{pref}_sexp', [128, 4 * 512], F32)
    pT = p.sbuf(f'{pref}_pT', [128, 4 * 512], BF16)
    rrow1 = p.sbuf(f'{pref}_rrow1', [1, 512], F32)
    lnt = p.sbuf(f'{pref}_lnt', [128, 512], F32)
    qscl = p.sbuf(f'{pref}_qscl', [128, 1], F32)
    v = p.op('vector', lambda e: e.memset(qscl[:], float(1.0 / np.sqrt(128.0))),
             sem=f'{pref}_cm')
    p.wait('vector', f'{pref}_cm', v)
    rb = p.sbuf(f'{pref}_rb', [128, 512], F32)
    att_out = p.sbuf(f'{pref}_att', [128, 2 * TG], BF16)
    SCAST, SMMD, SEVQ = f'{pref}_cast', f'{pref}_mmd', f'{pref}_evq'
    SMS, SCM, SAE, SMP, SCR, SMB, SST = (f'{pref}_ms', f'{pref}_cm', f'{pref}_ae',
                                         f'{pref}_mp', f'{pref}_cr', f'{pref}_mb', 'ST')
    p.wait('vector', 'ALM', mskv)
    cast_done = []
    mmdone = []
    st_after_head = []
    kki = 0
    msv, cmv, aev, mpv = {}, {}, {}, {}
    evq = 0
    tr_copy_done = []
    for h in range(nheads):
        vev = []
        for idx, part in enumerate((1, 2, 0)):
            pseq = h * 3 + idx
            par = pseq % 2
            col = part * 2048 + h * 128
            first_cast = True
            for s in range(4):
                sl = slab_q[0] % 4
                if slab_q[0] >= 4:
                    p.wait('sync', SCAST, qc_vals[slab_q[0] - 4])
                dv = p.op('sync', lambda e, s=s, sl=sl, col=col: e.dma_start(
                    out=w32[:, sl * 512:(sl + 1) * 512]
                        .rearrange("p (k f) -> p k f", k=4),
                    in_=wqkv[s * 512:(s + 1) * 512, col:col + 128]
                        .rearrange("(k p) f -> p k f", p=128)),
                    sem=f'{pref}_WS{sl}', n=16)
                p.wait('scalar', f'{pref}_WS{sl}', dv)
                if first_cast and pseq >= 2:
                    p.wait('scalar', SMMD, mmdone[pseq - 2])
                first_cast = False
                qc_vals.append(p.op('scalar', lambda e, sl=sl, s=s, par=par: e.copy(
                    wbf[:, par * 2048 + s * 512:par * 2048 + (s + 1) * 512],
                    w32[:, sl * 512:(sl + 1) * 512]), sem=SCAST))
                slab_q[0] += 1
            cast_done.append(p.val(SCAST))
            p.wait('tensor', SCAST, cast_done[pseq])
            T = TG if part == 0 else TK
            roff = 576 if part == 0 else 0
            dst = (qh, kh, vh)[part]
            nchq = len(chunks(T))
            for ci, (t0, nt) in enumerate(chunks(T)):
                bk = ci % 2
                p.wait('tensor', SEVQ, evq if (ci == 0 and idx == 0) else evq - 1)
                for k in range(16):
                    p.op('tensor', lambda e, k=k, par=par, bk=bk, t0=t0, nt=nt,
                         roff=roff, T=T: e.matmul(
                             cx.ps[bk][:, 0:nt],
                             wbf[:, par * 2048 + k * 128:par * 2048 + (k + 1) * 128],
                             lnx[:, k * TK + roff + t0:k * TK + roff + t0 + nt],
                             start=(k == 0), stop=(k == 15)),
                         sem=(SMMD if k == 15 else None))
                mmv = p.val(SMMD)
                p.wait('vector', SMMD, mmv)
                if part == 0:
                    evq = p.op('vector', lambda e, bk=bk, t0=t0, nt=nt, dst=dst:
                               e.tensor_scalar(dst[:, t0:t0 + nt], cx.ps[bk][:, 0:nt],
                                               qscl[:, 0:1], None, op0=ALU.mult),
                               sem=SEVQ)
                else:
                    evq = p.op('vector', lambda e, bk=bk, t0=t0, nt=nt, dst=dst:
                               e.tensor_copy(dst[:, t0:t0 + nt], cx.ps[bk][:, 0:nt]),
                               sem=SEVQ)
                    if part == 2:
                        vev.append(evq)
                if part == 0:
                    kt_tgt = (22 * (ci + 1) + nchq - 1) // nchq
                    while len(tr_copy_done) < h * 22 + min(22, kt_tgt):
                        kt = len(tr_copy_done) - h * 22
                        gt = len(tr_copy_done)
                        tslot = gt % 2
                        tbank = cx.psbf[:, 0:128] if tslot == 0 else \
                            cx.ps[6].bitcast(BF16)[:, 0:128]
                        p.wait('tensor', SEVQ, vev[((kt + 1) * 128 - 1) // 512])
                        if gt >= 2:
                            p.wait('tensor', SCM, tr_copy_done[gt - 2])
                        p.op('tensor', lambda e, kt=kt, tbank=tbank: e.matmul(
                            tbank, vh[:, kt * 128:(kt + 1) * 128],
                            cx.ident[:], start=True, stop=True,
                            is_transpose=True), sem=SMS)
                        p.wait('vector', SMS, p.val(SMS))
                        tr_copy_done.append(p.op(
                            'vector', lambda e, kt=kt, tbank=tbank: e.tensor_copy(
                                v_tm[:, kt * 128:(kt + 1) * 128].bitcast(F32),
                                tbank.bitcast(F32)), sem=SCM))
            mmdone.append(p.val(SMMD))
        vtm_done = p.val(SCM)
        if 'C' not in phases:
            continue
        p.wait('tensor', SCM, vtm_done)
        tiles = []
        for qc, (qq, nq) in enumerate(L0_QCHUNKS):
            for kk in range(L0_NKK[qc]):
                tiles.append((qc, kk, L0_NKK[qc], qq, nq))
        LA = 3
        ntl = len(tiles)
        kki0 = kki
        for it in range(ntl + LA):
            if it < ntl:
                qc, kk, nkk, qq, nq = tiles[it]
                kg = kki0 + it
                kp = kg % 4
                kbase = (4 * qc + kk) * 128
                slot = l0_mask_slot(qc, kk)
                if kg >= 4:
                    p.wait('tensor', SCM, cmv[kg - 4])
                msv[kg] = p.op('tensor', lambda e, kp=kp, kbase=kbase, qq=qq,
                               nq=nq: e.matmul(
                    cx.ps[kp][:, 0:nq], kh[:, kbase:kbase + 128],
                    qh[:, qq:qq + nq], start=True, stop=True), sem=SMS)
                p.wait('vector', SMS, msv[kg])
                if kg >= 4:
                    p.wait('vector', SAE, aev[kg - 4])
                cmv[kg] = p.op('vector', lambda e, kp=kp, slot=slot, nq=nq:
                               e.tensor_tensor(
                    sexp[:, kp * 512:kp * 512 + nq], cx.ps[kp][:, 0:nq],
                    msk[:, slot * 512:slot * 512 + nq], op=ALU.add), sem=SCM)
                p.wait('scalar', SCM, cmv[kg])
                if kg >= 4:
                    p.wait('scalar', SMP, mpv[kg - 4])
                aev[kg] = p.op('scalar', lambda e, kp=kp, nq=nq: e.activation(
                    pT[:, kp * 512:kp * 512 + nq], sexp[:, kp * 512:kp * 512 + nq],
                    AF.Exp), sem=SAE)
            jt = it - LA
            if jt < 0:
                continue
            qc, kk, nkk, qq, nq = tiles[jt]
            kg = kki0 + jt
            kp = kg % 4
            kbase = (4 * qc + kk) * 128
            p.wait('tensor', SAE, aev[kg])
            if kk == 0 and qc > 0:
                p.wait('tensor', SEVQ, evq)
            p.op('tensor', lambda e, kp=kp, kbase=kbase, nq=nq, kk=kk, nkk=nkk:
                 e.matmul(
                     cx.ps[4][:, 0:nq], v_tm[:, kbase:kbase + 128],
                     pT[:, kp * 512:kp * 512 + nq],
                     start=(kk == 0), stop=(kk == nkk - 1)))
            p.op('tensor', lambda e, kp=kp, nq=nq, kk=kk, nkk=nkk: e.matmul(
                cx.ps[5][0:1, 0:nq], cx.ones_col[:],
                pT[:, kp * 512:kp * 512 + nq],
                start=(kk == 0), stop=(kk == nkk - 1)),
                sem=SMP, n=2)
            mpv[kg] = p.val(SMP)
            if kk == nkk - 1:
                p.wait('vector', SMP, mpv[kg])
                v = p.op('vector', lambda e, nq=nq: e.tensor_copy(
                    rrow1[0:1, 0:nq], cx.ps[5][0:1, 0:nq]), sem=SCR)
                p.wait('tensor', SCR, v)
                p.op('tensor', lambda e, nq=nq: e.matmul(
                    cx.ps[6][:, 0:nq], cx.ones_row[:, :], rrow1[0:1, 0:nq],
                    start=True, stop=True), sem=SMB)
                p.wait('scalar', SMB, p.val(SMB))
                av = p.op('scalar', lambda e, nq=nq: e.activation(
                    lnt[:, 0:nq], cx.ps[6][:, 0:nq], AF.Ln), sem=SAE)
                p.wait('scalar', SAE, av)
                av = p.op('scalar', lambda e, nq=nq: e.activation(
                    rb[:, 0:nq], lnt[:, 0:nq], AF.Exp, scale=cx.neg1[:, 0:1]),
                    sem=SAE)
                p.wait('vector', SAE, av)
                if h >= 2:
                    p.wait('vector', f'{pref}_S{h % 2}', st_after_head[h - 2])
                evq = p.op('vector', lambda e, nq=nq, qq=qq, h=h: e.tensor_tensor(
                    att_out[:, (h % 2) * TG + qq:(h % 2) * TG + qq + nq],
                    cx.ps[4][:, 0:nq], rb[:, 0:nq], op=ALU.mult), sem=SEVQ)
        kki = kki0 + ntl
        if 'D' not in phases:
            continue
        p.wait('gpsimd', SEVQ, evq)
        p.op('gpsimd', lambda e, h=h: e.dma_start(
            out=attnT_hbm[h * 128:(h + 1) * 128, :],
            in_=att_out[:, (h % 2) * TG:(h % 2) * TG + TG]),
            sem=f'{pref}_S{h % 2}', n=16)
        st_after_head.append(p.val(f'{pref}_S{h % 2}'))
    return (p.val(f'{pref}_S0'), p.val(f'{pref}_S1'))


def load_resident(p, hbm, T, nk, dst, sem):
    for k in range(nk):
        v = p.op('gpsimd', lambda e, k=k: e.dma_start(
            out=dst[:, k * T:(k + 1) * T], in_=hbm[k * 128:(k + 1) * 128, :]),
            sem=sem, n=16)
    return v


def emit_dense2(p, cx, pref, wsrcs, X_ap, Tch, nk, evac, extra_mm=None,
                bank0=0):
    nch = len(Tch)
    nslab = (nk + 3) // 4
    w32 = p.sbuf(f"{pref}_w32", [128, 2 * nk * 128], F32)
    wbf = p.sbuf(f"{pref}_wbf", [128, 2 * nk * 128], BF16)
    SCAST, SMMD, SEV = f'{pref}_cast', f'{pref}_mmd', f'{pref}_ev'
    sev_map = {}
    cast_vals = []
    mmd_vals = []
    for oi, wsrc in enumerate(wsrcs):
        par = oi % 2
        if oi >= 2:
            p.wait('sync', SCAST, cast_vals[oi - 2])
        for s in range(nslab):
            k0 = s * 4
            kn = min(4, nk - k0)
            p.op('sync', lambda e, k0=k0, kn=kn, par=par, wsrc=wsrc: e.dma_start(
                out=w32[:, par * nk * 128 + k0 * 128:par * nk * 128 + (k0 + kn) * 128]
                    .rearrange("p (k f) -> p k f", k=kn),
                in_=wsrc[k0 * 128:(k0 + kn) * 128, :]
                    .rearrange("(k p) f -> p k f", p=128)), sem=f'{pref}_W{par}', n=16)
        p.wait('scalar', f'{pref}_W{par}', p.val(f'{pref}_W{par}'))
        if oi >= 2:
            p.wait('scalar', SMMD, mmd_vals[oi - 2])
        cast_vals.append(p.op('scalar', lambda e, par=par: e.copy(
            wbf[:, par * nk * 128:(par + 1) * nk * 128],
            w32[:, par * nk * 128:(par + 1) * nk * 128]), sem=SCAST))
        p.wait('tensor', SCAST, cast_vals[oi])
        for ci, (t0, nt) in enumerate(Tch):
            if oi >= 1:
                p.wait('tensor', SEV, sev_map[(oi - 1, ci)])
            for k in range(nk):
                last = (k == nk - 1) and (extra_mm is None)
                p.op('tensor', lambda e, k=k, par=par, ci=ci, t0=t0, nt=nt,
                     last=last: e.matmul(
                         cx.ps[bank0 + ci][:, 0:nt],
                         wbf[:, par * nk * 128 + k * 128:par * nk * 128 + (k + 1) * 128],
                         X_ap(k, t0, nt), start=(k == 0), stop=last),
                     sem=(SMMD if last else None))
            if extra_mm is not None:
                extra_mm(p, oi, ci, cx.ps[bank0 + ci], t0, nt, SMMD)
            fv = evac(p, oi, cx.ps[bank0 + ci], t0, nt, ci)
            sev_map[(oi, ci)] = fv if fv else p.val(SEV)
        mmd_vals.append(p.val(SMMD))
    return p.val(SEV)


def emit_ffn(p, cx, pref, x1_hbm, w1_hbm, w3_hbm, w2_hbm, wg_hbm, sel_hbm,
             x2_hbm, stats_in=None, stats_out=None):
    TH = 1088
    lnx1 = p.sbuf(f'{pref}_lnx', [128, 16 * TH], BF16)
    combT = p.sbuf(f'{pref}_comb', [8, TH], F32)
    cbuf = p.sbuf(f'{pref}_cbuf', [128, TH], F32)
    g_sb = p.sbuf(f'{pref}_g', [128, 8 * TH], BF16)
    yacc = p.sbuf(f'{pref}_y', [128, 16 * TH], F32)
    sil = p.sbuf(f'{pref}_sil', [128, 512], F32)
    tmp = p.sbuf(f'{pref}_tmp', [128, 512], F32)
    selb = p.sbuf(f'{pref}_selb', [8, 8 * 128], F32)
    hw32 = p.sbuf(f'{pref}_hw32', [128, 4 * 512], F32)
    hwbf = p.sbuf(f'{pref}_hwbf', [128, 2 * 2 * 2048 + 2 * 1024], BF16)
    slab_i = [0]
    xr = p.sbuf(f'{pref}_xr', [128, 3 * 512], F32)
    x2o = p.sbuf(f'{pref}_x2o', [128, 3 * 512], F32)
    p.op('gpsimd', lambda e: e.dma_start(
        out=selb.rearrange("p (e f) -> p e f", e=8),
        in_=sel_hbm.rearrange("e p f -> p e f")), sem=f'{pref}_SEL', n=16)
    p.wait('tensor', f'{pref}_SEL', 16)
    SMMD, SEV, SAC, SCV, SCB = (f'{pref}_mmd', f'{pref}_ev', f'{pref}_ac',
                                f'{pref}_cv', f'{pref}_cbb')
    Tch = chunks(TH)
    nch = len(Tch)
    ost = {0: [], 1: [], 2: []}
    gi = [0]
    hev = {}
    wev = {}
    hc_vals = []
    for half in range(2):
        col0 = half * TH
        if half == 1:
            for s in (0, 1, 2):
                if ost[s]:
                    p.wait('gpsimd', f'{pref}_O{s}', ost[s][-1])
            p.barrier(('gpsimd', 'vector', 'scalar', 'tensor', 'sync'))
        if stats_in is not None:
            lnve = emit_ln_apply(p, cx, x1_hbm, TH, lnx1, f'{pref}n{half}',
                                 stats_in, TG, col0=col0)
        else:
            lnve = emit_ln(p, cx, x1_hbm, TH, lnx1, f'{pref}n{half}', col0=col0)
        for eng in ('tensor', 'vector', 'scalar'):
            p.wait(eng, f'{pref}n{half}_ve', lnve)
        cb = emit_gate(p, cx, f'{pref}g{half}', lnx1, TH, wg_hbm, combT,
                       l1_norm=False)
        for eng in ('tensor', 'vector'):
            p.wait(eng, f'{pref}g{half}_cb', cb)
        for e in range(9):
            if e < 8:
                for ci, (t0, nt) in enumerate(Tch):
                    vb = p.wait('tensor', SCV, p.val(SCV))
                    p.op('tensor', lambda e2, e=e, t0=t0, nt=nt: e2.matmul(
                        cx.ps[6][:, 0:nt], selb[0:8, e * 128:(e + 1) * 128],
                        combT[0:8, t0:t0 + nt], start=True, stop=True),
                        sem=f'{pref}_mb')
                    p.wait('vector', f'{pref}_mb', p.val(f'{pref}_mb'))
                    vv = p.op('vector', lambda e2, t0=t0, nt=nt: e2.tensor_copy(
                        cbuf[:, t0:t0 + nt], cx.ps[6][:, 0:nt]), sem=SCV)
                    p.wait('vector', SCV, vv)
            p.wait('tensor', f'{pref}_yev', p.val(f'{pref}_yev'))
            for ot in range(8):
                pi = (half * 9 + e) * 8 + ot
                par = pi % 2
                po = par * 2 * 2048
                first_cast = True
                for wi, wh in enumerate((w1_hbm, w3_hbm)):
                    for s in range(4):
                        sl = slab_i[0] % 4
                        if slab_i[0] >= 4:
                            p.wait('sync', f'{pref}_hcast', hc_vals[slab_i[0] - 4])
                        dv = p.op('sync', lambda e2, s=s, sl=sl, e=e, ot=ot, wh=wh:
                                  e2.dma_start(
                                      out=hw32[:, sl * 512:(sl + 1) * 512]
                                      .rearrange("p (k f) -> p k f", k=4),
                                      in_=wh[e, s * 512:(s + 1) * 512,
                                             ot * 128:(ot + 1) * 128]
                                      .rearrange("(k p) f -> p k f", p=128)),
                                  sem=f'{pref}_HS{sl}', n=16)
                        p.wait('scalar', f'{pref}_HS{sl}', dv)
                        if first_cast and pi >= 2:
                            p.wait('scalar', SMMD, p.val(SMMD) - nch)
                        first_cast = False
                        hc_vals.append(p.op('scalar', lambda e2, sl=sl, wi=wi, s=s,
                                            po=po: e2.copy(
                            hwbf[:, po + wi * 2048 + s * 512:
                                 po + wi * 2048 + (s + 1) * 512],
                            hw32[:, sl * 512:(sl + 1) * 512]), sem=f'{pref}_hcast'))
                        slab_i[0] += 1
                p.wait('tensor', f'{pref}_hcast', p.val(f'{pref}_hcast'))
                for ci, (t0, nt) in enumerate(Tch):
                    if (pi - 1, ci) in hev:
                        p.wait('tensor', SEV, hev[(pi - 1, ci)])
                    for k in range(16):
                        p.op('tensor', lambda e2, k=k, po=po, ci=ci, t0=t0, nt=nt:
                             e2.matmul(
                                 cx.ps[ci][:, 0:nt],
                                 hwbf[:, po + k * 128:po + (k + 1) * 128],
                                 lnx1[:, k * TH + t0:k * TH + t0 + nt],
                                 start=(k == 0), stop=(k == 15)))
                    for k in range(16):
                        p.op('tensor', lambda e2, k=k, po=po, ci=ci, t0=t0, nt=nt:
                             e2.matmul(
                                 cx.ps[3 + ci][:, 0:nt],
                                 hwbf[:, po + 2048 + k * 128:po + 2048 + (k + 1) * 128],
                                 lnx1[:, k * TH + t0:k * TH + t0 + nt],
                                 start=(k == 0), stop=(k == 15)),
                             sem=(SMMD if k == 15 else None))
                    mmv = p.val(SMMD)
                    p.wait('scalar', SMMD, mmv)
                    p.wait('scalar', SCV, p.val(SCV))
                    av = p.op('scalar', lambda e2, ci=ci, nt=nt: e2.activation(
                        sil[:, 0:nt], cx.ps[ci][:, 0:nt], AF.Silu), sem=SAC)
                    p.wait('vector', SAC, av)
                    p.wait('vector', SEV, p.val(SEV))
                    vv = p.op('vector', lambda e2, ci=ci, nt=nt: e2.tensor_tensor(
                        tmp[:, 0:nt], cx.ps[3 + ci][:, 0:nt],
                        sil[:, 0:nt], op=ALU.mult), sem=SCV)
                    p.wait('vector', SCV, vv)
                    if e < 8:
                        p.op('vector', lambda e2, ot=ot, t0=t0, nt=nt:
                             e2.tensor_tensor(
                                 g_sb[:, ot * TH + t0:ot * TH + t0 + nt],
                                 tmp[:, 0:nt], cbuf[:, t0:t0 + nt],
                                 op=ALU.mult), sem=SEV)
                    else:
                        p.op('vector', lambda e2, ot=ot, t0=t0, nt=nt:
                             e2.tensor_copy(
                                 g_sb[:, ot * TH + t0:ot * TH + t0 + nt],
                                 tmp[:, 0:nt]), sem=SEV)
                    hev[(pi, ci)] = p.val(SEV)
                    gi[0] += 1
            p.wait('tensor', SEV, p.val(SEV))
            for o2 in range(16):
                pi2 = (half * 9 + e) * 16 + o2
                par = pi2 % 2
                po = par * 1024
                first_cast = True
                for s in range(2):
                    sl = slab_i[0] % 4
                    if slab_i[0] >= 4:
                        p.wait('sync', f'{pref}_hcast', hc_vals[slab_i[0] - 4])
                    dv = p.op('sync', lambda e2, s=s, sl=sl, e=e, o2=o2: e2.dma_start(
                        out=hw32[:, sl * 512:(sl + 1) * 512]
                        .rearrange("p (k f) -> p k f", k=4),
                        in_=w2_hbm[e, s * 512:(s + 1) * 512, o2 * 128:(o2 + 1) * 128]
                        .rearrange("(k p) f -> p k f", p=128)),
                        sem=f'{pref}_HS{sl}', n=16)
                    p.wait('scalar', f'{pref}_HS{sl}', dv)
                    if first_cast and pi2 >= 2:
                        p.wait('scalar', f'{pref}_yev', p.val(f'{pref}_yev') - nch)
                    first_cast = False
                    hc_vals.append(p.op('scalar', lambda e2, sl=sl, s=s, po=po:
                                        e2.copy(
                        hwbf[:, 8192 + po + s * 512:8192 + po + (s + 1) * 512],
                        hw32[:, sl * 512:(sl + 1) * 512]), sem=f'{pref}_hcast'))
                    slab_i[0] += 1
                p.wait('tensor', f'{pref}_hcast', p.val(f'{pref}_hcast'))
                for ci, (t0, nt) in enumerate(Tch):
                    if (pi2 - 1, ci) in wev:
                        p.wait('tensor', f'{pref}_yev', wev[(pi2 - 1, ci)])
                    for k in range(8):
                        p.op('tensor', lambda e2, k=k, po=po, ci=ci, t0=t0, nt=nt:
                             e2.matmul(
                                 cx.ps[ci][:, 0:nt],
                                 hwbf[:, 8192 + po + k * 128:
                                      8192 + po + (k + 1) * 128],
                                 g_sb[:, k * TH + t0:k * TH + t0 + nt],
                                 start=(k == 0), stop=(k == 7)),
                             sem=(f'{pref}_wmm' if k == 7 else None))
                    p.wait('vector', f'{pref}_wmm', p.val(f'{pref}_wmm'))
                    first = (e == 0)
                    if first:
                        p.op('vector', lambda e2, o2=o2, t0=t0, nt=nt, ci=ci:
                             e2.tensor_copy(
                                 yacc[:, o2 * TH + t0:o2 * TH + t0 + nt],
                                 cx.ps[ci][:, 0:nt]), sem=f'{pref}_yev')
                        wev[(pi2, ci)] = p.val(f'{pref}_yev')
                    else:
                        p.op('vector', lambda e2, o2=o2, t0=t0, nt=nt, ci=ci:
                             e2.tensor_tensor(
                                 yacc[:, o2 * TH + t0:o2 * TH + t0 + nt],
                                 cx.ps[ci][:, 0:nt],
                                 yacc[:, o2 * TH + t0:o2 * TH + t0 + nt],
                                 op=ALU.add), sem=f'{pref}_yev')
                    wev[(pi2, ci)] = p.val(f'{pref}_yev')
        yevf = p.val(f'{pref}_yev')
        xb2 = p.sbuf('lnsc_xbf', [128, 2 * 512], BF16)
        qb2 = p.sbuf('lnsc_sqbf', [128, 2 * 512], BF16)
        srw = p.sbuf('lnsc_srows', [1, 2 * 512], F32)
        for ci, (t0, nt) in enumerate(Tch):
            for o in range(16):
                g2 = gi[0]
                sl = g2 % 3
                par2 = g2 % 3
                ps2 = g2 % 2
                if g2 >= 3:
                    p.wait('gpsimd', SEV, p.val(SEV) - 2)
                rv = p.op('gpsimd', lambda e2, o=o, t0=t0, nt=nt, sl=sl, col0=col0:
                          e2.dma_start(
                              out=xr[:, sl * 512:sl * 512 + nt],
                              in_=x1_hbm[o * 128:(o + 1) * 128,
                                         col0 + t0:col0 + t0 + nt]),
                          sem=f'{pref}_R{sl}', n=16)
                p.wait('vector', f'{pref}_R{sl}', rv)
                pi2e8 = (half * 9 + 8) * 16 + o
                p.wait('vector', f'{pref}_yev', wev[(pi2e8, ci)])
                if ost[par2]:
                    p.wait('vector', f'{pref}_O{par2}', ost[par2][-1])
                if stats_out is not None:
                    p.wait('vector', f'{pref}_sa2', p.val(f'{pref}_sa2'))
                av = p.op('vector', lambda e2, o=o, t0=t0, nt=nt, sl=sl, par2=par2:
                     e2.tensor_tensor(
                         x2o[:, par2 * 512:par2 * 512 + nt],
                         yacc[:, o * TH + t0:o * TH + t0 + nt],
                         xr[:, sl * 512:sl * 512 + nt], op=ALU.add), sem=SEV)
                p.wait('gpsimd', SEV, p.val(SEV))
                ov = p.op('gpsimd', lambda e2, o=o, t0=t0, nt=nt, par2=par2,
                          col0=col0: e2.dma_start(
                              out=x2_hbm[o * 128:(o + 1) * 128,
                                         col0 + t0:col0 + t0 + nt],
                              in_=x2o[:, par2 * 512:par2 * 512 + nt]),
                          sem=f'{pref}_O{par2}', n=16)
                ost[par2].append(ov)
                if stats_out is not None:
                    p.wait('vector', SEV, av)
                    p.wait('vector', f'{pref}_sm2', max(0, p.val(f'{pref}_sm2') - 1))
                    cv = p.op('vector', lambda e2, nt=nt, par2=par2, ps2=ps2:
                              e2.tensor_copy(
                        xb2[:, ps2 * 512:ps2 * 512 + nt],
                        x2o[:, par2 * 512:par2 * 512 + nt]), sem=f'{pref}_sc2')
                    p.wait('scalar', f'{pref}_sc2', cv)
                    p.wait('scalar', f'{pref}_sm2', max(0, p.val(f'{pref}_sm2') - 1))
                    aq = p.op('scalar', lambda e2, nt=nt, par2=par2, ps2=ps2:
                              e2.square(
                        qb2[:, ps2 * 512:ps2 * 512 + nt],
                        x2o[:, par2 * 512:par2 * 512 + nt]), sem=f'{pref}_sa2')
                    p.wait('tensor', f'{pref}_sc2', cv)
                    p.wait('tensor', f'{pref}_sa2', aq)
                    if o == 0:
                        p.wait('tensor', f'{pref}_sv2', p.val(f'{pref}_sv2'))
                        p.wait('tensor', f'{pref}_yev', wev[(pi2e8, ci)])
                    p.op('tensor', lambda e2, nt=nt, ps2=ps2, o=o: e2.matmul(
                        cx.ps[3][0:1, 0:nt], cx.ones_col[:],
                        xb2[:, ps2 * 512:ps2 * 512 + nt],
                        start=(o == 0), stop=(o == 15), skip_group_check=True))
                    p.op('tensor', lambda e2, nt=nt, ps2=ps2, o=o: e2.matmul(
                        cx.ps[4][0:1, 0:nt], cx.ones_col[:],
                        qb2[:, ps2 * 512:ps2 * 512 + nt],
                        start=(o == 0), stop=(o == 15), skip_group_check=True),
                        sem=f'{pref}_sm2')
                gi[0] += 1
            if stats_out is not None:
                p.wait('vector', f'{pref}_sm2', p.val(f'{pref}_sm2'))
                if ci > 0:
                    p.wait('vector', f'{pref}_sS', p.val(f'{pref}_sS'))
                p.op('vector', lambda e2, nt=nt: e2.tensor_copy(
                    srw[0:1, 0:nt], cx.ps[3][0:1, 0:nt]))
                sv2 = p.op('vector', lambda e2, nt=nt: e2.tensor_copy(
                    srw[0:1, 512:512 + nt], cx.ps[4][0:1, 0:nt]),
                    sem=f'{pref}_sv2')
                p.wait('gpsimd', f'{pref}_sv2', sv2)
                p.op('gpsimd', lambda e2, col0=col0, t0=t0, nt=nt: e2.dma_start(
                    out=stats_out[0:1, col0 + t0:col0 + t0 + nt],
                    in_=srw[0:1, 0:nt]), sem=f'{pref}_sS', n=16)
                p.op('gpsimd', lambda e2, col0=col0, t0=t0, nt=nt: e2.dma_start(
                    out=stats_out[1:2, col0 + t0:col0 + t0 + nt],
                    in_=srw[0:1, 512:512 + nt]), sem=f'{pref}_sS', n=16)
        if stats_out is not None:
            p.wait('gpsimd', f'{pref}_sS', p.val(f'{pref}_sS'))
            p.wait('tensor', f'{pref}_sv2', p.val(f'{pref}_sv2'))
    return ost[0][-1], ost[1][-1], ost[2][-1]


def emit_outproj_residual(p, cx, pref, attn_res, wT_hbm, x_hbm, xcol0, xout_hbm,
                          T, bias_sb=None, stats_hbm=None):
    Tch = chunks(T)
    nch = len(Tch)
    xr = p.sbuf(f'{pref}_xr', [128, 4 * 512], F32)
    x1o = p.sbuf(f'{pref}_x1o', [128, 4 * 512], F32)
    SEV, SMMD = f'{pref}_ev', f'{pref}_mmd'
    gi = [0]
    olast = {0: [], 1: [], 2: [], 3: []}
    sev_g = []
    st = stats_init(p, pref, T) if stats_hbm is not None else None

    def X_ap(k, t0, nt):
        return attn_res[:, k * T + t0:k * T + t0 + nt]

    def evac(p2, oi, bank, t0, nt, ci):
        g = gi[0]
        sl = g % 4
        par = g % 4
        if g >= 4:
            p.wait('gpsimd', SEV, sev_g[g - 4])
        v = p.op('gpsimd', lambda e, oi=oi, t0=t0, nt=nt, sl=sl: e.dma_start(
            out=xr[:, sl * 512:sl * 512 + nt],
            in_=x_hbm[oi * 128:(oi + 1) * 128, xcol0 + t0:xcol0 + t0 + nt]),
            sem=f'{pref}_R{sl}', n=16)
        p.wait('vector', f'{pref}_R{sl}', v)
        p.wait('vector', SMMD, p.val(SMMD))
        if olast[par]:
            p.wait('vector', f'{pref}_O{par}', olast[par][-1])
        if bias_sb is not None:
            vv = p.op('vector', lambda e, oi=oi, nt=nt, sl=sl, bank=bank: e.tensor_scalar(
                xr[:, sl * 512:sl * 512 + nt], xr[:, sl * 512:sl * 512 + nt],
                bias_sb[:, oi:oi + 1], None, op0=ALU.add), sem=f'{pref}_bs')
            p.wait('vector', f'{pref}_bs', vv)
        fv = p.op('vector', lambda e, nt=nt, sl=sl, par=par, bank=bank: e.tensor_tensor(
            x1o[:, par * 512:par * 512 + nt], bank[:, 0:nt],
            xr[:, sl * 512:sl * 512 + nt], op=ALU.add), sem=SEV)
        sev_g.append(p.val(SEV))
        p.wait('gpsimd', SEV, p.val(SEV))
        ov = p.op('gpsimd', lambda e, oi=oi, t0=t0, nt=nt, par=par: e.dma_start(
            out=xout_hbm[oi * 128:(oi + 1) * 128, t0:t0 + nt],
            in_=x1o[:, par * 512:par * 512 + nt]), sem=f'{pref}_O{par}', n=16)
        olast[par].append(ov)
        if st is not None:
            p.wait('vector', SEV, fv)
            stats_acc(p, cx, pref, st, x1o[:, par * 512:par * 512 + nt],
                      t0, nt, first=(oi == 0))
        gi[0] += 1
        return fv

    wsrcs = [wT_hbm[:, oi * 128:(oi + 1) * 128] for oi in range(16)]
    emit_dense2(p, cx, pref, wsrcs, X_ap, Tch, 16, evac)
    if st is not None:
        stats_store(p, pref, st, stats_hbm)
    for s in range(4):
        if olast[s]:
            p.wait('gpsimd', f'{pref}_O{s}', olast[s][-1])
    return (0, 0)


def emit_gate(p, cx, pref, lnx, T, wg_hbm, combT, l1_norm=False,
              bank_g=5, bank_t0=0, bank_t1=1):
    wg32 = p.sbuf('gtsc_wg32', [128, 128], F32)
    wgbf = p.sbuf('gtsc_wgbf', [128, 128], BF16)
    lg = p.sbuf('gtsc_lg', [8, 512], F32)
    tl = p.sbuf('gtsc_tl', [128, 2 * 8], F32)
    prob = p.sbuf('gtsc_prob', [128, 8], F32)
    t2 = p.sbuf('gtsc_t2', [128, 8], F32)
    cmp2 = p.sbuf('gtsc_cmp2', [128, 8], F32)
    cm = p.sbuf('gtsc_cm', [128, 8], F32)
    r1 = p.sbuf('gtsc_r1', [128, 8], F32)
    kk2 = p.sbuf('gtsc_k2', [128, 2], F32)
    negbig = kk2[:, 0:1]
    c16 = kk2[:, 1:2]
    SMS, SCB, SAC = f'{pref}_ms', f'{pref}_cb', f'{pref}_ac'
    v = p.op('vector', lambda e: e.memset(negbig[:, :], -1e30))
    v = p.op('vector', lambda e: e.memset(c16[:, :], 16.0), sem=SCB)
    p.wait('vector', SCB, v)
    wgv = p.op('sync', lambda e: e.dma_start(
        out=wg32.rearrange("p (k e) -> p k e", k=16),
        in_=wg_hbm.rearrange("(k p) e -> p k e", p=128)), sem=f'{pref}_Wg', n=16)
    p.wait('scalar', f'{pref}_Wg', wgv)
    cv = p.op('scalar', lambda e: e.copy(wgbf[:], wg32[:]), sem=f'{pref}_cast')
    p.wait('tensor', f'{pref}_cast', cv)

    def dve(f):
        vv = p.op('vector', f, sem=SCB)
        p.wait('vector', SCB, vv)

    for ci, (t0, nt) in enumerate(chunks(T)):
        p.wait('tensor', SCB, p.val(SCB))
        for k in range(16):
            p.op('tensor', lambda e, k=k, t0=t0, nt=nt: e.matmul(
                cx.ps[bank_g][0:8, 0:nt], wgbf[:, k * 8:(k + 1) * 8],
                lnx[:, k * T + t0:k * T + t0 + nt], start=(k == 0), stop=(k == 15)),
                sem=(SMS if k == 15 else None))
        p.wait('vector', SMS, p.val(SMS))
        dve(lambda e, nt=nt: e.tensor_copy(lg[:, 0:nt], cx.ps[bank_g][0:8, 0:nt]))
        for si in range((nt + 127) // 128):
            ns = min(128, nt - si * 128)
            par = si % 2
            bt = bank_t0 if par == 0 else bank_t1
            p.wait('tensor', SCB, p.val(SCB))
            p.op('tensor', lambda e, si=si, ns=ns, bt=bt: e.matmul(
                cx.ps[bt][0:ns, 0:8], lg[0:8, si * 128:si * 128 + ns],
                cx.identf[0:8, 0:8], start=True, stop=True, is_transpose=True),
                sem=SMS)
            p.wait('vector', SMS, p.val(SMS))
            dve(lambda e, ns=ns, par=par, bt=bt: e.tensor_copy(
                tl[0:ns, par * 8:par * 8 + 8], cx.ps[bt][0:ns, 0:8]))
            dve(lambda e, ns=ns, par=par: e.reduce_max(
                r1[0:ns, 1:2], tl[0:ns, par * 8:par * 8 + 8], axis=AX.X, negate=True))
            p.wait('scalar', SCB, p.val(SCB))
            av = p.op('scalar', lambda e, ns=ns, par=par: e.activation(
                t2[0:ns, 0:8], tl[0:ns, par * 8:par * 8 + 8], AF.Exp,
                bias=r1[0:ns, 1:2]), sem=SAC)
            p.wait('vector', SAC, av)
            dve(lambda e, ns=ns: e.reduce_sum(r1[0:ns, 2:3], t2[0:ns, 0:8], axis=AX.X))
            dve(lambda e, ns=ns: e.reciprocal(r1[0:ns, 3:4], r1[0:ns, 2:3]))
            dve(lambda e, ns=ns: e.tensor_scalar(
                prob[0:ns, :], t2[0:ns, 0:8], r1[0:ns, 3:4], None, op0=ALU.mult))
            dve(lambda e, ns=ns: e.reduce_max(r1[0:ns, 4:5], prob[0:ns, :], axis=AX.X))
            dve(lambda e, ns=ns: e.tensor_scalar(
                cmp2[0:ns, :], prob[0:ns, :], r1[0:ns, 4:5], None, op0=ALU.is_ge))
            dve(lambda e, ns=ns: e.scalar_tensor_tensor(
                t2[0:ns, 0:8], cmp2[0:ns, :], negbig[0:ns, :], prob[0:ns, :],
                op0=ALU.mult, op1=ALU.add))
            dve(lambda e, ns=ns: e.reduce_max(r1[0:ns, 5:6], t2[0:ns, 0:8], axis=AX.X))
            dve(lambda e, ns=ns: e.tensor_scalar(
                cmp2[0:ns, :], prob[0:ns, :], r1[0:ns, 5:6], None, op0=ALU.is_ge))
            dve(lambda e, ns=ns: e.tensor_tensor(
                cm[0:ns, :], prob[0:ns, :], cmp2[0:ns, :], op=ALU.mult))
            if l1_norm:
                dve(lambda e, ns=ns: e.tensor_tensor(
                    r1[0:ns, 6:7], r1[0:ns, 4:5], r1[0:ns, 5:6], op=ALU.add))
                dve(lambda e, ns=ns: e.reciprocal(r1[0:ns, 7:8], r1[0:ns, 6:7]))
                dve(lambda e, ns=ns: e.tensor_scalar(
                    cm[0:ns, :], cm[0:ns, :], r1[0:ns, 7:8], None, op0=ALU.mult))
                dve(lambda e, ns=ns: e.tensor_scalar(
                    cm[0:ns, :], cm[0:ns, :], c16[0:ns, :], None, op0=ALU.mult))
            p.wait('tensor', SCB, p.val(SCB))
            p.op('tensor', lambda e, ns=ns, bt=bt: e.matmul(
                cx.ps[bt][0:8, 8:8 + ns], cm[0:ns, 0:8],
                cx.identf[0:ns, 0:ns], start=True, stop=True, is_transpose=True),
                sem=SMS)
            p.wait('vector', SMS, p.val(SMS))
            dve(lambda e, si=si, t0=t0, ns=ns, bt=bt: e.tensor_copy(
                combT[0:8, t0 + si * 128:t0 + si * 128 + ns],
                cx.ps[bt][0:8, 8:8 + ns]))
    return p.val(SCB)


def emit_l1qkv(p, cx, pref, x2_hbm, wqkv_hbm, bqkv_hbm, ropeC_hbm, ropeS_hbm,
               perm_hbm, qkv_hbm, stats_in=None):
    lnx2 = p.sbuf(f'{pref}_lnx', [128, 16 * TG], BF16)
    bia = p.sbuf(f'{pref}_bia', [128, 48], F32)
    rC = p.sbuf(f'{pref}_rC', [128, TG], F32)
    rS = p.sbuf(f'{pref}_rS', [128, TG], F32)
    prm = p.sbuf(f'{pref}_prm', [128, 128], BF16)
    qpre = p.sbuf(f'{pref}_qpre', [128, 2 * 512], BF16)
    qscl1 = p.sbuf(f'{pref}_qscl1', [128, 1], F32)
    vq = p.op('vector', lambda e: e.memset(qscl1[:], 0.125), sem=f'{pref}_ev')
    p.wait('vector', f'{pref}_ev', vq)
    rt1 = p.sbuf(f'{pref}_rt1', [128, 512], F32)
    rt2 = p.sbuf(f'{pref}_rt2', [128, 512], F32)
    outb = p.sbuf(f'{pref}_outb', [128, 2 * 512], BF16)
    for nm, t, hb in (('bia', bia, None), ('rC', rC, ropeC_hbm),
                      ('rS', rS, ropeS_hbm), ('prm', prm, perm_hbm)):
        pass
    p.op('gpsimd', lambda e: e.dma_start(out=bia[:], in_=bqkv_hbm[:]),
         sem=f'{pref}_B', n=16)
    p.op('gpsimd', lambda e: e.dma_start(out=rC[:], in_=ropeC_hbm[:]),
         sem=f'{pref}_B', n=16)
    p.op('gpsimd', lambda e: e.dma_start(out=rS[:], in_=ropeS_hbm[:]),
         sem=f'{pref}_B', n=16)
    p.op('gpsimd', lambda e: e.dma_start(out=prm[:], in_=perm_hbm[:]),
         sem=f'{pref}_B', n=16)
    bv = p.val(f'{pref}_B')
    if stats_in is not None:
        lnve = emit_ln_apply(p, cx, x2_hbm, TG, lnx2, f'{pref}n', stats_in, TG)
    else:
        lnve = emit_ln(p, cx, x2_hbm, TG, lnx2, f'{pref}n')
    for eng in ('tensor', 'vector', 'scalar'):
        p.wait(eng, f'{pref}n_ve', lnve)
    p.wait('vector', f'{pref}_B', bv)
    p.wait('tensor', f'{pref}_B', bv)
    Tch = chunks(TG)
    nch = len(Tch)
    SEV, SMMD = f'{pref}_ev', f'{pref}_mmd'
    ost = {0: [], 1: []}
    gq = [0]

    def X_ap(k, t0, nt):
        return lnx2[:, k * TG + t0:k * TG + t0 + nt]

    def evac(p2, oi, bank, t0, nt, ci):
        g = gq[0]
        par = g % 2
        rb = 5 + (g % 2)
        p.wait('vector', SMMD, p.val(SMMD))
        if ost[par]:
            p.wait('vector', f'{pref}_O{par}', ost[par][-1])
        if oi < 16:
            vv = p.op('vector', lambda e, oi=oi, nt=nt, par=par, bank=bank:
                      e.tensor_scalar(qpre[:, par * 512:par * 512 + nt],
                                      bank[:, 0:nt], bia[:, oi:oi + 1],
                                      qscl1[:, 0:1], op0=ALU.add, op1=ALU.mult),
                      sem=SEV)
        else:
            vv = p.op('vector', lambda e, oi=oi, nt=nt, par=par, bank=bank:
                      e.tensor_scalar(qpre[:, par * 512:par * 512 + nt],
                                      bank[:, 0:nt], bia[:, oi:oi + 1], None,
                                      op0=ALU.add), sem=SEV)
        p.wait('vector', SEV, vv)
        freev = vv
        if oi < 32:
            p.wait('tensor', SEV, vv)
            mv = p.op('tensor', lambda e, nt=nt, par=par, rb=rb: e.matmul(
                cx.ps[rb][:, 0:nt], prm[:], qpre[:, par * 512:par * 512 + nt],
                start=True, stop=True), sem=f'{pref}_pm')
            vv = p.op('vector', lambda e, t0=t0, nt=nt, par=par: e.tensor_tensor(
                rt1[:, 0:nt], qpre[:, par * 512:par * 512 + nt], rC[:, t0:t0 + nt],
                op=ALU.mult), sem=SEV)
            p.wait('vector', f'{pref}_pm', mv)
            p.wait('vector', SEV, vv)
            vv = p.op('vector', lambda e, t0=t0, nt=nt, rb=rb: e.tensor_tensor(
                rt2[:, 0:nt], cx.ps[rb][:, 0:nt], rS[:, t0:t0 + nt],
                op=ALU.mult), sem=SEV)
            p.wait('vector', SEV, vv)
            vv = p.op('vector', lambda e, nt=nt, par=par: e.tensor_tensor(
                outb[:, par * 512:par * 512 + nt], rt1[:, 0:nt], rt2[:, 0:nt],
                op=ALU.add), sem=SEV)
            p.wait('gpsimd', SEV, vv)
            src = outb
        else:
            p.wait('gpsimd', SEV, vv)
            src = qpre
        ov = p.op('gpsimd', lambda e, oi=oi, t0=t0, nt=nt, par=par, src=src:
                  e.dma_start(out=qkv_hbm[oi * 128:(oi + 1) * 128, t0:t0 + nt],
                              in_=src[:, par * 512:par * 512 + nt]),
                  sem=f'{pref}_O{par}', n=16)
        ost[par].append(ov)
        gq[0] += 1
        return freev

    wsrcs = [wqkv_hbm[:, oi * 128:(oi + 1) * 128] for oi in range(48)]
    emit_dense2(p, cx, pref, wsrcs, X_ap, Tch, 16, evac)
    return ost[0][-1], ost[1][-1]


def emit_l1attn(p, cx, pref, qkv_hbm, masks_hbm, sinksT_hbm, sinksaug_hbm,
                attn1_hbm):
    msk = p.sbuf(f'{pref}_msk', [128, 7 * 512], BF16)
    q2 = p.sbuf(f'{pref}_q2', [128, 2 * TG], BF16)
    k2 = p.sbuf(f'{pref}_k2', [128, 2 * TG], BF16)
    v2 = p.sbuf(f'{pref}_v2', [128, 2 * TG], BF16)
    vtm2 = p.sbuf(f'{pref}_vtm2', [128, 34 * 66], BF16)
    skT32 = p.sbuf(f'{pref}_skT32', [128, 512], F32)
    skT = p.sbuf(f'{pref}_skT', [128, 512], BF16)
    ska32 = p.sbuf(f'{pref}_ska32', [16, 32 * 65], F32)
    skab = p.sbuf(f'{pref}_skab', [48, 16 * 65], BF16)
    psnk = p.sbuf(f'{pref}_psnk', [48, 2 * 512], BF16)
    sexp = p.sbuf(f'{pref}_sexp', [128, 4 * 512], F32)
    pT = p.sbuf(f'{pref}_pT', [128, 4 * 512], BF16)
    rr2 = p.sbuf(f'{pref}_rr2', [1, 2 * 512], F32)
    lnt5 = p.sbuf(f'{pref}_lnt5', [128, 512], F32)
    rb = p.sbuf(f'{pref}_rb', [128, 512], F32)
    att1 = p.sbuf(f'{pref}_att1', [128, 2 * 2048], BF16)
    SMS, SCM, SAE, SMP, SCR, SMB = (f'{pref}_ms', f'{pref}_cm', f'{pref}_ae',
                                    f'{pref}_mp', f'{pref}_cr', f'{pref}_mb')
    p.op('gpsimd', lambda e: e.dma_start(
        out=msk.rearrange("p (s f) -> p s f", s=7),
        in_=masks_hbm.rearrange("s p f -> p s f")), sem=f'{pref}_M', n=16)
    p.op('gpsimd', lambda e: e.dma_start(
        out=skT32[0:64, :].rearrange("p (h s) -> p h s", h=32),
        in_=sinksT_hbm.rearrange("h p s -> p h s")), sem=f'{pref}_M', n=16)
    p.op('gpsimd', lambda e: e.dma_start(
        out=skT32[64:128, :].rearrange("p (h s) -> p h s", h=32),
        in_=sinksT_hbm.rearrange("h p s -> p h s")), sem=f'{pref}_M', n=16)
    p.op('gpsimd', lambda e: e.dma_start(
        out=ska32.rearrange("p (h f) -> p h f", h=32),
        in_=sinksaug_hbm.rearrange("h p f -> p h f")), sem=f'{pref}_M', n=16)
    mv = p.val(f'{pref}_M')
    p.wait('vector', f'{pref}_M', mv)
    v = p.op('vector', lambda e: e.memset(vtm2[:], 1.0), sem=SCM)
    v = p.op('vector', lambda e: e.tensor_copy(skT[:], skT32[:]), sem=SCM)
    for j in range(16):
        for b in range(2):
            v = p.op('vector', lambda e, j=j, b=b: e.tensor_copy(
                skab[b * 32:b * 32 + 16, j * 65:(j + 1) * 65],
                ska32[0:16, (2 * j + b) * 65:(2 * j + b + 1) * 65]), sem=SCM)
    p.wait('vector', SCM, v)
    p.wait('tensor', SCM, v)
    st_pairs = {0: [], 1: []}
    pe_done = []
    kki = 0
    sms_s = {}
    scm_done = []
    sae_done = []
    smp_done = []
    sae_ln = None
    smp_pv = None
    evq_last = None
    for hp in range(16):
        par = hp % 2
        if hp >= 2:
            p.wait('gpsimd', SMP, pe_done[hp - 2])
        lv = 0
        for nm, dst, r0 in (('q', q2, 0), ('k', k2, 2048), ('v', v2, 4096)):
            lv = p.op('gpsimd', lambda e, dst=dst, r0=r0, par=par, hp=hp:
                      e.dma_start(out=dst[:, par * TG:(par + 1) * TG],
                                  in_=qkv_hbm[r0 + hp * 128:r0 + (hp + 1) * 128, :]),
                      sem=f'{pref}_L{par}', n=16)
        p.wait('tensor', f'{pref}_L{par}', lv)
        scm0 = p.val(SCM)
        for kt in range(17):
            bk = kt % 2
            if kt >= 2:
                p.wait('tensor', SCM, scm0 + 2 * (kt - 2) + 2)
            p.op('tensor', lambda e, kt=kt, bk=bk, par=par: e.matmul(
                cx.ps[bk].bitcast(BF16)[:, 0:128],
                v2[:, par * TG + kt * 128:par * TG + (kt + 1) * 128],
                cx.ident[:], start=True, stop=True, is_transpose=True), sem=SMS)
            p.wait('vector', SMS, p.val(SMS))
            for hi in range(2):
                p.op('vector', lambda e, kt=kt, hi=hi, bk=bk: e.tensor_copy(
                    vtm2[:, (kt * 2 + hi) * 66:(kt * 2 + hi) * 66 + 64].bitcast(F32),
                    cx.ps[bk].bitcast(BF16)[:, hi * 64:(hi + 1) * 64].bitcast(F32)),
                    sem=SCM)
        vtm_v = p.val(SCM)
        p.wait('tensor', SCM, vtm_v)
        h0 = hp * 2
        for qc in range(4):
            qq = 64 + 512 * qc
            nq = 512
            qbase = kki

            def emit_spair(tstep, kk, par=par, qq=qq, nq=nq):
                kp0 = (tstep % 2) * 2
                kb = 512 * qc + kk * 128
                if tstep >= 2:
                    p.wait('tensor', SCM, scm_done[tstep - 2])
                p.op('tensor', lambda e, kb=kb, qq=qq, nq=nq, kp0=kp0, par=par:
                     e.matmul(cx.ps[kp0][:, 0:nq],
                              k2[0:64, par * TG + kb:par * TG + kb + 128],
                              q2[0:64, par * TG + qq:par * TG + qq + nq],
                              start=True, stop=True))
                v = p.op('tensor', lambda e, kb=kb, qq=qq, nq=nq, kp0=kp0,
                         par=par:
                         e.matmul(cx.ps[kp0 + 1][:, 0:nq],
                                  k2[64:128, par * TG + kb:par * TG + kb + 128],
                                  q2[64:128, par * TG + qq:par * TG + qq + nq],
                                  start=True, stop=True), sem=SMS)
                sms_s[tstep] = v

            emit_spair(qbase, 0)
            emit_spair(qbase + 1, 1)
            if sae_ln is not None:
                p.wait('tensor', SAE, sae_ln)
            p.op('tensor', lambda e, h0=h0, qq=qq, nq=nq, par=par: e.matmul(
                cx.ps[6][0:16, 0:nq],
                skT[0:64, h0 * 16:(h0 + 1) * 16],
                q2[0:64, par * TG + qq:par * TG + qq + nq],
                start=True, stop=True))
            p.op('tensor', lambda e, h0=h0, qq=qq, nq=nq, par=par: e.matmul(
                cx.ps[6][32:48, 0:nq],
                skT[64:128, (h0 + 1) * 16:(h0 + 2) * 16],
                q2[64:128, par * TG + qq:par * TG + qq + nq],
                start=True, stop=True), sem=SMS)
            p.wait('scalar', SMS, p.val(SMS))
            if smp_pv is not None:
                p.wait('scalar', SMP, smp_pv)
            av = p.op('scalar', lambda e, nq=nq, par=par: e.activation(
                psnk[0:48, par * 512:par * 512 + nq], cx.ps[6][0:48, 0:nq],
                AF.Exp, bias=cx.expbias[0:48, 0:1]), sem=SAE)
            p.wait('tensor', SAE, av)
            if evq_last is not None:
                p.wait('tensor', f'{pref}_evq', evq_last)
            p.op('tensor', lambda e, hp=hp, nq=nq, par=par: e.matmul(
                cx.ps[4][0:65, 0:nq], skab[0:16, hp * 65:(hp + 1) * 65],
                psnk[0:16, par * 512:par * 512 + nq], start=True, stop=False))
            p.op('tensor', lambda e, hp=hp, nq=nq, par=par: e.matmul(
                cx.ps[5][0:65, 0:nq], skab[32:48, hp * 65:(hp + 1) * 65],
                psnk[32:48, par * 512:par * 512 + nq], start=True, stop=False),
                sem=SMP)
            smp_pv = p.val(SMP)
            for kk in range(5):
                tstep = qbase + kk
                kp0 = (tstep % 2) * 2
                slot = l1_mask_slot(qc, kk)
                p.wait('vector', SMS, sms_s[tstep])
                if tstep >= 2:
                    p.wait('vector', SAE, sae_done[tstep - 2])
                p.op('vector', lambda e, kp0=kp0, slot=slot, nq=nq: e.tensor_tensor(
                    sexp[:, kp0 * 512:kp0 * 512 + nq], cx.ps[kp0][:, 0:nq],
                    msk[:, slot * 512:slot * 512 + nq], op=ALU.add))
                p.op('vector', lambda e, kp0=kp0, slot=slot, nq=nq: e.tensor_tensor(
                    sexp[:, (kp0 + 1) * 512:(kp0 + 1) * 512 + nq],
                    cx.ps[kp0 + 1][:, 0:nq],
                    msk[:, slot * 512:slot * 512 + nq], op=ALU.add), sem=SCM)
                scm_done.append(p.val(SCM))
                p.wait('scalar', SCM, scm_done[-1])
                if tstep >= 2:
                    p.wait('scalar', SMP, smp_done[tstep - 2])
                p.op('scalar', lambda e, kp0=kp0, nq=nq: e.activation(
                    pT[:, kp0 * 512:kp0 * 512 + 2 * nq],
                    sexp[:, kp0 * 512:kp0 * 512 + 2 * nq], AF.Exp), sem=SAE)
                sae_done.append(p.val(SAE))
                if kk < 3:
                    emit_spair(tstep + 2, kk + 2)
                p.wait('tensor', SAE, sae_done[tstep])
                p.op('tensor', lambda e, kt4=4 * qc + kk, kp0=kp0, nq=nq, kk=kk:
                     e.matmul(cx.ps[4][0:65, 0:nq],
                              vtm2[:, (kt4 * 2) * 66:(kt4 * 2) * 66 + 65],
                              pT[:, kp0 * 512:kp0 * 512 + nq],
                              start=False, stop=(kk == 4)))
                p.op('tensor', lambda e, kt4=4 * qc + kk, kp0=kp0, nq=nq, kk=kk:
                     e.matmul(cx.ps[5][0:65, 0:nq],
                              vtm2[:, (kt4 * 2 + 1) * 66:(kt4 * 2 + 1) * 66 + 65],
                              pT[:, (kp0 + 1) * 512:(kp0 + 1) * 512 + nq],
                              start=False, stop=(kk == 4)), sem=SMP)
                smp_done.append(p.val(SMP))
            kki = qbase + 5
            p.wait('vector', SMP, p.val(SMP))
            p.op('vector', lambda e, nq=nq: e.tensor_copy(
                rr2[0:1, 0:nq], cx.ps[4][64:65, 0:nq]))
            vv = p.op('vector', lambda e, nq=nq: e.tensor_copy(
                rr2[0:1, 512:512 + nq], cx.ps[5][64:65, 0:nq]), sem=SCR)
            p.wait('tensor', SCR, vv)
            p.op('tensor', lambda e, nq=nq: e.matmul(
                cx.ps[6][0:64, 0:nq], cx.ones_row[0:1, 0:64], rr2[0:1, 0:nq],
                start=True, stop=True))
            p.op('tensor', lambda e, nq=nq: e.matmul(
                cx.ps[6][64:128, 0:nq], cx.ones_row[0:1, 0:64],
                rr2[0:1, 512:512 + nq], start=True, stop=True), sem=SMB)
            p.wait('scalar', SMB, p.val(SMB))
            av = p.op('scalar', lambda e, nq=nq: e.activation(
                lnt5[:, 0:nq], cx.ps[6][:, 0:nq], AF.Ln), sem=SAE)
            sae_ln = av
            p.wait('scalar', SAE, av)
            av = p.op('scalar', lambda e, nq=nq: e.activation(
                rb[:, 0:nq], lnt5[:, 0:nq], AF.Exp,
                scale=cx.neg1[:, 0:1]), sem=SAE)
            p.wait('vector', SAE, av)
            if hp >= 2 and st_pairs[par]:
                p.wait('vector', f'{pref}_S{par}', st_pairs[par][-1])
            p.op('vector', lambda e, qq=qq, nq=nq, par=par: e.tensor_tensor(
                att1[0:64, par * 2048 + qq - 64:par * 2048 + qq - 64 + nq],
                cx.ps[4][0:64, 0:nq], rb[0:64, 0:nq], op=ALU.mult))
            evv = p.op('vector', lambda e, qq=qq, nq=nq, par=par: e.tensor_tensor(
                att1[64:128, par * 2048 + qq - 64:par * 2048 + qq - 64 + nq],
                cx.ps[5][0:64, 0:nq], rb[64:128, 0:nq], op=ALU.mult),
                sem=f'{pref}_evq')
            evq_last = evv
        pe_done.append(p.val(SMP))
        p.wait('gpsimd', f'{pref}_evq', p.val(f'{pref}_evq'))
        sv = p.op('gpsimd', lambda e, hp=hp, par=par: e.dma_start(
            out=attn1_hbm[hp * 128:(hp + 1) * 128, :],
            in_=att1[:, par * 2048:(par + 1) * 2048]), sem=f'{pref}_S{par}', n=16)
        st_pairs[par].append(sv)
    return st_pairs[0][-1], st_pairs[1][-1]


def emit_l1ffn(p, cx, pref, x3_hbm, wd_hbm, wg_hbm, Apack_hbm, Bpack_hbm,
               sel16_hbm, out_hbm, stats_in=None):
    TO = 2048
    lnx3 = p.sbuf(f'{pref}_lnx', [128, 16 * TO], BF16)
    combT = p.sbuf(f'{pref}_comb', [8, TO], F32)
    ts = p.sbuf(f'{pref}_ts', [128, TO], BF16)
    cwe = p.sbuf(f'{pref}_cwe', [128, 512], F32)
    aw32 = p.sbuf(f'{pref}_aw32', [128, 2048], F32)
    awbf = p.sbuf(f'{pref}_awbf', [128, 2048], BF16)
    bp32 = p.sbuf(f'{pref}_bp32', [128, 2048], F32)
    bpbf = p.sbuf(f'{pref}_bpbf', [128, 2048], BF16)
    s16 = p.sbuf(f'{pref}_s16', [8, 128], F32)
    SEV, SMMD = f'{pref}_ev', f'{pref}_mmd'
    p.op('sync', lambda e: e.dma_start(
        out=aw32.rearrange("p (k f) -> p k f", k=16),
        in_=Apack_hbm.rearrange("(k p) f -> p k f", p=128)), sem=f'{pref}_A', n=16)
    p.op('sync', lambda e: e.dma_start(out=bp32[:], in_=Bpack_hbm[:]),
         sem=f'{pref}_A', n=16)
    p.op('sync', lambda e: e.dma_start(out=s16[:], in_=sel16_hbm[:]),
         sem=f'{pref}_A', n=16)
    av = p.val(f'{pref}_A')
    p.wait('scalar', f'{pref}_A', av)
    cv = p.op('scalar', lambda e: e.copy(awbf[:], aw32[:]), sem=f'{pref}_acast')
    cv = p.op('scalar', lambda e: e.copy(bpbf[:], bp32[:]), sem=f'{pref}_acast')
    if stats_in is not None:
        lnve = emit_ln_apply(p, cx, x3_hbm, TO, lnx3, f'{pref}n', stats_in, TO)
    else:
        lnve = emit_ln(p, cx, x3_hbm, TO, lnx3, f'{pref}n')
    for eng in ('tensor', 'vector', 'scalar'):
        p.wait(eng, f'{pref}n_ve', lnve)
    cb = emit_gate(p, cx, f'{pref}g', lnx3, TO, wg_hbm, combT, l1_norm=True)
    for eng in ('tensor', 'vector'):
        p.wait(eng, f'{pref}g_cb', cb)
    p.wait('tensor', f'{pref}_acast', cv)
    Tch = chunks(TO)
    nch = len(Tch)
    for ci, (t0, nt) in enumerate(Tch):
        bk = ci % 2
        p.wait('tensor', f'{pref}_tsv', p.val(f'{pref}_tsv'))
        for k in range(16):
            p.op('tensor', lambda e, k=k, t0=t0, nt=nt, bk=bk: e.matmul(
                cx.ps[bk][:, 0:nt], awbf[:, k * 128:(k + 1) * 128],
                lnx3[:, k * TO + t0:k * TO + t0 + nt],
                start=(k == 0), stop=(k == 15)),
                sem=(f'{pref}_tmm' if k == 15 else None))
        p.op('tensor', lambda e, t0=t0, nt=nt: e.matmul(
            cx.ps[5][:, 0:nt], s16[0:8, :], combT[0:8, t0:t0 + nt],
            start=True, stop=True), sem=f'{pref}_tmm')
        p.wait('vector', f'{pref}_tmm', p.val(f'{pref}_tmm'))
        vv = p.op('vector', lambda e, nt=nt: e.tensor_copy(
            cwe[:, 0:nt], cx.ps[5][:, 0:nt]), sem=f'{pref}_tsv')
        p.wait('vector', f'{pref}_tsv', vv)
        p.op('vector', lambda e, t0=t0, nt=nt, bk=bk: e.tensor_tensor(
            ts[:, t0:t0 + nt], cx.ps[bk][:, 0:nt], cwe[:, 0:nt], op=ALU.mult),
            sem=f'{pref}_tsv')
    tsv = p.val(f'{pref}_tsv')
    p.wait('tensor', f'{pref}_tsv', tsv)

    xr = p.sbuf(f'{pref}_xr', [128, 4 * 512], F32)
    xo = p.sbuf(f'{pref}_xo', [128, 4 * 512], F32)
    ost = {0: [], 1: [], 2: [], 3: []}
    gi = [0]
    sev_g7 = []

    def X_ap(k, t0, nt):
        return lnx3[:, k * TO + t0:k * TO + t0 + nt]

    def extra(p2, oi, ci, bank, t0, nt, semname):
        p.op('tensor', lambda e, oi=oi, t0=t0, nt=nt, bank=bank: e.matmul(
            bank[:, 0:nt], bpbf[:, oi * 128:(oi + 1) * 128], ts[:, t0:t0 + nt],
            start=False, stop=True), sem=semname)

    def evac(p2, oi, bank, t0, nt, ci):
        g = gi[0]
        sl = g % 4
        par = g % 4
        if g >= 4:
            p.wait('gpsimd', SEV, sev_g7[g - 4])
        rv = p.op('gpsimd', lambda e, oi=oi, t0=t0, nt=nt, sl=sl: e.dma_start(
            out=xr[:, sl * 512:sl * 512 + nt],
            in_=x3_hbm[oi * 128:(oi + 1) * 128, t0:t0 + nt]), sem=f'{pref}_R{sl}', n=16)
        p.wait('vector', f'{pref}_R{sl}', rv)
        p.wait('vector', SMMD, p.val(SMMD))
        if ost[par]:
            p.wait('vector', f'{pref}_O{par}', ost[par][-1])
        fv = p.op('vector', lambda e, nt=nt, sl=sl, par=par, bank=bank: e.tensor_tensor(
            xo[:, par * 512:par * 512 + nt], bank[:, 0:nt],
            xr[:, sl * 512:sl * 512 + nt], op=ALU.add), sem=SEV)
        sev_g7.append(p.val(SEV))
        p.wait('gpsimd', SEV, p.val(SEV))
        ov = p.op('gpsimd', lambda e, oi=oi, t0=t0, nt=nt, par=par: e.dma_start(
            out=out_hbm[oi * 128:(oi + 1) * 128, t0:t0 + nt],
            in_=xo[:, par * 512:par * 512 + nt]), sem=f'{pref}_O{par}', n=16)
        ost[par].append(ov)
        gi[0] += 1
        return fv

    wsrcs = [wd_hbm[:, oi * 128:(oi + 1) * 128] for oi in range(16)]
    emit_dense2(p, cx, pref, wsrcs, X_ap, Tch, 16, evac, extra_mm=extra)
    for s in range(4):
        if ost[s]:
            p.wait('gpsimd', f'{pref}_O{s}', ost[s][-1])
    return (0, 0)


def stats_init(p, pref, T):
    st = {}
    st['acc'] = p.sbuf(f'{pref}_stacc', [1, 2 * 2176], F32)
    st['xb'] = p.sbuf('lnsc_xbf', [128, 2 * 512], BF16)
    st['qb'] = p.sbuf('lnsc_sqbf', [128, 2 * 512], BF16)
    st['T'] = T
    st['n'] = 0
    return st


def stats_acc(p, cx, pref, st, tile_ap, t0, nt, first):
    SC, SA, SM, SP = (f'{pref}_stc', f'{pref}_sta', f'{pref}_stm', f'{pref}_stp')
    v = p.op('vector', lambda e: e.tensor_copy(st['xb'][:, 0:nt], tile_ap), sem=SC)
    p.wait('scalar', SC, v)
    a = p.op('scalar', lambda e: e.square(st['qb'][:, 0:nt], tile_ap), sem=SA)
    p.wait('tensor', SC, p.val(SC))
    p.wait('tensor', SA, p.val(SA))
    p.wait('tensor', SP, p.val(SP))
    p.op('tensor', lambda e: e.matmul(cx.ps[5][0:1, 0:nt], cx.ones_col[:],
                                      st['xb'][:, 0:nt], start=True, stop=True))
    m = p.op('tensor', lambda e: e.matmul(cx.ps[6][0:1, 0:nt], cx.ones_col[:],
                                          st['qb'][:, 0:nt], start=True, stop=True),
             sem=SM)
    p.wait('vector', SM, m)
    T = st['T']
    if first:
        p.op('vector', lambda e: e.tensor_copy(
            st['acc'][0:1, t0:t0 + nt], cx.ps[5][0:1, 0:nt]))
        p.op('vector', lambda e: e.tensor_copy(
            st['acc'][0:1, T + t0:T + t0 + nt], cx.ps[6][0:1, 0:nt]), sem=SP)
    else:
        p.op('vector', lambda e: e.tensor_tensor(
            st['acc'][0:1, t0:t0 + nt], cx.ps[5][0:1, 0:nt],
            st['acc'][0:1, t0:t0 + nt], op=ALU.add))
        p.op('vector', lambda e: e.tensor_tensor(
            st['acc'][0:1, T + t0:T + t0 + nt], cx.ps[6][0:1, 0:nt],
            st['acc'][0:1, T + t0:T + t0 + nt], op=ALU.add), sem=SP)


def stats_store(p, pref, st, stats_hbm):
    T = st['T']
    p.wait('gpsimd', f'{pref}_stp', p.val(f'{pref}_stp'))
    p.op('gpsimd', lambda e: e.dma_start(out=stats_hbm[0:1, :],
                                         in_=st['acc'][0:1, 0:T]),
         sem=f'{pref}_stS', n=16)
    v = p.op('gpsimd', lambda e: e.dma_start(out=stats_hbm[1:2, :],
                                             in_=st['acc'][0:1, T:2 * T]),
         sem=f'{pref}_stS', n=16)
    p.wait('gpsimd', f'{pref}_stS', v)
    return v


def emit_ln_apply(p, cx, src_hbm, T, lnx, pref, stats_hbm, Tst, col0=0,
                  bank0=0):
    nk = 16
    ps_mb = cx.ps[bank0 + 2]
    ps_rb = cx.ps[bank0 + 3]
    xin = p.sbuf("lnsc_xin", [128, 5 * 512], F32)
    rows = p.sbuf("lnsc_rows", [1, 2 * 512], F32)
    srows = p.sbuf("lnsc_srows", [1, 2 * 512], F32)
    mrow = p.sbuf("lnsc_mrow", [128, 512], F32)
    rrow = p.sbuf("lnsc_rrow", [128, 512], F32)
    inv = p.const(1.0 / 2048.0)
    eps = p.const(EPS)
    neghalf = p.const(-0.5)
    SA, SC, SM, SV, SP, SL = (f'{pref}_ac', f'{pref}_cv', f'{pref}_mm',
                              f'{pref}_ve', f'{pref}_pl', f'{pref}_sl')
    ve_done = []
    pl_done = 0
    for ci, (t0, nt) in enumerate(chunks(T)):
        c0 = col0 + t0
        p.wait('gpsimd', SC, p.val(SC))
        p.op('gpsimd', lambda e, c0=c0, nt=nt: e.dma_start(
            out=srows[0:1, 0:nt], in_=stats_hbm[0:1, c0:c0 + nt]), sem=SL, n=16)
        lv = p.op('gpsimd', lambda e, c0=c0, nt=nt: e.dma_start(
            out=srows[0:1, 512:512 + nt], in_=stats_hbm[1:2, c0:c0 + nt]),
            sem=SL, n=16)
        p.wait('vector', SL, lv)
        v = p.op('vector', lambda e, nt=nt: e.tensor_scalar(
            rows[0:1, 0:nt], srows[0:1, 0:nt], inv, None, op0=ALU.mult),
            sem=SC)
        v = p.op('vector', lambda e, nt=nt: e.tensor_scalar(
            rows[0:1, 512:512 + nt], srows[0:1, 512:512 + nt], inv,
            None, op0=ALU.mult), sem=SC)
        p.wait('vector', SC, v)
        v = p.op('vector', lambda e, nt=nt: e.tensor_tensor(
            mrow[0:1, 0:nt], rows[0:1, 0:nt], rows[0:1, 0:nt],
            op=ALU.mult), sem=SC)
        p.wait('vector', SC, v)
        v = p.op('vector', lambda e, nt=nt: e.tensor_tensor(
            rows[0:1, 512:512 + nt], rows[0:1, 512:512 + nt],
            mrow[0:1, 0:nt], op=ALU.subtract), sem=SC)
        p.wait('scalar', SC, p.val(SC))
        sa1 = p.op('scalar', lambda e, nt=nt: e.activation(
            mrow[0:1, 0:nt], rows[0:1, 512:512 + nt], AF.Ln,
            bias=eps), sem=SA)
        p.wait('scalar', SA, sa1)
        p.op('scalar', lambda e, nt=nt: e.activation(
            rows[0:1, 512:512 + nt], mrow[0:1, 0:nt], AF.Exp,
            scale=neghalf), sem=SA)
        p.wait('tensor', SC, p.val(SC))
        p.wait('tensor', SA, p.val(SA))
        p.wait('tensor', SP, p.val(SP))
        p.wait('tensor', SV, ve_done[-1] if ve_done else 0)
        p.op('tensor', lambda e, nt=nt: e.matmul(
            ps_mb[:, 0:nt], cx.ones_row[:, :], rows[0:1, 0:nt], start=True, stop=True))
        p.op('tensor', lambda e, nt=nt: e.matmul(
            ps_rb[:, 0:nt], cx.ones_row[:, :], rows[0:1, 512:512 + nt],
            start=True, stop=True), sem=SM)
        p.wait('vector', SM, p.val(SM))
        p.wait('vector', SP, pl_done)
        p.op('vector', lambda e, nt=nt: e.tensor_copy(mrow[:, 0:nt], ps_mb[:, 0:nt]))
        v = p.op('vector', lambda e, nt=nt: e.tensor_copy(rrow[:, 0:nt], ps_rb[:, 0:nt]),
                 sem=SC)
        p.wait('vector', SC, v)
        sv0 = p.val(SV)
        p.wait('gpsimd', SV, sv0)
        p.wait('sync', SV, sv0)
        lvs = {}
        mulv = {}

        def emit_load(k):
            sl = k % 4
            q = 'gpsimd' if k % 2 == 0 else 'sync'
            if k >= 4:
                p.wait(q, SV, mulv[k - 4])
            lvs[k] = p.op(q, lambda e, k=k, sl=sl, c0=c0, nt=nt: e.dma_start(
                out=xin[:, sl * 512:sl * 512 + nt],
                in_=src_hbm[k * 128:(k + 1) * 128, c0:c0 + nt]),
                sem=f'{pref}_L{sl}', n=16)
        for k in range(4):
            emit_load(k)
        for k in range(nk):
            sl = k % 4
            p.wait('vector', f'{pref}_L{sl}', lvs[k])
            p.op('vector', lambda e, sl=sl, nt=nt: e.tensor_sub(
                xin[:, sl * 512:sl * 512 + nt], xin[:, sl * 512:sl * 512 + nt],
                mrow[:, 0:nt]), sem=SP)
            mulv[k] = p.op('vector', lambda e, k=k, sl=sl, nt=nt, t0=t0:
                           e.tensor_tensor(
                lnx[:, k * T + t0:k * T + t0 + nt], xin[:, sl * 512:sl * 512 + nt],
                rrow[:, 0:nt], op=ALU.mult), sem=SV)
            if k + 4 < nk:
                emit_load(k + 4)
        pl_done = p.val(SP)
        ve_done.append(p.val(SV))
    return ve_done[-1]


from concourse.bass_utils import run_bass_kernel_spmd


def _sel_consts():
    sel8 = np.zeros((8, 8, 128), np.float32)
    for e in range(8):
        sel8[e, e, :] = 1.0
    sel16 = np.zeros((8, 128), np.float32)
    for m in range(128):
        sel16[m // 16, m] = 1.0
    return sel8, sel16


def _build_s1():
    p = P()
    xT = p.dram('xT', [2048, TK], F32, kind='ExternalInput')
    ident = p.dram('ident', [128, 128], F32, kind='ExternalInput')
    wqkv = p.dram('wqkv0T', [2048, 6144], F32, kind='ExternalInput')
    masks0 = p.dram('masks0', [17, 128, 512], BF16, kind='ExternalInput')
    attnT = p.dram('attnT0', [2048, TG], BF16, kind='ExternalOutput')
    init_gate(p)
    cx = common_alloc(p, ident)
    lnx = p.sbuf('lnx', [128, 16 * TK], BF16)
    ve = emit_ln(p, cx, xT, TK, lnx, 'ln0')
    p.wait('tensor', 'ln0_ve', ve)
    p.wait('vector', 'ln0_ve', ve)
    st = emit_l0_attn(p, cx, lnx, wqkv, masks0, attnT, 'a0')
    p.wait('gpsimd', 'a0_S0', st[0])
    p.wait('gpsimd', 'a0_S1', st[1])
    p.finalize_consts()
    return p.build()


def _build_s2():
    p = P()
    xT = p.dram('xT', [2048, TK], F32, kind='ExternalInput')
    ident = p.dram('ident', [128, 128], F32, kind='ExternalInput')
    wout = p.dram('wout0T', [2048, 2048], F32, kind='ExternalInput')
    attnT = p.dram('attnT0', [2048, TG], BF16, kind='ExternalInput')
    x1T = p.dram('x1T', [2048, TG], F32, kind='ExternalOutput')
    init_gate(p)
    cx = common_alloc(p, ident)
    ares = p.sbuf('ares', [128, 16 * TG], BF16)
    v = load_resident(p, attnT, TG, 16, ares, 'ALR')
    p.wait('tensor', 'ALR', v)
    o0, o1 = emit_outproj_residual(p, cx, 's2', ares, wout, xT, 576, x1T, TG)
    p.wait('gpsimd', 's2_O0', o0)
    p.wait('gpsimd', 's2_O1', o1)
    p.finalize_consts()
    return p.build()


def _build_s3():
    p = P()
    ident = p.dram('ident', [128, 128], F32, kind='ExternalInput')
    x1T = p.dram('x1T', [2048, TG], F32, kind='ExternalInput')
    w1 = p.dram('w1T', [9, 2048, 1024], F32, kind='ExternalInput')
    w3 = p.dram('w3T', [9, 2048, 1024], F32, kind='ExternalInput')
    w2 = p.dram('w2T', [9, 1024, 2048], F32, kind='ExternalInput')
    wg = p.dram('wg0T', [2048, 8], F32, kind='ExternalInput')
    selh = p.dram('sel8', [8, 8, 128], F32, kind='ExternalInput')
    x2T = p.dram('x2T', [2048, TG], F32, kind='ExternalOutput')
    init_gate(p)
    cx = common_alloc(p, ident)
    o0, o1 = emit_ffn(p, cx, 'f', x1T, w1, w3, w2, wg, selh, x2T)
    p.wait('gpsimd', 'f_O0', o0)
    p.wait('gpsimd', 'f_O1', o1)
    p.finalize_consts()
    return p.build()


def _build_s4():
    p = P()
    ident = p.dram('ident', [128, 128], F32, kind='ExternalInput')
    x2T = p.dram('x2T', [2048, TG], F32, kind='ExternalInput')
    wq = p.dram('wqkv1T', [2048, 6144], F32, kind='ExternalInput')
    bq = p.dram('bqkv', [128, 48], F32, kind='ExternalInput')
    rC = p.dram('ropeC', [128, TG], F32, kind='ExternalInput')
    rS = p.dram('ropeS', [128, TG], F32, kind='ExternalInput')
    pm = p.dram('perm', [128, 128], BF16, kind='ExternalInput')
    qk = p.dram('qkvT1', [6144, TG], BF16, kind='ExternalOutput')
    init_gate(p)
    cx = common_alloc(p, ident)
    o0, o1 = emit_l1qkv(p, cx, 's4', x2T, wq, bq, rC, rS, pm, qk)
    p.wait('gpsimd', 's4_O0', o0)
    p.wait('gpsimd', 's4_O1', o1)
    p.finalize_consts()
    return p.build()


def _build_s5():
    p = P()
    ident = p.dram('ident', [128, 128], F32, kind='ExternalInput')
    qk = p.dram('qkvT1', [6144, TG], BF16, kind='ExternalInput')
    mk1 = p.dram('masks1', [7, 128, 512], BF16, kind='ExternalInput')
    skT = p.dram('sinksT', [32, 64, 16], F32, kind='ExternalInput')
    ska = p.dram('sinks_aug', [32, 16, 65], F32, kind='ExternalInput')
    at1 = p.dram('attn1T', [2048, 2048], BF16, kind='ExternalOutput')
    init_gate(p)
    cx = common_alloc(p, ident)
    s0, s1 = emit_l1attn(p, cx, 's5', qk, mk1, skT, ska, at1)
    p.wait('gpsimd', 's5_S0', s0)
    p.wait('gpsimd', 's5_S1', s1)
    p.finalize_consts()
    return p.build()


def _build_s6():
    p = P()
    ident = p.dram('ident', [128, 128], F32, kind='ExternalInput')
    at1 = p.dram('attn1T', [2048, 2048], BF16, kind='ExternalInput')
    wo = p.dram('wo1T', [2048, 2048], F32, kind='ExternalInput')
    bo = p.dram('bo', [128, 16], F32, kind='ExternalInput')
    x2T = p.dram('x2T', [2048, TG], F32, kind='ExternalInput')
    x3T = p.dram('x3T', [2048, 2048], F32, kind='ExternalOutput')
    init_gate(p)
    cx = common_alloc(p, ident)
    ares = p.sbuf('ares', [128, 16 * 2048], BF16)
    v = load_resident(p, at1, 2048, 16, ares, 'ALR')
    p.wait('tensor', 'ALR', v)
    bo_sb = p.sbuf('bo_sb', [128, 16], F32)
    p.op('gpsimd', lambda e: e.dma_start(out=bo_sb[:], in_=bo[:]), sem='BO', n=16)
    p.wait('vector', 'BO', 16)
    o0, o1 = emit_outproj_residual(p, cx, 's6', ares, wo, x2T, 64, x3T, 2048,
                                   bias_sb=bo_sb)
    p.wait('gpsimd', 's6_O0', o0)
    p.wait('gpsimd', 's6_O1', o1)
    p.finalize_consts()
    return p.build()


def _build_s7():
    p = P()
    ident = p.dram('ident', [128, 128], F32, kind='ExternalInput')
    x3T = p.dram('x3T', [2048, 2048], F32, kind='ExternalInput')
    wd = p.dram('wd1T', [2048, 2048], F32, kind='ExternalInput')
    wg = p.dram('wg1T', [2048, 8], F32, kind='ExternalInput')
    Ap = p.dram('ApackT', [2048, 128], F32, kind='ExternalInput')
    Bp = p.dram('Bpack', [128, 2048], F32, kind='ExternalInput')
    s16 = p.dram('sel16', [8, 128], F32, kind='ExternalInput')
    out = p.dram('outT', [2048, 2048], F32, kind='ExternalOutput')
    init_gate(p)
    cx = common_alloc(p, ident)
    o0, o1 = emit_l1ffn(p, cx, 's7', x3T, wd, wg, Ap, Bp, s16, out)
    p.wait('gpsimd', 's7_O0', o0)
    p.wait('gpsimd', 's7_O1', o1)
    p.finalize_consts()
    return p.build()


def kernel_unfused(**inputs):
    ins = {k: np.asarray(v) for k, v in inputs.items()}
    sh, per = host_prep(ins)
    sel8, sel16 = _sel_consts()
    N = 8
    cores = list(range(N))

    def run(nc, mk):
        res = run_bass_kernel_spmd(nc, [mk(c) for c in cores], core_ids=cores)
        return res.results

    r = run(_build_s1(), lambda c: {
        'xT': per[c]['xT'], 'ident': sh['ident'], 'wqkv0T': sh['wqkv0T'],
        'masks0': per[c]['masks0']})
    attnT = [np.asarray(r[c]['attnT0']) for c in cores]
    r = run(_build_s2(), lambda c: {
        'xT': per[c]['xT'], 'ident': sh['ident'], 'wout0T': sh['wout0T'],
        'attnT0': attnT[c]})
    x1T = [np.asarray(r[c]['x1T']) for c in cores]
    r = run(_build_s3(), lambda c: {
        'ident': sh['ident'], 'x1T': x1T[c], 'w1T': sh['w1T'], 'w3T': sh['w3T'],
        'w2T': sh['w2T'], 'wg0T': sh['wg0T'], 'sel8': sel8})
    x2T = [np.asarray(r[c]['x2T']) for c in cores]
    r = run(_build_s4(), lambda c: {
        'ident': sh['ident'], 'x2T': x2T[c], 'wqkv1T': sh['wqkv1T'],
        'bqkv': sh['bqkv'], 'ropeC': per[c]['ropeC'], 'ropeS': per[c]['ropeS'],
        'perm': per[c]['perm']})
    qkvT1 = [np.asarray(r[c]['qkvT1']) for c in cores]
    r = run(_build_s5(), lambda c: {
        'ident': sh['ident'], 'qkvT1': qkvT1[c], 'masks1': per[c]['masks1'],
        'sinksT': sh['sinksT'], 'sinks_aug': sh['sinks_aug']})
    attn1T = [np.asarray(r[c]['attn1T']) for c in cores]
    r = run(_build_s6(), lambda c: {
        'ident': sh['ident'], 'attn1T': attn1T[c], 'wo1T': sh['wo1T'],
        'bo': sh['bo'], 'x2T': x2T[c]})
    x3T = [np.asarray(r[c]['x3T']) for c in cores]
    r = run(_build_s7(), lambda c: {
        'ident': sh['ident'], 'x3T': x3T[c], 'wd1T': sh['wd1T'],
        'wg1T': sh['wg1T'], 'ApackT': sh['ApackT'], 'Bpack': sh['Bpack'],
        'sel16': sel16})
    out = np.zeros((4, 4096, 2048), np.float32)
    for c in cores:
        b, h, s0, _, _ = core_geom(c)
        out[b, s0:s0 + 2048] = np.asarray(r[c]['outT'], np.float32).T
    return out



ARENA_F32 = 44928


class _Dummy:
    def __init__(self, name, shape, dtype):
        self.name, self.shape, self.dtype = name, list(shape), dtype

    def __getitem__(self, *a):
        return self

    def rearrange(self, *a, **k):
        return self

    def bitcast(self, *a):
        return self

    def then_inc(self, *a):
        return self


class DryP(P):

    def __init__(self):
        self.em = {e: [] for e in ENGS}
        self.semh = {}
        self.cnt = {}
        self.consts_needed = set()
        self.sbuf_log = []
        self._tcache = {}

    def S(self, name):
        name = self.canon(name)
        if name not in self.semh:
            self.semh[name] = name
            self.cnt[name] = 0
        return self.semh[name]

    def op(self, eng, f, sem=None, n=1):
        if sem is not None:
            self.S(sem)
            sem = self.canon(sem)
            self.cnt[sem] += n
            return self.cnt[sem]
        return None

    def wait(self, eng, sem, val):
        pass

    def barrier(self, engines=ENGS, name=None):
        pass

    def sbuf(self, name, shape, dtype):
        if name in self._tcache:
            return self._tcache[name]
        t = _Dummy(name, shape, dtype)
        self._tcache[name] = t
        self.sbuf_log.append((name, list(shape), dtype))
        return t

    def stage_mark(self):
        self.sbuf_log.append(('__STAGE__', None, None))

    def psum(self, name, shape, dtype=F32):
        return _Dummy(name, shape, dtype)

    def dram(self, name, shape, dtype, kind):
        return _Dummy(name, shape, dtype)


class ArenaP(P):

    def __init__(self, layout, shared):
        super().__init__()
        self.layout = layout
        self.shared = shared
        self.arena = None

    def make_arena(self):
        self.arena = self.nc.alloc_sbuf_tensor("ARENA", [128, ARENA_F32], F32)

    def sbuf(self, name, shape, dtype):
        if name in self.layout:
            off, sh, dt = self.layout[name]
            assert dt == dtype and list(sh) == list(shape), name
            if dtype == F32:
                return self.arena[0:shape[0], off:off + shape[1]]
            elif dtype == BF16:
                return self.arena.bitcast(BF16)[0:shape[0], 2 * off:2 * off + shape[1]]
            else:
                raise ValueError(dtype)
        return super().sbuf(name, shape, dtype)


def _elems_f32(shape, dtype):
    n = shape[1] * (2 if dtype == F32 else 1)
    return (n + 1) // 2 * 2


def _stage_layouts(stage_logs):
    layout = {}
    for log in stage_logs:
        off = 0
        for name, shape, dtype in log:
            nf32 = shape[1] if dtype == F32 else (shape[1] + 1) // 2
            nf32 = (nf32 + 3) // 4 * 4
            layout[name] = (off, shape, dtype)
            off += nf32
        assert off <= ARENA_F32, f"stage overflow {off}"
    return layout


def emit_all_stages(p, t, upto=7):
    cx = common_alloc(p, t['ident'])
    lnx = p.sbuf('lnx', [128, 16 * TK], BF16)
    ve = emit_ln(p, cx, t['xT'], TK, lnx, 'ln0')
    p.wait('tensor', 'ln0_ve', ve)
    p.wait('vector', 'ln0_ve', ve)
    import os as _os
    _nh = int(_os.environ.get('S1NH', '16'))
    _ph = _os.environ.get('S1PH', 'ABCD')
    st = emit_l0_attn(p, cx, lnx, t['wqkv0T'], t['masks0'], t['attnT0'], 'a0',
                      nheads=_nh, phases=_ph)
    if st[0]:
        p.wait('gpsimd', 'a0_S0', st[0])
    if st[1]:
        p.wait('gpsimd', 'a0_S1', st[1])
    p.barrier()
    p.stage_mark()
    if upto < 2:
        return
    ares = p.sbuf('ares', [128, 16 * TG], BF16)
    v = load_resident(p, t['attnT0'], TG, 16, ares, 'ALR')
    p.wait('tensor', 'ALR', v)
    o0, o1 = emit_outproj_residual(p, cx, 's2', ares, t['wout0T'], t['xT'], 576,
                                   t['x1T'], TG, stats_hbm=t['stat1'])
    p.wait('gpsimd', 's2_O0', o0)
    p.wait('gpsimd', 's2_O1', o1)
    p.barrier()
    p.stage_mark()
    if upto < 3:
        return
    o0, o1, o2 = emit_ffn(p, cx, 'f', t['x1T'], t['w1T'], t['w3T'], t['w2T'],
                          t['wg0T'], t['sel8'], t['x2T'], stats_in=t['stat1'],
                          stats_out=t['stat2'])
    p.wait('gpsimd', 'f_O0', o0)
    p.wait('gpsimd', 'f_O1', o1)
    p.wait('gpsimd', 'f_O2', o2)
    p.barrier()
    p.stage_mark()
    if upto < 4:
        return
    o0, o1 = emit_l1qkv(p, cx, 's4', t['x2T'], t['wqkv1T'], t['bqkv'],
                        t['ropeC'], t['ropeS'], t['perm'], t['qkvT1'],
                        stats_in=t['stat2'])
    p.wait('gpsimd', 's4_O0', o0)
    p.wait('gpsimd', 's4_O1', o1)
    p.barrier()
    p.stage_mark()
    if upto < 5:
        return
    s0, s1 = emit_l1attn(p, cx, 's5', t['qkvT1'], t['masks1'], t['sinksT'],
                         t['sinks_aug'], t['attn1T'])
    p.wait('gpsimd', 's5_S0', s0)
    p.wait('gpsimd', 's5_S1', s1)
    p.barrier()
    p.stage_mark()
    if upto < 6:
        return
    ares6 = p.sbuf('ares6', [128, 16 * 2048], BF16)
    v = load_resident(p, t['attn1T'], 2048, 16, ares6, 'ALR6')
    p.wait('tensor', 'ALR6', v)
    bo_sb = p.sbuf('bo_sb', [128, 16], F32)
    p.op('gpsimd', lambda e: e.dma_start(out=bo_sb[:], in_=t['bo'][:]),
         sem='BO', n=16)
    p.wait('vector', 'BO', 16)
    o0, o1 = emit_outproj_residual(p, cx, 's6', ares6, t['wo1T'], t['x2T'], 64,
                                   t['x3T'], 2048, bias_sb=bo_sb,
                                   stats_hbm=t['stat3'])
    p.wait('gpsimd', 's6_O0', o0)
    p.wait('gpsimd', 's6_O1', o1)
    p.barrier()
    p.stage_mark()
    if upto < 7:
        return
    o0, o1 = emit_l1ffn(p, cx, 's7', t['x3T'], t['wd1T'], t['wg1T'],
                        t['ApackT'], t['Bpack'], t['sel16'], t['outT'],
                        stats_in=t['stat3'])
    p.wait('gpsimd', 's7_O0', o0)
    p.wait('gpsimd', 's7_O1', o1)




def _declare_drams(p, ext_scratch=False):
    t = {}
    SK = 'ExternalOutput' if ext_scratch else 'Internal'
    t['ident'] = p.dram('ident', [128, 128], F32, kind='ExternalInput')
    t['xT'] = p.dram('xT', [2048, TK], F32, kind='ExternalInput')
    t['wqkv0T'] = p.dram('wqkv0T', [2048, 6144], F32, kind='ExternalInput')
    t['masks0'] = p.dram('masks0', [17, 128, 512], BF16, kind='ExternalInput')
    t['wout0T'] = p.dram('wout0T', [2048, 2048], F32, kind='ExternalInput')
    t['w1T'] = p.dram('w1T', [9, 2048, 1024], F32, kind='ExternalInput')
    t['w3T'] = p.dram('w3T', [9, 2048, 1024], F32, kind='ExternalInput')
    t['w2T'] = p.dram('w2T', [9, 1024, 2048], F32, kind='ExternalInput')
    t['wg0T'] = p.dram('wg0T', [2048, 8], F32, kind='ExternalInput')
    t['sel8'] = p.dram('sel8', [8, 8, 128], F32, kind='ExternalInput')
    t['wqkv1T'] = p.dram('wqkv1T', [2048, 6144], F32, kind='ExternalInput')
    t['bqkv'] = p.dram('bqkv', [128, 48], F32, kind='ExternalInput')
    t['ropeC'] = p.dram('ropeC', [128, TG], F32, kind='ExternalInput')
    t['ropeS'] = p.dram('ropeS', [128, TG], F32, kind='ExternalInput')
    t['perm'] = p.dram('perm', [128, 128], BF16, kind='ExternalInput')
    t['masks1'] = p.dram('masks1', [7, 128, 512], BF16, kind='ExternalInput')
    t['sinksT'] = p.dram('sinksT', [32, 64, 16], F32, kind='ExternalInput')
    t['sinks_aug'] = p.dram('sinks_aug', [32, 16, 65], F32, kind='ExternalInput')
    t['wo1T'] = p.dram('wo1T', [2048, 2048], F32, kind='ExternalInput')
    t['bo'] = p.dram('bo', [128, 16], F32, kind='ExternalInput')
    t['wd1T'] = p.dram('wd1T', [2048, 2048], F32, kind='ExternalInput')
    t['wg1T'] = p.dram('wg1T', [2048, 8], F32, kind='ExternalInput')
    t['ApackT'] = p.dram('ApackT', [2048, 128], F32, kind='ExternalInput')
    t['Bpack'] = p.dram('Bpack', [128, 2048], F32, kind='ExternalInput')
    t['sel16'] = p.dram('sel16', [8, 128], F32, kind='ExternalInput')
    t['stat1'] = p.dram('stat1', [2, TG], F32, kind='Internal')
    t['stat2'] = p.dram('stat2', [2, TG], F32, kind='Internal')
    t['stat3'] = p.dram('stat3', [2, 2048], F32, kind='Internal')
    t['attnT0'] = p.dram('attnT0', [2048, TG], BF16, kind=SK)
    t['x1T'] = p.dram('x1T', [2048, TG], F32, kind=SK)
    t['x2T'] = p.dram('x2T', [2048, TG], F32, kind=SK)
    t['qkvT1'] = p.dram('qkvT1', [6144, TG], BF16, kind=SK)
    t['attn1T'] = p.dram('attn1T', [2048, 2048], BF16, kind=SK)
    t['x3T'] = p.dram('x3T', [2048, 2048], F32, kind=SK)
    t['outT'] = p.dram('outT', [2048, 2048], F32, kind='ExternalOutput')
    return t


def build_fused(upto=7, ext_scratch=False):
    dp = DryP()
    td = _declare_drams(dp)
    emit_all_stages(dp, td, upto)
    logs = [[]]
    shared = set()
    for name, shape, dtype in dp.sbuf_log:
        if name == '__STAGE__':
            logs.append([])
            continue
        if name.startswith('cx_') or name.startswith('lnsc_') or \
           name.startswith('gtsc_') or name == 'c_consts':
            shared.add(name)
            continue
        logs[-1].append((name, shape, dtype))
    layout = _stage_layouts(logs)
    p = ArenaP(layout, shared)
    init_gate(p)
    p.make_arena()
    t = _declare_drams(p, ext_scratch)
    emit_all_stages(p, t, upto)
    p.finalize_consts()
    return p


def kernel_fused(**inputs):
    ins = {k: np.asarray(v) for k, v in inputs.items()}
    sh, per = host_prep(ins)
    sel8, sel16 = _sel_consts()
    p = build_fused()
    nc = p.build()
    cores = list(range(8))
    in_maps = []
    for c in cores:
        m = {'ident': sh['ident'], 'xT': per[c]['xT'], 'wqkv0T': sh['wqkv0T'],
             'masks0': per[c]['masks0'], 'wout0T': sh['wout0T'],
             'w1T': sh['w1T'], 'w3T': sh['w3T'], 'w2T': sh['w2T'],
             'wg0T': sh['wg0T'], 'sel8': sel8, 'wqkv1T': sh['wqkv1T'],
             'bqkv': sh['bqkv'], 'ropeC': per[c]['ropeC'],
             'ropeS': per[c]['ropeS'], 'perm': per[c]['perm'],
             'masks1': per[c]['masks1'], 'sinksT': sh['sinksT'],
             'sinks_aug': sh['sinks_aug'], 'wo1T': sh['wo1T'], 'bo': sh['bo'],
             'wd1T': sh['wd1T'], 'wg1T': sh['wg1T'], 'ApackT': sh['ApackT'],
             'Bpack': sh['Bpack'], 'sel16': sel16}
        in_maps.append(m)
    res = run_bass_kernel_spmd(nc, in_maps, core_ids=cores)
    out = np.zeros((4, 4096, 2048), np.float32)
    for c in cores:
        b, h, s0, _, _ = core_geom(c)
        out[b, s0:s0 + 2048] = np.asarray(res.results[c]['outT'], np.float32).T
    return out


kernel = kernel_fused


# version 28 (final)
# speedup vs baseline: 1.0800x; 1.0327x over previous
import numpy as np
import ml_dtypes
import concourse.bass as bass
import concourse.mybir as mybir

F32 = mybir.dt.float32
BF16 = mybir.dt.bfloat16
AF = mybir.ActivationFunctionType
ALU = mybir.AluOpType
AX = mybir.AxisListType
BF = ml_dtypes.bfloat16

C_SHIFT = 12.0
MASK_NEG = -200.0
EPS = 1e-5
TG = 2176
TK = 2816
OWN0, OWN1 = 64, 2112

ENGS = ('sync', 'gpsimd', 'vector', 'scalar', 'tensor')


def chunks(T, step=512):
    out = []
    t = 0
    while t < T:
        out.append((t, min(step, T - t)))
        t += step
    return out


class P:
    def __init__(self, name="prog"):
        self.nc = bass.Bass("TRN2", target_bir_lowering=False)
        self.em = {e: [] for e in ENGS}
        self.semh = {}
        self.cnt = {}
        self.consts_needed = set()

    @staticmethod
    def canon(name):
        return name.split('_', 1)[1] if '_' in name else name

    def S(self, name):
        name = self.canon(name)
        if name not in self.semh:
            self.semh[name] = self.nc.alloc_semaphore("sem_" + name)
            self.cnt[name] = 0
        return self.semh[name]

    def val(self, sem):
        self.S(sem)
        return self.cnt[self.canon(sem)]

    def op(self, eng, f, sem=None, n=1):
        if sem is not None:
            h = self.S(sem)
            sem = self.canon(sem)
            self.cnt[sem] += n
            nn = n

            def g(e):
                f(e).then_inc(h, nn)
            self.em[eng].append(g)
            return self.cnt[sem]
        else:
            self.em[eng].append(lambda e: f(e))
            return None

    def wait(self, eng, sem, val):
        if val <= 0:
            return
        h = self.S(sem)
        self.em[eng].append(lambda e: e.wait_ge(h, val))

    def stage_mark(self):
        pass

    def barrier(self, engines=ENGS, name=None):
        vals = {}
        for eng in engines:
            vals[eng] = self.op(eng, lambda e: e.nop(), sem=f"SB{eng}")
        for eng in engines:
            for o in engines:
                if o != eng:
                    self.wait(eng, f"SB{o}", vals[o])

    def sbuf(self, name, shape, dtype):
        if not hasattr(self, '_tcache'):
            self._tcache = {}
        if name in self._tcache:
            return self._tcache[name]
        t = self.nc.alloc_sbuf_tensor(name, list(shape), dtype)
        self._tcache[name] = t
        return t

    def psum(self, name, shape, dtype=F32):
        return self.nc.alloc_psum_tensor(name, list(shape), dtype)

    def dram(self, name, shape, dtype, kind):
        return self.nc.dram_tensor(name, list(shape), dtype, kind=kind)

    def const(self, v):
        self.consts_needed.add(float(v))
        return float(v)

    def finalize_consts(self):
        vals = sorted(self.consts_needed)
        ct = self.sbuf("c_consts", [1, max(1, len(vals) + 1)], F32)
        pre = []
        for i, v in enumerate(vals):
            ap = ct[0:1, i:i + 1]
            self.nc.const_aps.register(F32, v, ap)
            pre.append(lambda e, ap=ap, vv=v: e.memset(ap, vv))
        h = self.S('init')
        pre.append(lambda e, ct=ct: e.memset(
            ct[0:1, len(vals):len(vals) + 1], 0.0).then_inc(h, 1))
        self.em['vector'] = pre + self.em['vector']

    def build(self):
        with self.nc.Block() as block:
            for eng in ENGS:
                if not self.em[eng]:
                    continue
                fns = self.em[eng]

                def runner(e, fns=fns):
                    for f in fns:
                        f(e)
                getattr(block, eng)(runner)
        return self.nc


def init_gate(p):
    p.S('init')
    p.cnt['init'] = 1
    for eng in ENGS:
        if eng != 'vector':
            p.wait(eng, 'init', 1)


def core_geom(c):
    b, h = c // 2, c % 2
    s0 = 2048 * h
    return b, h, s0, s0 - 640, s0 - 64


L0_QCHUNKS = [(0, 512), (512, 512), (1024, 512), (1536, 512), (2048, 128)]
L0_NKK = [9, 9, 9, 9, 6]
L1_QCHUNKS = [(64 + 512 * i, 512) for i in range(4)]


def l0_mask_slot(qc, kk):
    if qc == 0 and kk <= 4:
        return 9 + kk
    if qc == 1 and kk == 0:
        return 14
    if qc == 4 and kk >= 4:
        return 15 + (kk - 4)
    return kk


def l1_mask_slot(qc, kk):
    if qc == 0 and kk == 0:
        return 5
    if qc == 3 and kk == 4:
        return 6
    return kk


def build_l0_masks(c):
    _, h, s0, gk0, g0 = core_geom(c)
    out = np.full((17, 128, 512), MASK_NEG, np.float32)
    for kk in range(9):
        jj = np.arange(128)[:, None] + kk * 128
        ii = np.arange(512)[None, :]
        band = (jj >= ii + 64) & (jj <= ii + 576)
        out[kk] = np.where(band, -C_SHIFT, MASK_NEG)
    for qc, kk in ([(0, k) for k in range(5)] + [(1, 0), (4, 4), (4, 5)]):
        qq, nq = L0_QCHUNKS[qc]
        jj = np.arange(128)[:, None] + 4 * qc * 128 + kk * 128
        ii = np.arange(nq)[None, :] + qq
        band = (jj >= ii + 64) & (jj <= ii + 576)
        valid = band & (gk0 + jj >= 0)
        qpos = g0 + ii
        phantom = (qpos < 0) | (qpos >= 4096)
        valid = np.where(phantom, jj == ii + 576, valid)
        out[l0_mask_slot(qc, kk), :, :nq] = np.where(valid, -C_SHIFT, MASK_NEG)
    return out.astype(BF)


def build_l1_masks(c):
    _, h, s0, gk0, g0 = core_geom(c)
    out = np.full((7, 128, 512), MASK_NEG, np.float32)
    for qc, (qq, nq) in enumerate(L1_QCHUNKS):
        for kk in range(5):
            jj = np.arange(128)[:, None] + (qq - 64) + kk * 128
            ii = np.arange(nq)[None, :] + qq
            valid = (np.abs(ii - jj) <= 64) & (g0 + jj >= 0) & (g0 + jj < 4096)
            out[l1_mask_slot(qc, kk), :, :nq] = np.where(valid, -C_SHIFT, MASK_NEG)
    return out.astype(BF)


def build_rope(c):
    _, _, _, _, g0 = core_geom(c)
    half = 32
    inv_freq = (1.0 / (10000.0 ** (np.arange(half, dtype=np.float64) / half))).astype(np.float32)
    pos = (g0 + np.arange(TG)).astype(np.float32)
    ang = pos[None, :] * inv_freq[:, None]
    cos, sin = np.cos(ang).astype(np.float32), np.sin(ang).astype(np.float32)
    r = np.arange(128)
    f = r % 32
    Cfull = cos[f]
    sign = np.where((r % 64) < 32, -1.0, 1.0).astype(np.float32)
    Sfull = sin[f] * sign[:, None]
    perm = np.zeros((128, 128), np.float32)
    for m in range(128):
        base = (m // 64) * 64
        perm[base + (m - base + 32) % 64, m] = 1.0
    return Cfull, Sfull, perm.astype(BF)


def slice_pad_T(x_bse, b, start, length):
    S = x_bse.shape[1]
    out = np.zeros((length, x_bse.shape[2]), np.float32)
    lo, hi = max(0, start), min(S, start + length)
    if hi > lo:
        out[lo - start:hi - start] = x_bse[b, lo:hi]
    return np.ascontiguousarray(out.T)


def host_prep(ins):
    sh = {}
    sh['wqkv0T'] = np.ascontiguousarray(ins['L0_attn_w_qkv'].T)
    sh['wout0T'] = np.ascontiguousarray(ins['L0_attn_w_out'].T)
    sh['wg0T'] = np.ascontiguousarray(ins['L0_ffn_gate_w'].T)
    w1T = np.concatenate([np.transpose(ins['L0_ffn_w1'], (0, 2, 1)),
                          ins['L0_ffn_ws1'].T[None]], 0)
    w3T = np.concatenate([np.transpose(ins['L0_ffn_w3'], (0, 2, 1)),
                          ins['L0_ffn_ws3'].T[None]], 0)
    w2T = np.concatenate([np.transpose(ins['L0_ffn_w2'], (0, 2, 1)),
                          ins['L0_ffn_ws2'].T[None]], 0)
    sh['w1T'] = np.ascontiguousarray(w1T)
    sh['w3T'] = np.ascontiguousarray(w3T)
    sh['w2T'] = np.ascontiguousarray(w2T)
    sh['wqkv1T'] = np.ascontiguousarray(ins['L1_attn_Wqkv'].T)
    sh['bqkv'] = np.ascontiguousarray(ins['L1_attn_bqkv'].reshape(48, 128).T)
    sh['wo1T'] = np.ascontiguousarray(ins['L1_attn_Wo'].T)
    sh['bo'] = np.ascontiguousarray(ins['L1_attn_bo'].reshape(16, 128).T)
    sk = ins['L1_attn_sinks']
    sh['sinksT'] = np.ascontiguousarray(np.transpose(sk, (0, 2, 1)))
    sh['sinks_aug'] = np.ascontiguousarray(
        np.concatenate([sk, np.ones((32, 16, 1), np.float32)], 2))
    sh['wd1T'] = np.ascontiguousarray(ins['L1_ffn_weight'].T)
    sh['wg1T'] = np.ascontiguousarray(ins['L1_ffn_gate_w'].T)
    sh['ApackT'] = np.ascontiguousarray(ins['L1_ffn_A_w'].reshape(128, 2048).T)
    sh['Bpack'] = np.ascontiguousarray(
        np.transpose(ins['L1_ffn_B_w'], (0, 2, 1)).reshape(128, 2048))
    sh['ident'] = np.eye(128, dtype=np.float32)
    per = []
    for c in range(8):
        b, h, s0, gk0, g0 = core_geom(c)
        d = {}
        d['xT'] = slice_pad_T(ins['L0_attn_x'], b, gk0, TK)
        d['masks0'] = build_l0_masks(c)
        d['masks1'] = build_l1_masks(c)
        Cf, Sf, perm = build_rope(c)
        d['ropeC'] = Cf
        d['ropeS'] = Sf
        d['perm'] = perm.astype(BF)
        per.append(d)
    return sh, per


class Ctx:
    pass


def common_alloc(p, ident_hbm):
    cx = Ctx()
    cx.ident = p.sbuf("cx_ident", [128, 128], BF16)
    cx.ones_col = p.sbuf("cx_ones_col", [128, 1], BF16)
    cx.ones_row = p.sbuf("cx_ones_row", [1, 128], F32)
    cx.ps = [p.psum(f"cx_ps{i}", [128, 512], F32) for i in range(7)]
    cx.psbf = p.psum("cx_psbf", [128, 512], BF16)
    identf = p.sbuf("cx_identf", [128, 128], F32)
    cx.identf = identf
    p.op('gpsimd', lambda e: e.dma_start(out=identf[:], in_=ident_hbm[:]),
         sem='ALI', n=16)
    p.wait('vector', 'ALI', 16)
    p.op('vector', lambda e: e.tensor_copy(cx.ident[:], identf[:]))
    p.op('vector', lambda e: e.memset(cx.ones_col[:], 1.0))
    p.op('vector', lambda e: e.memset(cx.ones_row[:], 1.0))
    cx.expbias = p.sbuf("cx_expbias", [128, 1], F32)
    p.op('vector', lambda e: e.memset(cx.expbias[:], -C_SHIFT))
    cx.neg1 = p.sbuf("cx_neg1", [128, 1], F32)
    v = p.op('vector', lambda e: e.memset(cx.neg1[:], -1.0), sem='cominit')
    p.wait('vector', 'cominit', v)
    return cx


def emit_ln(p, cx, src_hbm, T, lnx, pref, bank0=0, col0=0):
    nk = 16
    ps_sum = cx.ps[bank0][0:1, :]
    ps_sq = cx.ps[bank0 + 1][0:1, :]
    ps_mb = cx.ps[bank0 + 2]
    ps_rb = cx.ps[bank0 + 3]
    xin = p.sbuf("lnsc_xin", [128, 5 * 512], F32)
    xbf = p.sbuf("lnsc_xbf", [128, 2 * 512], BF16)
    sqbf = p.sbuf("lnsc_sqbf", [128, 2 * 512], BF16)
    rows = p.sbuf("lnsc_rows", [1, 2 * 512], F32)
    mrow = p.sbuf("lnsc_mrow", [128, 512], F32)
    rrow = p.sbuf("lnsc_rrow", [128, 512], F32)
    inv = p.const(1.0 / 2048.0)
    eps = p.const(EPS)
    neghalf = p.const(-0.5)
    SA, SC, SM, SV, SP = (f'{pref}_ac', f'{pref}_cv', f'{pref}_mm',
                          f'{pref}_ve', f'{pref}_pl')
    for ci, (t0, nt) in enumerate(chunks(T)):
        p.wait('tensor', SC, p.val(SC))
        if ci > 0:
            p.wait('gpsimd', SV, p.val(SV))
            p.wait('sync', SV, p.val(SV))
        sm0 = p.val(SM)
        for k in range(nk):
            sl = k % 4
            kp = k % 2
            q = 'gpsimd' if k % 2 == 0 else 'sync'
            if k >= 4:
                p.wait(q, SM, sm0 + 2 * (k - 4) + 2)
            alv = p.op(q, lambda e, k=k, sl=sl, t0=t0, nt=nt: e.dma_start(
                out=xin[:, sl * 512:sl * 512 + nt],
                in_=src_hbm[k * 128:(k + 1) * 128, col0 + t0:col0 + t0 + nt]),
                sem=f'{pref}_L{sl}', n=16)
            p.wait('scalar', f'{pref}_L{sl}', alv)
            p.wait('vector', f'{pref}_L{sl}', alv)
            if k >= 2:
                p.wait('scalar', SM, sm0 + 2 * (k - 2) + 2)
                p.wait('vector', SM, sm0 + 2 * (k - 2) + 2)
            p.op('scalar', lambda e, sl=sl, kp=kp, nt=nt: e.square(
                sqbf[:, kp * 512:kp * 512 + nt], xin[:, sl * 512:sl * 512 + nt]),
                sem=SA)
            p.op('vector', lambda e, sl=sl, kp=kp, nt=nt: e.tensor_copy(
                xbf[:, kp * 512:kp * 512 + nt], xin[:, sl * 512:sl * 512 + nt]),
                sem=SC)
            p.wait('tensor', SA, p.val(SA))
            p.wait('tensor', SC, p.val(SC))
            p.op('tensor', lambda e, k=k, kp=kp, nt=nt: e.matmul(
                ps_sum[:, 0:nt], cx.ones_col[:], xbf[:, kp * 512:kp * 512 + nt],
                start=(k == 0), stop=(k == nk - 1)))
            p.op('tensor', lambda e, k=k, kp=kp, nt=nt: e.matmul(
                ps_sq[:, 0:nt], cx.ones_col[:], sqbf[:, kp * 512:kp * 512 + nt],
                start=(k == 0), stop=(k == nk - 1)), sem=SM, n=2)
        p.wait('vector', SM, p.val(SM))
        v = p.op('vector', lambda e, nt=nt: e.tensor_scalar(
            rows[0:1, 0:nt], ps_sum[:, 0:nt], inv, None, op0=ALU.mult), sem=SC)
        v = p.op('vector', lambda e, nt=nt: e.tensor_scalar(
            rows[0:1, 512:512 + nt], ps_sq[:, 0:nt], inv, None, op0=ALU.mult), sem=SC)
        p.wait('vector', SC, v)
        v = p.op('vector', lambda e, nt=nt: e.tensor_tensor(
            mrow[0:1, 0:nt], rows[0:1, 0:nt], rows[0:1, 0:nt],
            op=ALU.mult), sem=SC)
        p.wait('vector', SC, v)
        v = p.op('vector', lambda e, nt=nt: e.tensor_tensor(
            rows[0:1, 512:512 + nt], rows[0:1, 512:512 + nt],
            mrow[0:1, 0:nt], op=ALU.subtract), sem=SC)
        p.wait('scalar', SC, p.val(SC))
        sa1 = p.op('scalar', lambda e, nt=nt: e.activation(
            mrow[0:1, 0:nt], rows[0:1, 512:512 + nt], AF.Ln,
            bias=eps), sem=SA)
        p.wait('scalar', SA, sa1)
        p.op('scalar', lambda e, nt=nt: e.activation(
            rows[0:1, 512:512 + nt], mrow[0:1, 0:nt], AF.Exp,
            scale=neghalf), sem=SA)
        p.wait('vector', SA, p.val(SA))
        p.op('vector', lambda e: e.nop(), sem=SC)
        p.wait('tensor', SC, p.val(SC))
        p.wait('tensor', SP, p.val(SP))
        p.op('tensor', lambda e, nt=nt: e.matmul(
            ps_mb[:, 0:nt], cx.ones_row[:, :], rows[0:1, 0:nt], start=True, stop=True))
        p.op('tensor', lambda e, nt=nt: e.matmul(
            ps_rb[:, 0:nt], cx.ones_row[:, :], rows[0:1, 512:512 + nt],
            start=True, stop=True), sem=SM)
        p.wait('vector', SM, p.val(SM))
        p.wait('vector', SP, p.val(SP))
        p.op('vector', lambda e, nt=nt: e.tensor_copy(mrow[:, 0:nt], ps_mb[:, 0:nt]))
        v = p.op('vector', lambda e, nt=nt: e.tensor_copy(rrow[:, 0:nt], ps_rb[:, 0:nt]),
             sem=SC)
        p.wait('vector', SC, v)
        scv = p.val(SC)
        sv0 = p.val(SV)
        p.wait('gpsimd', SC, scv)
        p.wait('gpsimd', SV, sv0)
        p.wait('sync', SC, scv)
        p.wait('sync', SV, sv0)
        lvs = []
        for k in range(4):
            q = 'gpsimd' if k % 2 == 0 else 'sync'
            lvs.append(p.op(q, lambda e, k=k, t0=t0, nt=nt: e.dma_start(
                out=xin[:, (k % 4) * 512:(k % 4) * 512 + nt],
                in_=src_hbm[k * 128:(k + 1) * 128, col0 + t0:col0 + t0 + nt]),
                sem=f'{pref}_L{k % 4}', n=16))
        for k in range(nk):
            sl = k % 4
            p.wait('gpsimd', f'{pref}_L{sl}', lvs[k])
            pl = p.op('gpsimd', lambda e, sl=sl, nt=nt: e.tensor_sub(
                xin[:, sl * 512:sl * 512 + nt], xin[:, sl * 512:sl * 512 + nt],
                mrow[:, 0:nt]), sem=SP)
            p.wait('vector', SP, pl)
            p.op('vector', lambda e, k=k, sl=sl, nt=nt, t0=t0: e.tensor_tensor(
                lnx[:, k * T + t0:k * T + t0 + nt], xin[:, sl * 512:sl * 512 + nt],
                rrow[:, 0:nt], op=ALU.mult), sem=SV)
            if k + 4 < nk:
                q = 'gpsimd' if k % 2 == 0 else 'sync'
                p.wait(q, SV, sv0 + k + 1)
                lvs.append(p.op(q, lambda e, k=k, t0=t0, nt=nt: e.dma_start(
                    out=xin[:, ((k + 4) % 4) * 512:((k + 4) % 4) * 512 + nt],
                    in_=src_hbm[(k + 4) * 128:(k + 5) * 128,
                                col0 + t0:col0 + t0 + nt]),
                    sem=f'{pref}_L{(k + 4) % 4}', n=16))
    return p.val(SV)


def load_wcol(p, pref, wsrc, w32ap, wbfap, ncols, nk=16, queue='sync'):
    nslab = (nk + 3) // 4
    for s in range(min(nslab, (nk + 3) // 4)):
        k0 = s * 4
        kn = min(4, nk - k0)
        p.op(queue, lambda e, k0=k0, kn=kn, s=s: e.dma_start(
            out=w32ap[:, k0 * ncols:(k0 + kn) * ncols]
                .rearrange("p (k f) -> p k f", k=kn),
            in_=wsrc[k0 * 128:(k0 + kn) * 128, :]
                .rearrange("(k p) f -> p k f", p=128)), sem='WL', n=16)
    wl = p.val('WL')
    p.wait('scalar', 'WL', wl)
    p.op('scalar', lambda e: e.copy(wbfap[:, 0:nk * ncols], w32ap[:, 0:nk * ncols]),
         sem=f'{pref}_cast')
    return wl, p.val(f'{pref}_cast')


def emit_l0_attn(p, cx, lnx, wqkv, masks_hbm, attnT_hbm, pref='a0', nheads=16, phases='ABCD'):
    msk = p.sbuf(f'{pref}_msk', [128, 17 * 512], BF16)
    p.op('gpsimd', lambda e: e.dma_start(
        out=msk.rearrange("p (s f) -> p s f", s=17),
        in_=masks_hbm.rearrange("s p f -> p s f")), sem='ALM', n=16)
    mskv = p.val('ALM')
    qh = p.sbuf(f'{pref}_qh', [128, TG], BF16)
    kh = p.sbuf(f'{pref}_kh', [128, TK], BF16)
    vh = p.sbuf(f'{pref}_vh', [128, TK], BF16)
    v_tm = p.sbuf(f'{pref}_vtm', [128, 22 * 128], BF16)
    w32 = p.sbuf(f'{pref}_w32', [128, 4 * 512], F32)
    wbf = p.sbuf(f'{pref}_wbf', [128, 2 * 2048], BF16)
    slab_q = [0]
    qc_vals = []
    sexp = p.sbuf(f'{pref}_sexp', [128, 4 * 512], F32)
    pT = p.sbuf(f'{pref}_pT', [128, 4 * 512], BF16)
    rrow1 = p.sbuf(f'{pref}_rrow1', [1, 512], F32)
    lnt = p.sbuf(f'{pref}_lnt', [128, 512], F32)
    qscl = p.sbuf(f'{pref}_qscl', [128, 1], F32)
    v = p.op('vector', lambda e: e.memset(qscl[:], float(1.0 / np.sqrt(128.0))),
             sem=f'{pref}_cm')
    p.wait('vector', f'{pref}_cm', v)
    rb = p.sbuf(f'{pref}_rb', [128, 512], F32)
    att_out = p.sbuf(f'{pref}_att', [128, 2 * TG], BF16)
    SCAST, SMMD, SEVQ = f'{pref}_cast', f'{pref}_mmd', f'{pref}_evq'
    SMS, SCM, SAE, SMP, SCR, SMB, SST = (f'{pref}_ms', f'{pref}_cm', f'{pref}_ae',
                                         f'{pref}_mp', f'{pref}_cr', f'{pref}_mb', 'ST')
    p.wait('vector', 'ALM', mskv)
    cast_done = []
    mmdone = []
    st_after_head = []
    kki = 0
    msv, cmv, aev, mpv = {}, {}, {}, {}
    evq = 0
    tr_copy_done = []
    for h in range(nheads):
        vev = []
        for idx, part in enumerate((1, 2, 0)):
            pseq = h * 3 + idx
            par = pseq % 2
            col = part * 2048 + h * 128
            first_cast = True
            for s in range(4):
                sl = slab_q[0] % 4
                if slab_q[0] >= 4:
                    p.wait('sync', SCAST, qc_vals[slab_q[0] - 4])
                dv = p.op('sync', lambda e, s=s, sl=sl, col=col: e.dma_start(
                    out=w32[:, sl * 512:(sl + 1) * 512]
                        .rearrange("p (k f) -> p k f", k=4),
                    in_=wqkv[s * 512:(s + 1) * 512, col:col + 128]
                        .rearrange("(k p) f -> p k f", p=128)),
                    sem=f'{pref}_WS{sl}', n=16)
                p.wait('scalar', f'{pref}_WS{sl}', dv)
                if first_cast and pseq >= 2:
                    p.wait('scalar', SMMD, mmdone[pseq - 2])
                first_cast = False
                qc_vals.append(p.op('scalar', lambda e, sl=sl, s=s, par=par: e.copy(
                    wbf[:, par * 2048 + s * 512:par * 2048 + (s + 1) * 512],
                    w32[:, sl * 512:(sl + 1) * 512]), sem=SCAST))
                slab_q[0] += 1
            cast_done.append(p.val(SCAST))
            p.wait('tensor', SCAST, cast_done[pseq])
            T = TG if part == 0 else TK
            roff = 576 if part == 0 else 0
            dst = (qh, kh, vh)[part]
            nchq = len(chunks(T))
            for ci, (t0, nt) in enumerate(chunks(T)):
                bk = ci % 2
                p.wait('tensor', SEVQ, evq if (ci == 0 and idx == 0) else evq - 1)
                for k in range(16):
                    p.op('tensor', lambda e, k=k, par=par, bk=bk, t0=t0, nt=nt,
                         roff=roff, T=T: e.matmul(
                             cx.ps[bk][:, 0:nt],
                             wbf[:, par * 2048 + k * 128:par * 2048 + (k + 1) * 128],
                             lnx[:, k * TK + roff + t0:k * TK + roff + t0 + nt],
                             start=(k == 0), stop=(k == 15)),
                         sem=(SMMD if k == 15 else None))
                mmv = p.val(SMMD)
                p.wait('vector', SMMD, mmv)
                if part == 0:
                    evq = p.op('vector', lambda e, bk=bk, t0=t0, nt=nt, dst=dst:
                               e.tensor_scalar(dst[:, t0:t0 + nt], cx.ps[bk][:, 0:nt],
                                               qscl[:, 0:1], None, op0=ALU.mult),
                               sem=SEVQ)
                else:
                    evq = p.op('vector', lambda e, bk=bk, t0=t0, nt=nt, dst=dst:
                               e.tensor_copy(dst[:, t0:t0 + nt], cx.ps[bk][:, 0:nt]),
                               sem=SEVQ)
                    if part == 2:
                        vev.append(evq)
                if part == 0:
                    kt_tgt = (22 * (ci + 1) + nchq - 1) // nchq
                    while len(tr_copy_done) < h * 22 + min(22, kt_tgt):
                        kt = len(tr_copy_done) - h * 22
                        gt = len(tr_copy_done)
                        tslot = gt % 2
                        tbank = cx.psbf[:, 0:128] if tslot == 0 else \
                            cx.ps[6].bitcast(BF16)[:, 0:128]
                        p.wait('tensor', SEVQ, vev[((kt + 1) * 128 - 1) // 512])
                        if gt >= 2:
                            p.wait('tensor', SCM, tr_copy_done[gt - 2])
                        p.op('tensor', lambda e, kt=kt, tbank=tbank: e.matmul(
                            tbank, vh[:, kt * 128:(kt + 1) * 128],
                            cx.ident[:], start=True, stop=True,
                            is_transpose=True), sem=SMS)
                        p.wait('vector', SMS, p.val(SMS))
                        tr_copy_done.append(p.op(
                            'vector', lambda e, kt=kt, tbank=tbank: e.tensor_copy(
                                v_tm[:, kt * 128:(kt + 1) * 128].bitcast(F32),
                                tbank.bitcast(F32)), sem=SCM))
            mmdone.append(p.val(SMMD))
        vtm_done = p.val(SCM)
        if 'C' not in phases:
            continue
        p.wait('tensor', SCM, vtm_done)
        tiles = []
        for qc, (qq, nq) in enumerate(L0_QCHUNKS):
            for kk in range(L0_NKK[qc]):
                tiles.append((qc, kk, L0_NKK[qc], qq, nq))
        LA = 3
        ntl = len(tiles)
        kki0 = kki
        for it in range(ntl + LA):
            if it < ntl:
                qc, kk, nkk, qq, nq = tiles[it]
                kg = kki0 + it
                kp = kg % 4
                kbase = (4 * qc + kk) * 128
                slot = l0_mask_slot(qc, kk)
                if kg >= 4:
                    p.wait('tensor', SCM, cmv[kg - 4])
                msv[kg] = p.op('tensor', lambda e, kp=kp, kbase=kbase, qq=qq,
                               nq=nq: e.matmul(
                    cx.ps[kp][:, 0:nq], kh[:, kbase:kbase + 128],
                    qh[:, qq:qq + nq], start=True, stop=True), sem=SMS)
                p.wait('vector', SMS, msv[kg])
                if kg >= 4:
                    p.wait('vector', SAE, aev[kg - 4])
                cmv[kg] = p.op('vector', lambda e, kp=kp, slot=slot, nq=nq:
                               e.tensor_tensor(
                    sexp[:, kp * 512:kp * 512 + nq], cx.ps[kp][:, 0:nq],
                    msk[:, slot * 512:slot * 512 + nq], op=ALU.add), sem=SCM)
                p.wait('scalar', SCM, cmv[kg])
                if kg >= 4:
                    p.wait('scalar', SMP, mpv[kg - 4])
                aev[kg] = p.op('scalar', lambda e, kp=kp, nq=nq: e.activation(
                    pT[:, kp * 512:kp * 512 + nq], sexp[:, kp * 512:kp * 512 + nq],
                    AF.Exp), sem=SAE)
            jt = it - LA
            if jt < 0:
                continue
            qc, kk, nkk, qq, nq = tiles[jt]
            kg = kki0 + jt
            kp = kg % 4
            kbase = (4 * qc + kk) * 128
            p.wait('tensor', SAE, aev[kg])
            if kk == 0 and qc > 0:
                p.wait('tensor', SEVQ, evq)
            p.op('tensor', lambda e, kp=kp, kbase=kbase, nq=nq, kk=kk, nkk=nkk:
                 e.matmul(
                     cx.ps[4][:, 0:nq], v_tm[:, kbase:kbase + 128],
                     pT[:, kp * 512:kp * 512 + nq],
                     start=(kk == 0), stop=(kk == nkk - 1)))
            p.op('tensor', lambda e, kp=kp, nq=nq, kk=kk, nkk=nkk: e.matmul(
                cx.ps[5][0:1, 0:nq], cx.ones_col[:],
                pT[:, kp * 512:kp * 512 + nq],
                start=(kk == 0), stop=(kk == nkk - 1)),
                sem=SMP, n=2)
            mpv[kg] = p.val(SMP)
            if kk == nkk - 1:
                p.wait('vector', SMP, mpv[kg])
                v = p.op('vector', lambda e, nq=nq: e.tensor_copy(
                    rrow1[0:1, 0:nq], cx.ps[5][0:1, 0:nq]), sem=SCR)
                p.wait('tensor', SCR, v)
                p.op('tensor', lambda e, nq=nq: e.matmul(
                    cx.ps[6][:, 0:nq], cx.ones_row[:, :], rrow1[0:1, 0:nq],
                    start=True, stop=True), sem=SMB)
                p.wait('scalar', SMB, p.val(SMB))
                av = p.op('scalar', lambda e, nq=nq: e.activation(
                    lnt[:, 0:nq], cx.ps[6][:, 0:nq], AF.Ln), sem=SAE)
                p.wait('scalar', SAE, av)
                av = p.op('scalar', lambda e, nq=nq: e.activation(
                    rb[:, 0:nq], lnt[:, 0:nq], AF.Exp, scale=cx.neg1[:, 0:1]),
                    sem=SAE)
                p.wait('vector', SAE, av)
                if h >= 2:
                    p.wait('vector', f'{pref}_S{h % 2}', st_after_head[h - 2])
                evq = p.op('vector', lambda e, nq=nq, qq=qq, h=h: e.tensor_tensor(
                    att_out[:, (h % 2) * TG + qq:(h % 2) * TG + qq + nq],
                    cx.ps[4][:, 0:nq], rb[:, 0:nq], op=ALU.mult), sem=SEVQ)
        kki = kki0 + ntl
        if 'D' not in phases:
            continue
        p.wait('gpsimd', SEVQ, evq)
        p.op('gpsimd', lambda e, h=h: e.dma_start(
            out=attnT_hbm[h * 128:(h + 1) * 128, :],
            in_=att_out[:, (h % 2) * TG:(h % 2) * TG + TG]),
            sem=f'{pref}_S{h % 2}', n=16)
        st_after_head.append(p.val(f'{pref}_S{h % 2}'))
    return (p.val(f'{pref}_S0'), p.val(f'{pref}_S1'))


def load_resident(p, hbm, T, nk, dst, sem):
    v = p.val(sem)
    for k in range(nk):
        q = 'gpsimd' if k % 2 == 0 else 'sync'
        v = p.op(q, lambda e, k=k: e.dma_start(
            out=dst[:, k * T:(k + 1) * T], in_=hbm[k * 128:(k + 1) * 128, :]),
            sem=sem, n=16)
    return p.val(sem)


def emit_dense2(p, cx, pref, wsrcs, X_ap, Tch, nk, evac, extra_mm=None,
                bank0=0):
    nch = len(Tch)
    nslab = (nk + 3) // 4
    w32 = p.sbuf(f"{pref}_w32", [128, 2 * nk * 128], F32)
    wbf = p.sbuf(f"{pref}_wbf", [128, 2 * nk * 128], BF16)
    SCAST, SMMD, SEV = f'{pref}_cast', f'{pref}_mmd', f'{pref}_ev'
    sev_map = {}
    cast_vals = []
    mmd_vals = []
    for oi, wsrc in enumerate(wsrcs):
        par = oi % 2
        if oi >= 2:
            p.wait('sync', SCAST, cast_vals[oi - 2])
        for s in range(nslab):
            k0 = s * 4
            kn = min(4, nk - k0)
            p.op('sync', lambda e, k0=k0, kn=kn, par=par, wsrc=wsrc: e.dma_start(
                out=w32[:, par * nk * 128 + k0 * 128:par * nk * 128 + (k0 + kn) * 128]
                    .rearrange("p (k f) -> p k f", k=kn),
                in_=wsrc[k0 * 128:(k0 + kn) * 128, :]
                    .rearrange("(k p) f -> p k f", p=128)), sem=f'{pref}_W{par}', n=16)
        p.wait('scalar', f'{pref}_W{par}', p.val(f'{pref}_W{par}'))
        if oi >= 2:
            p.wait('scalar', SMMD, mmd_vals[oi - 2])
        cast_vals.append(p.op('scalar', lambda e, par=par: e.copy(
            wbf[:, par * nk * 128:(par + 1) * nk * 128],
            w32[:, par * nk * 128:(par + 1) * nk * 128]), sem=SCAST))
        p.wait('tensor', SCAST, cast_vals[oi])
        for ci, (t0, nt) in enumerate(Tch):
            if oi >= 1:
                p.wait('tensor', SEV, sev_map[(oi - 1, ci)])
            for k in range(nk):
                last = (k == nk - 1) and (extra_mm is None)
                p.op('tensor', lambda e, k=k, par=par, ci=ci, t0=t0, nt=nt,
                     last=last: e.matmul(
                         cx.ps[bank0 + ci][:, 0:nt],
                         wbf[:, par * nk * 128 + k * 128:par * nk * 128 + (k + 1) * 128],
                         X_ap(k, t0, nt), start=(k == 0), stop=last),
                     sem=(SMMD if last else None))
            if extra_mm is not None:
                extra_mm(p, oi, ci, cx.ps[bank0 + ci], t0, nt, SMMD)
            fv = evac(p, oi, cx.ps[bank0 + ci], t0, nt, ci)
            sev_map[(oi, ci)] = fv if fv else p.val(SEV)
        mmd_vals.append(p.val(SMMD))
    return p.val(SEV)


def emit_ffn(p, cx, pref, x1_hbm, w1_hbm, w3_hbm, w2_hbm, wg_hbm, sel_hbm,
             x2_hbm, stats_in=None, stats_out=None):
    TH = 1088
    lnx1 = p.sbuf(f'{pref}_lnx', [128, 16 * TH], BF16)
    combT = p.sbuf(f'{pref}_comb', [8, TH], F32)
    cbuf = p.sbuf(f'{pref}_cbuf', [128, TH], F32)
    g_sb = p.sbuf(f'{pref}_g', [128, 8 * TH], BF16)
    yacc = p.sbuf(f'{pref}_y', [128, 16 * TH], F32)
    sil = p.sbuf(f'{pref}_sil', [128, 512], F32)
    tmp = p.sbuf(f'{pref}_tmp', [128, 512], F32)
    selb = p.sbuf(f'{pref}_selb', [8, 8 * 128], F32)
    hw32 = p.sbuf(f'{pref}_hw32', [128, 4 * 512], F32)
    hwbf = p.sbuf(f'{pref}_hwbf', [128, 2 * 2 * 2048 + 2 * 1024], BF16)
    slab_i = [0]
    xr = p.sbuf(f'{pref}_xr', [128, 3 * 512], F32)
    x2o = p.sbuf(f'{pref}_x2o', [128, 3 * 512], F32)
    p.op('gpsimd', lambda e: e.dma_start(
        out=selb.rearrange("p (e f) -> p e f", e=8),
        in_=sel_hbm.rearrange("e p f -> p e f")), sem=f'{pref}_SEL', n=16)
    p.wait('tensor', f'{pref}_SEL', 16)
    SMMD, SEV, SAC, SCV, SCB = (f'{pref}_mmd', f'{pref}_ev', f'{pref}_ac',
                                f'{pref}_cv', f'{pref}_cbb')
    Tch = chunks(TH)
    nch = len(Tch)
    ost = {0: [], 1: [], 2: []}
    gi = [0]
    hev = {}
    wev = {}
    hc_vals = []
    for half in range(2):
        col0 = half * TH
        if half == 1:
            for s in (0, 1, 2):
                if ost[s]:
                    p.wait('gpsimd', f'{pref}_O{s}', ost[s][-1])
            p.barrier(('gpsimd', 'vector', 'scalar', 'tensor', 'sync'))
        if stats_in is not None:
            lnve = emit_ln_apply(p, cx, x1_hbm, TH, lnx1, f'{pref}n{half}',
                                 stats_in, TG, col0=col0)
        else:
            lnve = emit_ln(p, cx, x1_hbm, TH, lnx1, f'{pref}n{half}', col0=col0)
        for eng in ('tensor', 'vector', 'scalar'):
            p.wait(eng, f'{pref}n{half}_ve', lnve)
        cb = emit_gate(p, cx, f'{pref}g{half}', lnx1, TH, wg_hbm, combT,
                       l1_norm=False)
        for eng in ('tensor', 'vector'):
            p.wait(eng, f'{pref}g{half}_cb', cb)
        for e in range(9):
            if e < 8:
                for ci, (t0, nt) in enumerate(Tch):
                    vb = p.wait('tensor', SCV, p.val(SCV))
                    p.op('tensor', lambda e2, e=e, t0=t0, nt=nt: e2.matmul(
                        cx.ps[6][:, 0:nt], selb[0:8, e * 128:(e + 1) * 128],
                        combT[0:8, t0:t0 + nt], start=True, stop=True),
                        sem=f'{pref}_mb')
                    p.wait('vector', f'{pref}_mb', p.val(f'{pref}_mb'))
                    vv = p.op('vector', lambda e2, t0=t0, nt=nt: e2.tensor_copy(
                        cbuf[:, t0:t0 + nt], cx.ps[6][:, 0:nt]), sem=SCV)
                    p.wait('vector', SCV, vv)
            p.wait('tensor', f'{pref}_yev', p.val(f'{pref}_yev'))
            for ot in range(8):
                pi = (half * 9 + e) * 8 + ot
                par = pi % 2
                po = par * 2 * 2048
                first_cast = True
                for wi, wh in enumerate((w1_hbm, w3_hbm)):
                    for s in range(4):
                        sl = slab_i[0] % 4
                        if slab_i[0] >= 4:
                            p.wait('sync', f'{pref}_hcast', hc_vals[slab_i[0] - 4])
                        dv = p.op('sync', lambda e2, s=s, sl=sl, e=e, ot=ot, wh=wh:
                                  e2.dma_start(
                                      out=hw32[:, sl * 512:(sl + 1) * 512]
                                      .rearrange("p (k f) -> p k f", k=4),
                                      in_=wh[e, s * 512:(s + 1) * 512,
                                             ot * 128:(ot + 1) * 128]
                                      .rearrange("(k p) f -> p k f", p=128)),
                                  sem=f'{pref}_HS{sl}', n=16)
                        p.wait('scalar', f'{pref}_HS{sl}', dv)
                        if first_cast and pi >= 2:
                            p.wait('scalar', SMMD, p.val(SMMD) - nch)
                        first_cast = False
                        hc_vals.append(p.op('scalar', lambda e2, sl=sl, wi=wi, s=s,
                                            po=po: e2.copy(
                            hwbf[:, po + wi * 2048 + s * 512:
                                 po + wi * 2048 + (s + 1) * 512],
                            hw32[:, sl * 512:(sl + 1) * 512]), sem=f'{pref}_hcast'))
                        slab_i[0] += 1
                p.wait('tensor', f'{pref}_hcast', p.val(f'{pref}_hcast'))
                for ci, (t0, nt) in enumerate(Tch):
                    if (pi - 1, ci) in hev:
                        p.wait('tensor', SEV, hev[(pi - 1, ci)])
                    for k in range(16):
                        p.op('tensor', lambda e2, k=k, po=po, ci=ci, t0=t0, nt=nt:
                             e2.matmul(
                                 cx.ps[ci][:, 0:nt],
                                 hwbf[:, po + k * 128:po + (k + 1) * 128],
                                 lnx1[:, k * TH + t0:k * TH + t0 + nt],
                                 start=(k == 0), stop=(k == 15)))
                    for k in range(16):
                        p.op('tensor', lambda e2, k=k, po=po, ci=ci, t0=t0, nt=nt:
                             e2.matmul(
                                 cx.ps[3 + ci][:, 0:nt],
                                 hwbf[:, po + 2048 + k * 128:po + 2048 + (k + 1) * 128],
                                 lnx1[:, k * TH + t0:k * TH + t0 + nt],
                                 start=(k == 0), stop=(k == 15)),
                             sem=(SMMD if k == 15 else None))
                    mmv = p.val(SMMD)
                    p.wait('scalar', SMMD, mmv)
                    p.wait('scalar', SCV, p.val(SCV))
                    av = p.op('scalar', lambda e2, ci=ci, nt=nt: e2.activation(
                        sil[:, 0:nt], cx.ps[ci][:, 0:nt], AF.Silu), sem=SAC)
                    p.wait('vector', SAC, av)
                    p.wait('vector', SEV, p.val(SEV))
                    vv = p.op('vector', lambda e2, ci=ci, nt=nt: e2.tensor_tensor(
                        tmp[:, 0:nt], cx.ps[3 + ci][:, 0:nt],
                        sil[:, 0:nt], op=ALU.mult), sem=SCV)
                    p.wait('vector', SCV, vv)
                    if e < 8:
                        p.op('vector', lambda e2, ot=ot, t0=t0, nt=nt:
                             e2.tensor_tensor(
                                 g_sb[:, ot * TH + t0:ot * TH + t0 + nt],
                                 tmp[:, 0:nt], cbuf[:, t0:t0 + nt],
                                 op=ALU.mult), sem=SEV)
                    else:
                        p.op('vector', lambda e2, ot=ot, t0=t0, nt=nt:
                             e2.tensor_copy(
                                 g_sb[:, ot * TH + t0:ot * TH + t0 + nt],
                                 tmp[:, 0:nt]), sem=SEV)
                    hev[(pi, ci)] = p.val(SEV)
                    gi[0] += 1
            p.wait('tensor', SEV, p.val(SEV))
            for o2 in range(16):
                pi2 = (half * 9 + e) * 16 + o2
                par = pi2 % 2
                po = par * 1024
                first_cast = True
                for s in range(2):
                    sl = slab_i[0] % 4
                    if slab_i[0] >= 4:
                        p.wait('sync', f'{pref}_hcast', hc_vals[slab_i[0] - 4])
                    dv = p.op('sync', lambda e2, s=s, sl=sl, e=e, o2=o2: e2.dma_start(
                        out=hw32[:, sl * 512:(sl + 1) * 512]
                        .rearrange("p (k f) -> p k f", k=4),
                        in_=w2_hbm[e, s * 512:(s + 1) * 512, o2 * 128:(o2 + 1) * 128]
                        .rearrange("(k p) f -> p k f", p=128)),
                        sem=f'{pref}_HS{sl}', n=16)
                    p.wait('scalar', f'{pref}_HS{sl}', dv)
                    if first_cast and pi2 >= 2:
                        p.wait('scalar', f'{pref}_yev', p.val(f'{pref}_yev') - nch)
                    first_cast = False
                    hc_vals.append(p.op('scalar', lambda e2, sl=sl, s=s, po=po:
                                        e2.copy(
                        hwbf[:, 8192 + po + s * 512:8192 + po + (s + 1) * 512],
                        hw32[:, sl * 512:(sl + 1) * 512]), sem=f'{pref}_hcast'))
                    slab_i[0] += 1
                p.wait('tensor', f'{pref}_hcast', p.val(f'{pref}_hcast'))
                for ci, (t0, nt) in enumerate(Tch):
                    if (pi2 - 1, ci) in wev:
                        p.wait('tensor', f'{pref}_yev', wev[(pi2 - 1, ci)])
                    for k in range(8):
                        p.op('tensor', lambda e2, k=k, po=po, ci=ci, t0=t0, nt=nt:
                             e2.matmul(
                                 cx.ps[ci][:, 0:nt],
                                 hwbf[:, 8192 + po + k * 128:
                                      8192 + po + (k + 1) * 128],
                                 g_sb[:, k * TH + t0:k * TH + t0 + nt],
                                 start=(k == 0), stop=(k == 7)),
                             sem=(f'{pref}_wmm' if k == 7 else None))
                    p.wait('vector', f'{pref}_wmm', p.val(f'{pref}_wmm'))
                    first = (e == 0)
                    if first:
                        p.op('vector', lambda e2, o2=o2, t0=t0, nt=nt, ci=ci:
                             e2.tensor_copy(
                                 yacc[:, o2 * TH + t0:o2 * TH + t0 + nt],
                                 cx.ps[ci][:, 0:nt]), sem=f'{pref}_yev')
                        wev[(pi2, ci)] = p.val(f'{pref}_yev')
                    else:
                        p.op('vector', lambda e2, o2=o2, t0=t0, nt=nt, ci=ci:
                             e2.tensor_tensor(
                                 yacc[:, o2 * TH + t0:o2 * TH + t0 + nt],
                                 cx.ps[ci][:, 0:nt],
                                 yacc[:, o2 * TH + t0:o2 * TH + t0 + nt],
                                 op=ALU.add), sem=f'{pref}_yev')
                    wev[(pi2, ci)] = p.val(f'{pref}_yev')
        yevf = p.val(f'{pref}_yev')
        xb2 = p.sbuf('lnsc_xbf', [128, 2 * 512], BF16)
        qb2 = p.sbuf('lnsc_sqbf', [128, 2 * 512], BF16)
        srw = p.sbuf('lnsc_srows', [1, 2 * 512], F32)
        for ci, (t0, nt) in enumerate(Tch):
            for o in range(16):
                g2 = gi[0]
                sl = g2 % 3
                par2 = g2 % 3
                ps2 = g2 % 2
                if g2 >= 3:
                    p.wait('gpsimd', SEV, p.val(SEV) - 2)
                rv = p.op('gpsimd', lambda e2, o=o, t0=t0, nt=nt, sl=sl, col0=col0:
                          e2.dma_start(
                              out=xr[:, sl * 512:sl * 512 + nt],
                              in_=x1_hbm[o * 128:(o + 1) * 128,
                                         col0 + t0:col0 + t0 + nt]),
                          sem=f'{pref}_R{sl}', n=16)
                p.wait('vector', f'{pref}_R{sl}', rv)
                pi2e8 = (half * 9 + 8) * 16 + o
                p.wait('vector', f'{pref}_yev', wev[(pi2e8, ci)])
                if ost[par2]:
                    p.wait('vector', f'{pref}_O{par2}', ost[par2][-1])
                if stats_out is not None:
                    p.wait('vector', f'{pref}_sa2', p.val(f'{pref}_sa2'))
                av = p.op('vector', lambda e2, o=o, t0=t0, nt=nt, sl=sl, par2=par2:
                     e2.tensor_tensor(
                         x2o[:, par2 * 512:par2 * 512 + nt],
                         yacc[:, o * TH + t0:o * TH + t0 + nt],
                         xr[:, sl * 512:sl * 512 + nt], op=ALU.add), sem=SEV)
                p.wait('gpsimd', SEV, p.val(SEV))
                ov = p.op('gpsimd', lambda e2, o=o, t0=t0, nt=nt, par2=par2,
                          col0=col0: e2.dma_start(
                              out=x2_hbm[o * 128:(o + 1) * 128,
                                         col0 + t0:col0 + t0 + nt],
                              in_=x2o[:, par2 * 512:par2 * 512 + nt]),
                          sem=f'{pref}_O{par2}', n=16)
                ost[par2].append(ov)
                if stats_out is not None:
                    p.wait('vector', SEV, av)
                    p.wait('vector', f'{pref}_sm2', max(0, p.val(f'{pref}_sm2') - 1))
                    cv = p.op('vector', lambda e2, nt=nt, par2=par2, ps2=ps2:
                              e2.tensor_copy(
                        xb2[:, ps2 * 512:ps2 * 512 + nt],
                        x2o[:, par2 * 512:par2 * 512 + nt]), sem=f'{pref}_sc2')
                    p.wait('scalar', f'{pref}_sc2', cv)
                    p.wait('scalar', f'{pref}_sm2', max(0, p.val(f'{pref}_sm2') - 1))
                    aq = p.op('scalar', lambda e2, nt=nt, par2=par2, ps2=ps2:
                              e2.square(
                        qb2[:, ps2 * 512:ps2 * 512 + nt],
                        x2o[:, par2 * 512:par2 * 512 + nt]), sem=f'{pref}_sa2')
                    p.wait('tensor', f'{pref}_sc2', cv)
                    p.wait('tensor', f'{pref}_sa2', aq)
                    if o == 0:
                        p.wait('tensor', f'{pref}_sv2', p.val(f'{pref}_sv2'))
                        p.wait('tensor', f'{pref}_yev', wev[(pi2e8, ci)])
                    p.op('tensor', lambda e2, nt=nt, ps2=ps2, o=o: e2.matmul(
                        cx.ps[3][0:1, 0:nt], cx.ones_col[:],
                        xb2[:, ps2 * 512:ps2 * 512 + nt],
                        start=(o == 0), stop=(o == 15), skip_group_check=True))
                    p.op('tensor', lambda e2, nt=nt, ps2=ps2, o=o: e2.matmul(
                        cx.ps[4][0:1, 0:nt], cx.ones_col[:],
                        qb2[:, ps2 * 512:ps2 * 512 + nt],
                        start=(o == 0), stop=(o == 15), skip_group_check=True),
                        sem=f'{pref}_sm2')
                gi[0] += 1
            if stats_out is not None:
                p.wait('vector', f'{pref}_sm2', p.val(f'{pref}_sm2'))
                if ci > 0:
                    p.wait('vector', f'{pref}_sS', p.val(f'{pref}_sS'))
                p.op('vector', lambda e2, nt=nt: e2.tensor_copy(
                    srw[0:1, 0:nt], cx.ps[3][0:1, 0:nt]))
                sv2 = p.op('vector', lambda e2, nt=nt: e2.tensor_copy(
                    srw[0:1, 512:512 + nt], cx.ps[4][0:1, 0:nt]),
                    sem=f'{pref}_sv2')
                p.wait('gpsimd', f'{pref}_sv2', sv2)
                p.op('gpsimd', lambda e2, col0=col0, t0=t0, nt=nt: e2.dma_start(
                    out=stats_out[0:1, col0 + t0:col0 + t0 + nt],
                    in_=srw[0:1, 0:nt]), sem=f'{pref}_sS', n=16)
                p.op('gpsimd', lambda e2, col0=col0, t0=t0, nt=nt: e2.dma_start(
                    out=stats_out[1:2, col0 + t0:col0 + t0 + nt],
                    in_=srw[0:1, 512:512 + nt]), sem=f'{pref}_sS', n=16)
        if stats_out is not None:
            p.wait('gpsimd', f'{pref}_sS', p.val(f'{pref}_sS'))
            p.wait('tensor', f'{pref}_sv2', p.val(f'{pref}_sv2'))
    return ost[0][-1], ost[1][-1], ost[2][-1]


def emit_outproj_residual(p, cx, pref, attn_res, wT_hbm, x_hbm, xcol0, xout_hbm,
                          T, bias_sb=None, stats_hbm=None):
    Tch = chunks(T)
    nch = len(Tch)
    xr = p.sbuf(f'{pref}_xr', [128, 4 * 512], F32)
    x1o = p.sbuf(f'{pref}_x1o', [128, 4 * 512], F32)
    SEV, SMMD = f'{pref}_ev', f'{pref}_mmd'
    gi = [0]
    olast = {0: [], 1: [], 2: [], 3: []}
    sev_g = []
    st = stats_init(p, pref, T) if stats_hbm is not None else None

    def X_ap(k, t0, nt):
        return attn_res[:, k * T + t0:k * T + t0 + nt]

    def evac(p2, oi, bank, t0, nt, ci):
        g = gi[0]
        sl = g % 4
        par = g % 4
        if g >= 4:
            p.wait('gpsimd', SEV, sev_g[g - 4])
        v = p.op('gpsimd', lambda e, oi=oi, t0=t0, nt=nt, sl=sl: e.dma_start(
            out=xr[:, sl * 512:sl * 512 + nt],
            in_=x_hbm[oi * 128:(oi + 1) * 128, xcol0 + t0:xcol0 + t0 + nt]),
            sem=f'{pref}_R{sl}', n=16)
        p.wait('vector', f'{pref}_R{sl}', v)
        p.wait('vector', SMMD, p.val(SMMD))
        if olast[par]:
            p.wait('vector', f'{pref}_O{par}', olast[par][-1])
        if bias_sb is not None:
            vv = p.op('vector', lambda e, oi=oi, nt=nt, sl=sl, bank=bank: e.tensor_scalar(
                xr[:, sl * 512:sl * 512 + nt], xr[:, sl * 512:sl * 512 + nt],
                bias_sb[:, oi:oi + 1], None, op0=ALU.add), sem=f'{pref}_bs')
            p.wait('vector', f'{pref}_bs', vv)
        fv = p.op('vector', lambda e, nt=nt, sl=sl, par=par, bank=bank: e.tensor_tensor(
            x1o[:, par * 512:par * 512 + nt], bank[:, 0:nt],
            xr[:, sl * 512:sl * 512 + nt], op=ALU.add), sem=SEV)
        sev_g.append(p.val(SEV))
        p.wait('gpsimd', SEV, p.val(SEV))
        ov = p.op('gpsimd', lambda e, oi=oi, t0=t0, nt=nt, par=par: e.dma_start(
            out=xout_hbm[oi * 128:(oi + 1) * 128, t0:t0 + nt],
            in_=x1o[:, par * 512:par * 512 + nt]), sem=f'{pref}_O{par}', n=16)
        olast[par].append(ov)
        if st is not None:
            p.wait('vector', SEV, fv)
            stats_acc(p, cx, pref, st, x1o[:, par * 512:par * 512 + nt],
                      t0, nt, first=(oi == 0))
        gi[0] += 1
        return fv

    wsrcs = [wT_hbm[:, oi * 128:(oi + 1) * 128] for oi in range(16)]
    emit_dense2(p, cx, pref, wsrcs, X_ap, Tch, 16, evac)
    if st is not None:
        stats_store(p, pref, st, stats_hbm)
    for s in range(4):
        if olast[s]:
            p.wait('gpsimd', f'{pref}_O{s}', olast[s][-1])
    return (0, 0)


def emit_gate(p, cx, pref, lnx, T, wg_hbm, combT, l1_norm=False,
              bank_g=5, bank_t0=0, bank_t1=1):
    wg32 = p.sbuf('gtsc_wg32', [128, 128], F32)
    wgbf = p.sbuf('gtsc_wgbf', [128, 128], BF16)
    lg = p.sbuf('gtsc_lg', [8, 512], F32)
    tl = p.sbuf('gtsc_tl', [128, 2 * 8], F32)
    prob = p.sbuf('gtsc_prob', [128, 8], F32)
    t2 = p.sbuf('gtsc_t2', [128, 8], F32)
    cmp2 = p.sbuf('gtsc_cmp2', [128, 8], F32)
    cm = p.sbuf('gtsc_cm', [128, 8], F32)
    r1 = p.sbuf('gtsc_r1', [128, 8], F32)
    kk2 = p.sbuf('gtsc_k2', [128, 2], F32)
    negbig = kk2[:, 0:1]
    c16 = kk2[:, 1:2]
    SMS, SCB, SAC = f'{pref}_ms', f'{pref}_cb', f'{pref}_ac'
    v = p.op('vector', lambda e: e.memset(negbig[:, :], -1e30))
    v = p.op('vector', lambda e: e.memset(c16[:, :], 16.0), sem=SCB)
    p.wait('vector', SCB, v)
    wgv = p.op('sync', lambda e: e.dma_start(
        out=wg32.rearrange("p (k e) -> p k e", k=16),
        in_=wg_hbm.rearrange("(k p) e -> p k e", p=128)), sem=f'{pref}_Wg', n=16)
    p.wait('scalar', f'{pref}_Wg', wgv)
    cv = p.op('scalar', lambda e: e.copy(wgbf[:], wg32[:]), sem=f'{pref}_cast')
    p.wait('tensor', f'{pref}_cast', cv)

    def dve(f):
        vv = p.op('vector', f, sem=SCB)
        p.wait('vector', SCB, vv)

    for ci, (t0, nt) in enumerate(chunks(T)):
        p.wait('tensor', SCB, p.val(SCB))
        for k in range(16):
            p.op('tensor', lambda e, k=k, t0=t0, nt=nt: e.matmul(
                cx.ps[bank_g][0:8, 0:nt], wgbf[:, k * 8:(k + 1) * 8],
                lnx[:, k * T + t0:k * T + t0 + nt], start=(k == 0), stop=(k == 15)),
                sem=(SMS if k == 15 else None))
        p.wait('vector', SMS, p.val(SMS))
        dve(lambda e, nt=nt: e.tensor_copy(lg[:, 0:nt], cx.ps[bank_g][0:8, 0:nt]))
        for si in range((nt + 127) // 128):
            ns = min(128, nt - si * 128)
            par = si % 2
            bt = bank_t0 if par == 0 else bank_t1
            p.wait('tensor', SCB, p.val(SCB))
            p.op('tensor', lambda e, si=si, ns=ns, bt=bt: e.matmul(
                cx.ps[bt][0:ns, 0:8], lg[0:8, si * 128:si * 128 + ns],
                cx.identf[0:8, 0:8], start=True, stop=True, is_transpose=True),
                sem=SMS)
            p.wait('vector', SMS, p.val(SMS))
            dve(lambda e, ns=ns, par=par, bt=bt: e.tensor_copy(
                tl[0:ns, par * 8:par * 8 + 8], cx.ps[bt][0:ns, 0:8]))
            dve(lambda e, ns=ns, par=par: e.reduce_max(
                r1[0:ns, 1:2], tl[0:ns, par * 8:par * 8 + 8], axis=AX.X, negate=True))
            p.wait('scalar', SCB, p.val(SCB))
            av = p.op('scalar', lambda e, ns=ns, par=par: e.activation(
                t2[0:ns, 0:8], tl[0:ns, par * 8:par * 8 + 8], AF.Exp,
                bias=r1[0:ns, 1:2]), sem=SAC)
            p.wait('vector', SAC, av)
            dve(lambda e, ns=ns: e.reduce_sum(r1[0:ns, 2:3], t2[0:ns, 0:8], axis=AX.X))
            dve(lambda e, ns=ns: e.reciprocal(r1[0:ns, 3:4], r1[0:ns, 2:3]))
            dve(lambda e, ns=ns: e.tensor_scalar(
                prob[0:ns, :], t2[0:ns, 0:8], r1[0:ns, 3:4], None, op0=ALU.mult))
            dve(lambda e, ns=ns: e.reduce_max(r1[0:ns, 4:5], prob[0:ns, :], axis=AX.X))
            dve(lambda e, ns=ns: e.tensor_scalar(
                cmp2[0:ns, :], prob[0:ns, :], r1[0:ns, 4:5], None, op0=ALU.is_ge))
            dve(lambda e, ns=ns: e.scalar_tensor_tensor(
                t2[0:ns, 0:8], cmp2[0:ns, :], negbig[0:ns, :], prob[0:ns, :],
                op0=ALU.mult, op1=ALU.add))
            dve(lambda e, ns=ns: e.reduce_max(r1[0:ns, 5:6], t2[0:ns, 0:8], axis=AX.X))
            dve(lambda e, ns=ns: e.tensor_scalar(
                cmp2[0:ns, :], prob[0:ns, :], r1[0:ns, 5:6], None, op0=ALU.is_ge))
            dve(lambda e, ns=ns: e.tensor_tensor(
                cm[0:ns, :], prob[0:ns, :], cmp2[0:ns, :], op=ALU.mult))
            if l1_norm:
                dve(lambda e, ns=ns: e.tensor_tensor(
                    r1[0:ns, 6:7], r1[0:ns, 4:5], r1[0:ns, 5:6], op=ALU.add))
                dve(lambda e, ns=ns: e.reciprocal(r1[0:ns, 7:8], r1[0:ns, 6:7]))
                dve(lambda e, ns=ns: e.tensor_scalar(
                    cm[0:ns, :], cm[0:ns, :], r1[0:ns, 7:8], None, op0=ALU.mult))
                dve(lambda e, ns=ns: e.tensor_scalar(
                    cm[0:ns, :], cm[0:ns, :], c16[0:ns, :], None, op0=ALU.mult))
            p.wait('tensor', SCB, p.val(SCB))
            p.op('tensor', lambda e, ns=ns, bt=bt: e.matmul(
                cx.ps[bt][0:8, 8:8 + ns], cm[0:ns, 0:8],
                cx.identf[0:ns, 0:ns], start=True, stop=True, is_transpose=True),
                sem=SMS)
            p.wait('vector', SMS, p.val(SMS))
            dve(lambda e, si=si, t0=t0, ns=ns, bt=bt: e.tensor_copy(
                combT[0:8, t0 + si * 128:t0 + si * 128 + ns],
                cx.ps[bt][0:8, 8:8 + ns]))
    return p.val(SCB)


def emit_l1qkv(p, cx, pref, x2_hbm, wqkv_hbm, bqkv_hbm, ropeC_hbm, ropeS_hbm,
               perm_hbm, qkv_hbm, stats_in=None):
    lnx2 = p.sbuf(f'{pref}_lnx', [128, 16 * TG], BF16)
    bia = p.sbuf(f'{pref}_bia', [128, 48], F32)
    rC = p.sbuf(f'{pref}_rC', [128, TG], F32)
    rS = p.sbuf(f'{pref}_rS', [128, TG], F32)
    prm = p.sbuf(f'{pref}_prm', [128, 128], BF16)
    qpre = p.sbuf(f'{pref}_qpre', [128, 2 * 512], BF16)
    qscl1 = p.sbuf(f'{pref}_qscl1', [128, 1], F32)
    vq = p.op('vector', lambda e: e.memset(qscl1[:], 0.125), sem=f'{pref}_ev')
    p.wait('vector', f'{pref}_ev', vq)
    rt1 = p.sbuf(f'{pref}_rt1', [128, 512], F32)
    rt2 = p.sbuf(f'{pref}_rt2', [128, 512], F32)
    outb = p.sbuf(f'{pref}_outb', [128, 2 * 512], BF16)
    for nm, t, hb in (('bia', bia, None), ('rC', rC, ropeC_hbm),
                      ('rS', rS, ropeS_hbm), ('prm', prm, perm_hbm)):
        pass
    p.op('gpsimd', lambda e: e.dma_start(out=bia[:], in_=bqkv_hbm[:]),
         sem=f'{pref}_B', n=16)
    p.op('gpsimd', lambda e: e.dma_start(out=rC[:], in_=ropeC_hbm[:]),
         sem=f'{pref}_B', n=16)
    p.op('gpsimd', lambda e: e.dma_start(out=rS[:], in_=ropeS_hbm[:]),
         sem=f'{pref}_B', n=16)
    p.op('gpsimd', lambda e: e.dma_start(out=prm[:], in_=perm_hbm[:]),
         sem=f'{pref}_B', n=16)
    bv = p.val(f'{pref}_B')
    if stats_in is not None:
        lnve = emit_ln_apply(p, cx, x2_hbm, TG, lnx2, f'{pref}n', stats_in, TG)
    else:
        lnve = emit_ln(p, cx, x2_hbm, TG, lnx2, f'{pref}n')
    for eng in ('tensor', 'vector', 'scalar'):
        p.wait(eng, f'{pref}n_ve', lnve)
    p.wait('vector', f'{pref}_B', bv)
    p.wait('tensor', f'{pref}_B', bv)
    Tch = chunks(TG)
    nch = len(Tch)
    SEV, SMMD = f'{pref}_ev', f'{pref}_mmd'
    ost = {0: [], 1: []}
    gq = [0]

    def X_ap(k, t0, nt):
        return lnx2[:, k * TG + t0:k * TG + t0 + nt]

    def evac(p2, oi, bank, t0, nt, ci):
        g = gq[0]
        par = g % 2
        rb = 5 + (g % 2)
        p.wait('vector', SMMD, p.val(SMMD))
        if ost[par]:
            p.wait('vector', f'{pref}_O{par}', ost[par][-1])
        if oi < 16:
            vv = p.op('vector', lambda e, oi=oi, nt=nt, par=par, bank=bank:
                      e.tensor_scalar(qpre[:, par * 512:par * 512 + nt],
                                      bank[:, 0:nt], bia[:, oi:oi + 1],
                                      qscl1[:, 0:1], op0=ALU.add, op1=ALU.mult),
                      sem=SEV)
        else:
            vv = p.op('vector', lambda e, oi=oi, nt=nt, par=par, bank=bank:
                      e.tensor_scalar(qpre[:, par * 512:par * 512 + nt],
                                      bank[:, 0:nt], bia[:, oi:oi + 1], None,
                                      op0=ALU.add), sem=SEV)
        p.wait('vector', SEV, vv)
        freev = vv
        if oi < 32:
            p.wait('tensor', SEV, vv)
            mv = p.op('tensor', lambda e, nt=nt, par=par, rb=rb: e.matmul(
                cx.ps[rb][:, 0:nt], prm[:], qpre[:, par * 512:par * 512 + nt],
                start=True, stop=True), sem=f'{pref}_pm')
            vv = p.op('vector', lambda e, t0=t0, nt=nt, par=par: e.tensor_tensor(
                rt1[:, 0:nt], qpre[:, par * 512:par * 512 + nt], rC[:, t0:t0 + nt],
                op=ALU.mult), sem=SEV)
            p.wait('vector', f'{pref}_pm', mv)
            p.wait('vector', SEV, vv)
            vv = p.op('vector', lambda e, t0=t0, nt=nt, rb=rb: e.tensor_tensor(
                rt2[:, 0:nt], cx.ps[rb][:, 0:nt], rS[:, t0:t0 + nt],
                op=ALU.mult), sem=SEV)
            p.wait('vector', SEV, vv)
            vv = p.op('vector', lambda e, nt=nt, par=par: e.tensor_tensor(
                outb[:, par * 512:par * 512 + nt], rt1[:, 0:nt], rt2[:, 0:nt],
                op=ALU.add), sem=SEV)
            p.wait('gpsimd', SEV, vv)
            src = outb
        else:
            p.wait('gpsimd', SEV, vv)
            src = qpre
        ov = p.op('gpsimd', lambda e, oi=oi, t0=t0, nt=nt, par=par, src=src:
                  e.dma_start(out=qkv_hbm[oi * 128:(oi + 1) * 128, t0:t0 + nt],
                              in_=src[:, par * 512:par * 512 + nt]),
                  sem=f'{pref}_O{par}', n=16)
        ost[par].append(ov)
        gq[0] += 1
        return freev

    wsrcs = [wqkv_hbm[:, oi * 128:(oi + 1) * 128] for oi in range(48)]
    emit_dense2(p, cx, pref, wsrcs, X_ap, Tch, 16, evac)
    return ost[0][-1], ost[1][-1]


def emit_l1attn(p, cx, pref, qkv_hbm, masks_hbm, sinksT_hbm, sinksaug_hbm,
                attn1_hbm):
    msk = p.sbuf(f'{pref}_msk', [128, 7 * 512], BF16)
    q2 = p.sbuf(f'{pref}_q2', [128, 2 * TG], BF16)
    k2 = p.sbuf(f'{pref}_k2', [128, 2 * TG], BF16)
    v2 = p.sbuf(f'{pref}_v2', [128, 2 * TG], BF16)
    vtm2 = p.sbuf(f'{pref}_vtm2', [128, 34 * 66], BF16)
    skT32 = p.sbuf(f'{pref}_skT32', [128, 512], F32)
    skT = p.sbuf(f'{pref}_skT', [128, 512], BF16)
    ska32 = p.sbuf(f'{pref}_ska32', [16, 32 * 65], F32)
    skab = p.sbuf(f'{pref}_skab', [48, 16 * 65], BF16)
    psnk = p.sbuf(f'{pref}_psnk', [48, 2 * 512], BF16)
    sexp = p.sbuf(f'{pref}_sexp', [128, 4 * 512], F32)
    pT = p.sbuf(f'{pref}_pT', [128, 4 * 512], BF16)
    rr2 = p.sbuf(f'{pref}_rr2', [1, 2 * 512], F32)
    lnt5 = p.sbuf(f'{pref}_lnt5', [128, 512], F32)
    rb = p.sbuf(f'{pref}_rb', [128, 512], F32)
    att1 = p.sbuf(f'{pref}_att1', [128, 2 * 2048], BF16)
    SMS, SCM, SAE, SMP, SCR, SMB = (f'{pref}_ms', f'{pref}_cm', f'{pref}_ae',
                                    f'{pref}_mp', f'{pref}_cr', f'{pref}_mb')
    p.op('gpsimd', lambda e: e.dma_start(
        out=msk.rearrange("p (s f) -> p s f", s=7),
        in_=masks_hbm.rearrange("s p f -> p s f")), sem=f'{pref}_M', n=16)
    p.op('gpsimd', lambda e: e.dma_start(
        out=skT32[0:64, :].rearrange("p (h s) -> p h s", h=32),
        in_=sinksT_hbm.rearrange("h p s -> p h s")), sem=f'{pref}_M', n=16)
    p.op('gpsimd', lambda e: e.dma_start(
        out=skT32[64:128, :].rearrange("p (h s) -> p h s", h=32),
        in_=sinksT_hbm.rearrange("h p s -> p h s")), sem=f'{pref}_M', n=16)
    p.op('gpsimd', lambda e: e.dma_start(
        out=ska32.rearrange("p (h f) -> p h f", h=32),
        in_=sinksaug_hbm.rearrange("h p f -> p h f")), sem=f'{pref}_M', n=16)
    mv = p.val(f'{pref}_M')
    p.wait('vector', f'{pref}_M', mv)
    v = p.op('vector', lambda e: e.memset(vtm2[:], 1.0), sem=SCM)
    v = p.op('vector', lambda e: e.tensor_copy(skT[:], skT32[:]), sem=SCM)
    for j in range(16):
        for b in range(2):
            v = p.op('vector', lambda e, j=j, b=b: e.tensor_copy(
                skab[b * 32:b * 32 + 16, j * 65:(j + 1) * 65],
                ska32[0:16, (2 * j + b) * 65:(2 * j + b + 1) * 65]), sem=SCM)
    p.wait('vector', SCM, v)
    p.wait('tensor', SCM, v)
    st_pairs = {0: [], 1: []}
    pe_done = []
    kki = 0
    sms_s = {}
    scm_done = []
    sae_done = []
    smp_done = []
    sae_ln = None
    smp_pv = None
    evq_last = None
    for hp in range(16):
        par = hp % 2
        if hp >= 2:
            p.wait('gpsimd', SMP, pe_done[hp - 2])
        lv = 0
        for nm, dst, r0 in (('q', q2, 0), ('k', k2, 2048), ('v', v2, 4096)):
            lv = p.op('gpsimd', lambda e, dst=dst, r0=r0, par=par, hp=hp:
                      e.dma_start(out=dst[:, par * TG:(par + 1) * TG],
                                  in_=qkv_hbm[r0 + hp * 128:r0 + (hp + 1) * 128, :]),
                      sem=f'{pref}_L{par}', n=16)
        p.wait('tensor', f'{pref}_L{par}', lv)
        scm0 = p.val(SCM)
        for kt in range(17):
            bk = kt % 2
            if kt >= 2:
                p.wait('tensor', SCM, scm0 + 2 * (kt - 2) + 2)
            p.op('tensor', lambda e, kt=kt, bk=bk, par=par: e.matmul(
                cx.ps[bk].bitcast(BF16)[:, 0:128],
                v2[:, par * TG + kt * 128:par * TG + (kt + 1) * 128],
                cx.ident[:], start=True, stop=True, is_transpose=True), sem=SMS)
            p.wait('vector', SMS, p.val(SMS))
            for hi in range(2):
                p.op('vector', lambda e, kt=kt, hi=hi, bk=bk: e.tensor_copy(
                    vtm2[:, (kt * 2 + hi) * 66:(kt * 2 + hi) * 66 + 64].bitcast(F32),
                    cx.ps[bk].bitcast(BF16)[:, hi * 64:(hi + 1) * 64].bitcast(F32)),
                    sem=SCM)
        vtm_v = p.val(SCM)
        p.wait('tensor', SCM, vtm_v)
        h0 = hp * 2
        for qc in range(4):
            qq = 64 + 512 * qc
            nq = 512
            qbase = kki

            def emit_spair(tstep, kk, par=par, qq=qq, nq=nq):
                kp0 = (tstep % 2) * 2
                kb = 512 * qc + kk * 128
                if tstep >= 2:
                    p.wait('tensor', SCM, scm_done[tstep - 2])
                p.op('tensor', lambda e, kb=kb, qq=qq, nq=nq, kp0=kp0, par=par:
                     e.matmul(cx.ps[kp0][:, 0:nq],
                              k2[0:64, par * TG + kb:par * TG + kb + 128],
                              q2[0:64, par * TG + qq:par * TG + qq + nq],
                              start=True, stop=True))
                v = p.op('tensor', lambda e, kb=kb, qq=qq, nq=nq, kp0=kp0,
                         par=par:
                         e.matmul(cx.ps[kp0 + 1][:, 0:nq],
                                  k2[64:128, par * TG + kb:par * TG + kb + 128],
                                  q2[64:128, par * TG + qq:par * TG + qq + nq],
                                  start=True, stop=True), sem=SMS)
                sms_s[tstep] = v

            emit_spair(qbase, 0)
            emit_spair(qbase + 1, 1)
            if sae_ln is not None:
                p.wait('tensor', SAE, sae_ln)
            p.op('tensor', lambda e, h0=h0, qq=qq, nq=nq, par=par: e.matmul(
                cx.ps[6][0:16, 0:nq],
                skT[0:64, h0 * 16:(h0 + 1) * 16],
                q2[0:64, par * TG + qq:par * TG + qq + nq],
                start=True, stop=True))
            p.op('tensor', lambda e, h0=h0, qq=qq, nq=nq, par=par: e.matmul(
                cx.ps[6][32:48, 0:nq],
                skT[64:128, (h0 + 1) * 16:(h0 + 2) * 16],
                q2[64:128, par * TG + qq:par * TG + qq + nq],
                start=True, stop=True), sem=SMS)
            p.wait('scalar', SMS, p.val(SMS))
            if smp_pv is not None:
                p.wait('scalar', SMP, smp_pv)
            av = p.op('scalar', lambda e, nq=nq, par=par: e.activation(
                psnk[0:48, par * 512:par * 512 + nq], cx.ps[6][0:48, 0:nq],
                AF.Exp, bias=cx.expbias[0:48, 0:1]), sem=SAE)
            p.wait('tensor', SAE, av)
            if evq_last is not None:
                p.wait('tensor', f'{pref}_evq', evq_last)
            p.op('tensor', lambda e, hp=hp, nq=nq, par=par: e.matmul(
                cx.ps[4][0:65, 0:nq], skab[0:16, hp * 65:(hp + 1) * 65],
                psnk[0:16, par * 512:par * 512 + nq], start=True, stop=False))
            p.op('tensor', lambda e, hp=hp, nq=nq, par=par: e.matmul(
                cx.ps[5][0:65, 0:nq], skab[32:48, hp * 65:(hp + 1) * 65],
                psnk[32:48, par * 512:par * 512 + nq], start=True, stop=False),
                sem=SMP)
            smp_pv = p.val(SMP)
            for kk in range(5):
                tstep = qbase + kk
                kp0 = (tstep % 2) * 2
                slot = l1_mask_slot(qc, kk)
                p.wait('vector', SMS, sms_s[tstep])
                if tstep >= 2:
                    p.wait('vector', SAE, sae_done[tstep - 2])
                p.op('vector', lambda e, kp0=kp0, slot=slot, nq=nq: e.tensor_tensor(
                    sexp[:, kp0 * 512:kp0 * 512 + nq], cx.ps[kp0][:, 0:nq],
                    msk[:, slot * 512:slot * 512 + nq], op=ALU.add))
                p.op('vector', lambda e, kp0=kp0, slot=slot, nq=nq: e.tensor_tensor(
                    sexp[:, (kp0 + 1) * 512:(kp0 + 1) * 512 + nq],
                    cx.ps[kp0 + 1][:, 0:nq],
                    msk[:, slot * 512:slot * 512 + nq], op=ALU.add), sem=SCM)
                scm_done.append(p.val(SCM))
                p.wait('scalar', SCM, scm_done[-1])
                if tstep >= 2:
                    p.wait('scalar', SMP, smp_done[tstep - 2])
                p.op('scalar', lambda e, kp0=kp0, nq=nq: e.activation(
                    pT[:, kp0 * 512:kp0 * 512 + 2 * nq],
                    sexp[:, kp0 * 512:kp0 * 512 + 2 * nq], AF.Exp), sem=SAE)
                sae_done.append(p.val(SAE))
                if kk < 3:
                    emit_spair(tstep + 2, kk + 2)
                p.wait('tensor', SAE, sae_done[tstep])
                p.op('tensor', lambda e, kt4=4 * qc + kk, kp0=kp0, nq=nq, kk=kk:
                     e.matmul(cx.ps[4][0:65, 0:nq],
                              vtm2[:, (kt4 * 2) * 66:(kt4 * 2) * 66 + 65],
                              pT[:, kp0 * 512:kp0 * 512 + nq],
                              start=False, stop=(kk == 4)))
                p.op('tensor', lambda e, kt4=4 * qc + kk, kp0=kp0, nq=nq, kk=kk:
                     e.matmul(cx.ps[5][0:65, 0:nq],
                              vtm2[:, (kt4 * 2 + 1) * 66:(kt4 * 2 + 1) * 66 + 65],
                              pT[:, (kp0 + 1) * 512:(kp0 + 1) * 512 + nq],
                              start=False, stop=(kk == 4)), sem=SMP)
                smp_done.append(p.val(SMP))
            kki = qbase + 5
            p.wait('vector', SMP, p.val(SMP))
            p.op('vector', lambda e, nq=nq: e.tensor_copy(
                rr2[0:1, 0:nq], cx.ps[4][64:65, 0:nq]))
            vv = p.op('vector', lambda e, nq=nq: e.tensor_copy(
                rr2[0:1, 512:512 + nq], cx.ps[5][64:65, 0:nq]), sem=SCR)
            p.wait('tensor', SCR, vv)
            p.op('tensor', lambda e, nq=nq: e.matmul(
                cx.ps[6][0:64, 0:nq], cx.ones_row[0:1, 0:64], rr2[0:1, 0:nq],
                start=True, stop=True))
            p.op('tensor', lambda e, nq=nq: e.matmul(
                cx.ps[6][64:128, 0:nq], cx.ones_row[0:1, 0:64],
                rr2[0:1, 512:512 + nq], start=True, stop=True), sem=SMB)
            p.wait('scalar', SMB, p.val(SMB))
            av = p.op('scalar', lambda e, nq=nq: e.activation(
                lnt5[:, 0:nq], cx.ps[6][:, 0:nq], AF.Ln), sem=SAE)
            sae_ln = av
            p.wait('scalar', SAE, av)
            av = p.op('scalar', lambda e, nq=nq: e.activation(
                rb[:, 0:nq], lnt5[:, 0:nq], AF.Exp,
                scale=cx.neg1[:, 0:1]), sem=SAE)
            p.wait('vector', SAE, av)
            if hp >= 2 and st_pairs[par]:
                p.wait('vector', f'{pref}_S{par}', st_pairs[par][-1])
            p.op('vector', lambda e, qq=qq, nq=nq, par=par: e.tensor_tensor(
                att1[0:64, par * 2048 + qq - 64:par * 2048 + qq - 64 + nq],
                cx.ps[4][0:64, 0:nq], rb[0:64, 0:nq], op=ALU.mult))
            evv = p.op('vector', lambda e, qq=qq, nq=nq, par=par: e.tensor_tensor(
                att1[64:128, par * 2048 + qq - 64:par * 2048 + qq - 64 + nq],
                cx.ps[5][0:64, 0:nq], rb[64:128, 0:nq], op=ALU.mult),
                sem=f'{pref}_evq')
            evq_last = evv
        pe_done.append(p.val(SMP))
        p.wait('gpsimd', f'{pref}_evq', p.val(f'{pref}_evq'))
        sv = p.op('gpsimd', lambda e, hp=hp, par=par: e.dma_start(
            out=attn1_hbm[hp * 128:(hp + 1) * 128, :],
            in_=att1[:, par * 2048:(par + 1) * 2048]), sem=f'{pref}_S{par}', n=16)
        st_pairs[par].append(sv)
    return st_pairs[0][-1], st_pairs[1][-1]


def emit_l1ffn(p, cx, pref, x3_hbm, wd_hbm, wg_hbm, Apack_hbm, Bpack_hbm,
               sel16_hbm, out_hbm, stats_in=None):
    TO = 2048
    lnx3 = p.sbuf(f'{pref}_lnx', [128, 16 * TO], BF16)
    combT = p.sbuf(f'{pref}_comb', [8, TO], F32)
    ts = p.sbuf(f'{pref}_ts', [128, TO], BF16)
    cwe = p.sbuf(f'{pref}_cwe', [128, 512], F32)
    aw32 = p.sbuf(f'{pref}_aw32', [128, 2048], F32)
    awbf = p.sbuf(f'{pref}_awbf', [128, 2048], BF16)
    bp32 = p.sbuf(f'{pref}_bp32', [128, 2048], F32)
    bpbf = p.sbuf(f'{pref}_bpbf', [128, 2048], BF16)
    s16 = p.sbuf(f'{pref}_s16', [8, 128], F32)
    SEV, SMMD = f'{pref}_ev', f'{pref}_mmd'
    p.op('sync', lambda e: e.dma_start(
        out=aw32.rearrange("p (k f) -> p k f", k=16),
        in_=Apack_hbm.rearrange("(k p) f -> p k f", p=128)), sem=f'{pref}_A', n=16)
    p.op('sync', lambda e: e.dma_start(out=bp32[:], in_=Bpack_hbm[:]),
         sem=f'{pref}_A', n=16)
    p.op('sync', lambda e: e.dma_start(out=s16[:], in_=sel16_hbm[:]),
         sem=f'{pref}_A', n=16)
    av = p.val(f'{pref}_A')
    p.wait('scalar', f'{pref}_A', av)
    cv = p.op('scalar', lambda e: e.copy(awbf[:], aw32[:]), sem=f'{pref}_acast')
    cv = p.op('scalar', lambda e: e.copy(bpbf[:], bp32[:]), sem=f'{pref}_acast')
    if stats_in is not None:
        lnve = emit_ln_apply(p, cx, x3_hbm, TO, lnx3, f'{pref}n', stats_in, TO)
    else:
        lnve = emit_ln(p, cx, x3_hbm, TO, lnx3, f'{pref}n')
    for eng in ('tensor', 'vector', 'scalar'):
        p.wait(eng, f'{pref}n_ve', lnve)
    cb = emit_gate(p, cx, f'{pref}g', lnx3, TO, wg_hbm, combT, l1_norm=True)
    for eng in ('tensor', 'vector'):
        p.wait(eng, f'{pref}g_cb', cb)
    p.wait('tensor', f'{pref}_acast', cv)
    Tch = chunks(TO)
    nch = len(Tch)
    for ci, (t0, nt) in enumerate(Tch):
        bk = ci % 2
        p.wait('tensor', f'{pref}_tsv', p.val(f'{pref}_tsv'))
        for k in range(16):
            p.op('tensor', lambda e, k=k, t0=t0, nt=nt, bk=bk: e.matmul(
                cx.ps[bk][:, 0:nt], awbf[:, k * 128:(k + 1) * 128],
                lnx3[:, k * TO + t0:k * TO + t0 + nt],
                start=(k == 0), stop=(k == 15)),
                sem=(f'{pref}_tmm' if k == 15 else None))
        p.op('tensor', lambda e, t0=t0, nt=nt: e.matmul(
            cx.ps[5][:, 0:nt], s16[0:8, :], combT[0:8, t0:t0 + nt],
            start=True, stop=True), sem=f'{pref}_tmm')
        p.wait('vector', f'{pref}_tmm', p.val(f'{pref}_tmm'))
        vv = p.op('vector', lambda e, nt=nt: e.tensor_copy(
            cwe[:, 0:nt], cx.ps[5][:, 0:nt]), sem=f'{pref}_tsv')
        p.wait('vector', f'{pref}_tsv', vv)
        p.op('vector', lambda e, t0=t0, nt=nt, bk=bk: e.tensor_tensor(
            ts[:, t0:t0 + nt], cx.ps[bk][:, 0:nt], cwe[:, 0:nt], op=ALU.mult),
            sem=f'{pref}_tsv')
    tsv = p.val(f'{pref}_tsv')
    p.wait('tensor', f'{pref}_tsv', tsv)

    xr = p.sbuf(f'{pref}_xr', [128, 4 * 512], F32)
    xo = p.sbuf(f'{pref}_xo', [128, 4 * 512], F32)
    ost = {0: [], 1: [], 2: [], 3: []}
    gi = [0]
    sev_g7 = []

    def X_ap(k, t0, nt):
        return lnx3[:, k * TO + t0:k * TO + t0 + nt]

    def extra(p2, oi, ci, bank, t0, nt, semname):
        p.op('tensor', lambda e, oi=oi, t0=t0, nt=nt, bank=bank: e.matmul(
            bank[:, 0:nt], bpbf[:, oi * 128:(oi + 1) * 128], ts[:, t0:t0 + nt],
            start=False, stop=True), sem=semname)

    def evac(p2, oi, bank, t0, nt, ci):
        g = gi[0]
        sl = g % 4
        par = g % 4
        if g >= 4:
            p.wait('gpsimd', SEV, sev_g7[g - 4])
        rv = p.op('gpsimd', lambda e, oi=oi, t0=t0, nt=nt, sl=sl: e.dma_start(
            out=xr[:, sl * 512:sl * 512 + nt],
            in_=x3_hbm[oi * 128:(oi + 1) * 128, t0:t0 + nt]), sem=f'{pref}_R{sl}', n=16)
        p.wait('vector', f'{pref}_R{sl}', rv)
        p.wait('vector', SMMD, p.val(SMMD))
        if ost[par]:
            p.wait('vector', f'{pref}_O{par}', ost[par][-1])
        fv = p.op('vector', lambda e, nt=nt, sl=sl, par=par, bank=bank: e.tensor_tensor(
            xo[:, par * 512:par * 512 + nt], bank[:, 0:nt],
            xr[:, sl * 512:sl * 512 + nt], op=ALU.add), sem=SEV)
        sev_g7.append(p.val(SEV))
        p.wait('gpsimd', SEV, p.val(SEV))
        ov = p.op('gpsimd', lambda e, oi=oi, t0=t0, nt=nt, par=par: e.dma_start(
            out=out_hbm[oi * 128:(oi + 1) * 128, t0:t0 + nt],
            in_=xo[:, par * 512:par * 512 + nt]), sem=f'{pref}_O{par}', n=16)
        ost[par].append(ov)
        gi[0] += 1
        return fv

    wsrcs = [wd_hbm[:, oi * 128:(oi + 1) * 128] for oi in range(16)]
    emit_dense2(p, cx, pref, wsrcs, X_ap, Tch, 16, evac, extra_mm=extra)
    for s in range(4):
        if ost[s]:
            p.wait('gpsimd', f'{pref}_O{s}', ost[s][-1])
    return (0, 0)


def stats_init(p, pref, T):
    st = {}
    st['acc'] = p.sbuf(f'{pref}_stacc', [1, 2 * 2176], F32)
    st['xb'] = p.sbuf('lnsc_xbf', [128, 2 * 512], BF16)
    st['qb'] = p.sbuf('lnsc_sqbf', [128, 2 * 512], BF16)
    st['T'] = T
    st['n'] = 0
    return st


def stats_acc(p, cx, pref, st, tile_ap, t0, nt, first):
    SC, SA, SM, SP = (f'{pref}_stc', f'{pref}_sta', f'{pref}_stm', f'{pref}_stp')
    v = p.op('vector', lambda e: e.tensor_copy(st['xb'][:, 0:nt], tile_ap), sem=SC)
    p.wait('scalar', SC, v)
    a = p.op('scalar', lambda e: e.square(st['qb'][:, 0:nt], tile_ap), sem=SA)
    p.wait('tensor', SC, p.val(SC))
    p.wait('tensor', SA, p.val(SA))
    p.wait('tensor', SP, p.val(SP))
    p.op('tensor', lambda e: e.matmul(cx.ps[5][0:1, 0:nt], cx.ones_col[:],
                                      st['xb'][:, 0:nt], start=True, stop=True))
    m = p.op('tensor', lambda e: e.matmul(cx.ps[6][0:1, 0:nt], cx.ones_col[:],
                                          st['qb'][:, 0:nt], start=True, stop=True),
             sem=SM)
    p.wait('vector', SM, m)
    T = st['T']
    if first:
        p.op('vector', lambda e: e.tensor_copy(
            st['acc'][0:1, t0:t0 + nt], cx.ps[5][0:1, 0:nt]))
        p.op('vector', lambda e: e.tensor_copy(
            st['acc'][0:1, T + t0:T + t0 + nt], cx.ps[6][0:1, 0:nt]), sem=SP)
    else:
        p.op('vector', lambda e: e.tensor_tensor(
            st['acc'][0:1, t0:t0 + nt], cx.ps[5][0:1, 0:nt],
            st['acc'][0:1, t0:t0 + nt], op=ALU.add))
        p.op('vector', lambda e: e.tensor_tensor(
            st['acc'][0:1, T + t0:T + t0 + nt], cx.ps[6][0:1, 0:nt],
            st['acc'][0:1, T + t0:T + t0 + nt], op=ALU.add), sem=SP)


def stats_store(p, pref, st, stats_hbm):
    T = st['T']
    p.wait('gpsimd', f'{pref}_stp', p.val(f'{pref}_stp'))
    p.op('gpsimd', lambda e: e.dma_start(out=stats_hbm[0:1, :],
                                         in_=st['acc'][0:1, 0:T]),
         sem=f'{pref}_stS', n=16)
    v = p.op('gpsimd', lambda e: e.dma_start(out=stats_hbm[1:2, :],
                                             in_=st['acc'][0:1, T:2 * T]),
         sem=f'{pref}_stS', n=16)
    p.wait('gpsimd', f'{pref}_stS', v)
    return v


def emit_ln_apply(p, cx, src_hbm, T, lnx, pref, stats_hbm, Tst, col0=0,
                  bank0=0):
    nk = 16
    ps_mb = cx.ps[bank0 + 2]
    ps_rb = cx.ps[bank0 + 3]
    xin = p.sbuf("lnsc_xin", [128, 5 * 512], F32)
    rows = p.sbuf("lnsc_rows", [1, 2 * 512], F32)
    srows = p.sbuf("lnsc_srows", [1, 2 * 512], F32)
    mrow = p.sbuf("lnsc_mrow", [128, 512], F32)
    rrow = p.sbuf("lnsc_rrow", [128, 512], F32)
    inv = p.const(1.0 / 2048.0)
    eps = p.const(EPS)
    neghalf = p.const(-0.5)
    SA, SC, SM, SV, SP, SL = (f'{pref}_ac', f'{pref}_cv', f'{pref}_mm',
                              f'{pref}_ve', f'{pref}_pl', f'{pref}_sl')
    ve_done = []
    pl_done = 0
    for ci, (t0, nt) in enumerate(chunks(T)):
        c0 = col0 + t0
        p.wait('gpsimd', SC, p.val(SC))
        p.op('gpsimd', lambda e, c0=c0, nt=nt: e.dma_start(
            out=srows[0:1, 0:nt], in_=stats_hbm[0:1, c0:c0 + nt]), sem=SL, n=16)
        lv = p.op('gpsimd', lambda e, c0=c0, nt=nt: e.dma_start(
            out=srows[0:1, 512:512 + nt], in_=stats_hbm[1:2, c0:c0 + nt]),
            sem=SL, n=16)
        p.wait('vector', SL, lv)
        v = p.op('vector', lambda e, nt=nt: e.tensor_scalar(
            rows[0:1, 0:nt], srows[0:1, 0:nt], inv, None, op0=ALU.mult),
            sem=SC)
        v = p.op('vector', lambda e, nt=nt: e.tensor_scalar(
            rows[0:1, 512:512 + nt], srows[0:1, 512:512 + nt], inv,
            None, op0=ALU.mult), sem=SC)
        p.wait('vector', SC, v)
        v = p.op('vector', lambda e, nt=nt: e.tensor_tensor(
            mrow[0:1, 0:nt], rows[0:1, 0:nt], rows[0:1, 0:nt],
            op=ALU.mult), sem=SC)
        p.wait('vector', SC, v)
        v = p.op('vector', lambda e, nt=nt: e.tensor_tensor(
            rows[0:1, 512:512 + nt], rows[0:1, 512:512 + nt],
            mrow[0:1, 0:nt], op=ALU.subtract), sem=SC)
        p.wait('scalar', SC, p.val(SC))
        sa1 = p.op('scalar', lambda e, nt=nt: e.activation(
            mrow[0:1, 0:nt], rows[0:1, 512:512 + nt], AF.Ln,
            bias=eps), sem=SA)
        p.wait('scalar', SA, sa1)
        p.op('scalar', lambda e, nt=nt: e.activation(
            rows[0:1, 512:512 + nt], mrow[0:1, 0:nt], AF.Exp,
            scale=neghalf), sem=SA)
        p.wait('tensor', SC, p.val(SC))
        p.wait('tensor', SA, p.val(SA))
        p.wait('tensor', SP, p.val(SP))
        p.wait('tensor', SV, ve_done[-1] if ve_done else 0)
        p.op('tensor', lambda e, nt=nt: e.matmul(
            ps_mb[:, 0:nt], cx.ones_row[:, :], rows[0:1, 0:nt], start=True, stop=True))
        p.op('tensor', lambda e, nt=nt: e.matmul(
            ps_rb[:, 0:nt], cx.ones_row[:, :], rows[0:1, 512:512 + nt],
            start=True, stop=True), sem=SM)
        p.wait('vector', SM, p.val(SM))
        p.wait('vector', SP, pl_done)
        p.op('vector', lambda e, nt=nt: e.tensor_copy(mrow[:, 0:nt], ps_mb[:, 0:nt]))
        v = p.op('vector', lambda e, nt=nt: e.tensor_copy(rrow[:, 0:nt], ps_rb[:, 0:nt]),
                 sem=SC)
        p.wait('vector', SC, v)
        sv0 = p.val(SV)
        p.wait('gpsimd', SV, sv0)
        p.wait('sync', SV, sv0)
        lvs = {}
        mulv = {}

        def emit_load(k):
            sl = k % 4
            q = 'gpsimd' if k % 2 == 0 else 'sync'
            if k >= 4:
                p.wait(q, SV, mulv[k - 4])
            lvs[k] = p.op(q, lambda e, k=k, sl=sl, c0=c0, nt=nt: e.dma_start(
                out=xin[:, sl * 512:sl * 512 + nt],
                in_=src_hbm[k * 128:(k + 1) * 128, c0:c0 + nt]),
                sem=f'{pref}_L{sl}', n=16)
        for k in range(4):
            emit_load(k)
        for k in range(nk):
            sl = k % 4
            p.wait('vector', f'{pref}_L{sl}', lvs[k])
            p.op('vector', lambda e, sl=sl, nt=nt: e.tensor_sub(
                xin[:, sl * 512:sl * 512 + nt], xin[:, sl * 512:sl * 512 + nt],
                mrow[:, 0:nt]), sem=SP)
            mulv[k] = p.op('vector', lambda e, k=k, sl=sl, nt=nt, t0=t0:
                           e.tensor_tensor(
                lnx[:, k * T + t0:k * T + t0 + nt], xin[:, sl * 512:sl * 512 + nt],
                rrow[:, 0:nt], op=ALU.mult), sem=SV)
            if k + 4 < nk:
                emit_load(k + 4)
        pl_done = p.val(SP)
        ve_done.append(p.val(SV))
    return ve_done[-1]


from concourse.bass_utils import run_bass_kernel_spmd


def _sel_consts():
    sel8 = np.zeros((8, 8, 128), np.float32)
    for e in range(8):
        sel8[e, e, :] = 1.0
    sel16 = np.zeros((8, 128), np.float32)
    for m in range(128):
        sel16[m // 16, m] = 1.0
    return sel8, sel16


def _build_s1():
    p = P()
    xT = p.dram('xT', [2048, TK], F32, kind='ExternalInput')
    ident = p.dram('ident', [128, 128], F32, kind='ExternalInput')
    wqkv = p.dram('wqkv0T', [2048, 6144], F32, kind='ExternalInput')
    masks0 = p.dram('masks0', [17, 128, 512], BF16, kind='ExternalInput')
    attnT = p.dram('attnT0', [2048, TG], BF16, kind='ExternalOutput')
    init_gate(p)
    cx = common_alloc(p, ident)
    lnx = p.sbuf('lnx', [128, 16 * TK], BF16)
    ve = emit_ln(p, cx, xT, TK, lnx, 'ln0')
    p.wait('tensor', 'ln0_ve', ve)
    p.wait('vector', 'ln0_ve', ve)
    st = emit_l0_attn(p, cx, lnx, wqkv, masks0, attnT, 'a0')
    p.wait('gpsimd', 'a0_S0', st[0])
    p.wait('gpsimd', 'a0_S1', st[1])
    p.finalize_consts()
    return p.build()


def _build_s2():
    p = P()
    xT = p.dram('xT', [2048, TK], F32, kind='ExternalInput')
    ident = p.dram('ident', [128, 128], F32, kind='ExternalInput')
    wout = p.dram('wout0T', [2048, 2048], F32, kind='ExternalInput')
    attnT = p.dram('attnT0', [2048, TG], BF16, kind='ExternalInput')
    x1T = p.dram('x1T', [2048, TG], F32, kind='ExternalOutput')
    init_gate(p)
    cx = common_alloc(p, ident)
    ares = p.sbuf('ares', [128, 16 * TG], BF16)
    v = load_resident(p, attnT, TG, 16, ares, 'ALR')
    p.wait('tensor', 'ALR', v)
    o0, o1 = emit_outproj_residual(p, cx, 's2', ares, wout, xT, 576, x1T, TG)
    p.wait('gpsimd', 's2_O0', o0)
    p.wait('gpsimd', 's2_O1', o1)
    p.finalize_consts()
    return p.build()


def _build_s3():
    p = P()
    ident = p.dram('ident', [128, 128], F32, kind='ExternalInput')
    x1T = p.dram('x1T', [2048, TG], F32, kind='ExternalInput')
    w1 = p.dram('w1T', [9, 2048, 1024], F32, kind='ExternalInput')
    w3 = p.dram('w3T', [9, 2048, 1024], F32, kind='ExternalInput')
    w2 = p.dram('w2T', [9, 1024, 2048], F32, kind='ExternalInput')
    wg = p.dram('wg0T', [2048, 8], F32, kind='ExternalInput')
    selh = p.dram('sel8', [8, 8, 128], F32, kind='ExternalInput')
    x2T = p.dram('x2T', [2048, TG], F32, kind='ExternalOutput')
    init_gate(p)
    cx = common_alloc(p, ident)
    o0, o1 = emit_ffn(p, cx, 'f', x1T, w1, w3, w2, wg, selh, x2T)
    p.wait('gpsimd', 'f_O0', o0)
    p.wait('gpsimd', 'f_O1', o1)
    p.finalize_consts()
    return p.build()


def _build_s4():
    p = P()
    ident = p.dram('ident', [128, 128], F32, kind='ExternalInput')
    x2T = p.dram('x2T', [2048, TG], F32, kind='ExternalInput')
    wq = p.dram('wqkv1T', [2048, 6144], F32, kind='ExternalInput')
    bq = p.dram('bqkv', [128, 48], F32, kind='ExternalInput')
    rC = p.dram('ropeC', [128, TG], F32, kind='ExternalInput')
    rS = p.dram('ropeS', [128, TG], F32, kind='ExternalInput')
    pm = p.dram('perm', [128, 128], BF16, kind='ExternalInput')
    qk = p.dram('qkvT1', [6144, TG], BF16, kind='ExternalOutput')
    init_gate(p)
    cx = common_alloc(p, ident)
    o0, o1 = emit_l1qkv(p, cx, 's4', x2T, wq, bq, rC, rS, pm, qk)
    p.wait('gpsimd', 's4_O0', o0)
    p.wait('gpsimd', 's4_O1', o1)
    p.finalize_consts()
    return p.build()


def _build_s5():
    p = P()
    ident = p.dram('ident', [128, 128], F32, kind='ExternalInput')
    qk = p.dram('qkvT1', [6144, TG], BF16, kind='ExternalInput')
    mk1 = p.dram('masks1', [7, 128, 512], BF16, kind='ExternalInput')
    skT = p.dram('sinksT', [32, 64, 16], F32, kind='ExternalInput')
    ska = p.dram('sinks_aug', [32, 16, 65], F32, kind='ExternalInput')
    at1 = p.dram('attn1T', [2048, 2048], BF16, kind='ExternalOutput')
    init_gate(p)
    cx = common_alloc(p, ident)
    s0, s1 = emit_l1attn(p, cx, 's5', qk, mk1, skT, ska, at1)
    p.wait('gpsimd', 's5_S0', s0)
    p.wait('gpsimd', 's5_S1', s1)
    p.finalize_consts()
    return p.build()


def _build_s6():
    p = P()
    ident = p.dram('ident', [128, 128], F32, kind='ExternalInput')
    at1 = p.dram('attn1T', [2048, 2048], BF16, kind='ExternalInput')
    wo = p.dram('wo1T', [2048, 2048], F32, kind='ExternalInput')
    bo = p.dram('bo', [128, 16], F32, kind='ExternalInput')
    x2T = p.dram('x2T', [2048, TG], F32, kind='ExternalInput')
    x3T = p.dram('x3T', [2048, 2048], F32, kind='ExternalOutput')
    init_gate(p)
    cx = common_alloc(p, ident)
    ares = p.sbuf('ares', [128, 16 * 2048], BF16)
    v = load_resident(p, at1, 2048, 16, ares, 'ALR')
    p.wait('tensor', 'ALR', v)
    bo_sb = p.sbuf('bo_sb', [128, 16], F32)
    p.op('gpsimd', lambda e: e.dma_start(out=bo_sb[:], in_=bo[:]), sem='BO', n=16)
    p.wait('vector', 'BO', 16)
    o0, o1 = emit_outproj_residual(p, cx, 's6', ares, wo, x2T, 64, x3T, 2048,
                                   bias_sb=bo_sb)
    p.wait('gpsimd', 's6_O0', o0)
    p.wait('gpsimd', 's6_O1', o1)
    p.finalize_consts()
    return p.build()


def _build_s7():
    p = P()
    ident = p.dram('ident', [128, 128], F32, kind='ExternalInput')
    x3T = p.dram('x3T', [2048, 2048], F32, kind='ExternalInput')
    wd = p.dram('wd1T', [2048, 2048], F32, kind='ExternalInput')
    wg = p.dram('wg1T', [2048, 8], F32, kind='ExternalInput')
    Ap = p.dram('ApackT', [2048, 128], F32, kind='ExternalInput')
    Bp = p.dram('Bpack', [128, 2048], F32, kind='ExternalInput')
    s16 = p.dram('sel16', [8, 128], F32, kind='ExternalInput')
    out = p.dram('outT', [2048, 2048], F32, kind='ExternalOutput')
    init_gate(p)
    cx = common_alloc(p, ident)
    o0, o1 = emit_l1ffn(p, cx, 's7', x3T, wd, wg, Ap, Bp, s16, out)
    p.wait('gpsimd', 's7_O0', o0)
    p.wait('gpsimd', 's7_O1', o1)
    p.finalize_consts()
    return p.build()


def kernel_unfused(**inputs):
    ins = {k: np.asarray(v) for k, v in inputs.items()}
    sh, per = host_prep(ins)
    sel8, sel16 = _sel_consts()
    N = 8
    cores = list(range(N))

    def run(nc, mk):
        res = run_bass_kernel_spmd(nc, [mk(c) for c in cores], core_ids=cores)
        return res.results

    r = run(_build_s1(), lambda c: {
        'xT': per[c]['xT'], 'ident': sh['ident'], 'wqkv0T': sh['wqkv0T'],
        'masks0': per[c]['masks0']})
    attnT = [np.asarray(r[c]['attnT0']) for c in cores]
    r = run(_build_s2(), lambda c: {
        'xT': per[c]['xT'], 'ident': sh['ident'], 'wout0T': sh['wout0T'],
        'attnT0': attnT[c]})
    x1T = [np.asarray(r[c]['x1T']) for c in cores]
    r = run(_build_s3(), lambda c: {
        'ident': sh['ident'], 'x1T': x1T[c], 'w1T': sh['w1T'], 'w3T': sh['w3T'],
        'w2T': sh['w2T'], 'wg0T': sh['wg0T'], 'sel8': sel8})
    x2T = [np.asarray(r[c]['x2T']) for c in cores]
    r = run(_build_s4(), lambda c: {
        'ident': sh['ident'], 'x2T': x2T[c], 'wqkv1T': sh['wqkv1T'],
        'bqkv': sh['bqkv'], 'ropeC': per[c]['ropeC'], 'ropeS': per[c]['ropeS'],
        'perm': per[c]['perm']})
    qkvT1 = [np.asarray(r[c]['qkvT1']) for c in cores]
    r = run(_build_s5(), lambda c: {
        'ident': sh['ident'], 'qkvT1': qkvT1[c], 'masks1': per[c]['masks1'],
        'sinksT': sh['sinksT'], 'sinks_aug': sh['sinks_aug']})
    attn1T = [np.asarray(r[c]['attn1T']) for c in cores]
    r = run(_build_s6(), lambda c: {
        'ident': sh['ident'], 'attn1T': attn1T[c], 'wo1T': sh['wo1T'],
        'bo': sh['bo'], 'x2T': x2T[c]})
    x3T = [np.asarray(r[c]['x3T']) for c in cores]
    r = run(_build_s7(), lambda c: {
        'ident': sh['ident'], 'x3T': x3T[c], 'wd1T': sh['wd1T'],
        'wg1T': sh['wg1T'], 'ApackT': sh['ApackT'], 'Bpack': sh['Bpack'],
        'sel16': sel16})
    out = np.zeros((4, 4096, 2048), np.float32)
    for c in cores:
        b, h, s0, _, _ = core_geom(c)
        out[b, s0:s0 + 2048] = np.asarray(r[c]['outT'], np.float32).T
    return out



ARENA_F32 = 44928


class _Dummy:
    def __init__(self, name, shape, dtype):
        self.name, self.shape, self.dtype = name, list(shape), dtype

    def __getitem__(self, *a):
        return self

    def rearrange(self, *a, **k):
        return self

    def bitcast(self, *a):
        return self

    def then_inc(self, *a):
        return self


class DryP(P):

    def __init__(self):
        self.em = {e: [] for e in ENGS}
        self.semh = {}
        self.cnt = {}
        self.consts_needed = set()
        self.sbuf_log = []
        self._tcache = {}

    def S(self, name):
        name = self.canon(name)
        if name not in self.semh:
            self.semh[name] = name
            self.cnt[name] = 0
        return self.semh[name]

    def op(self, eng, f, sem=None, n=1):
        if sem is not None:
            self.S(sem)
            sem = self.canon(sem)
            self.cnt[sem] += n
            return self.cnt[sem]
        return None

    def wait(self, eng, sem, val):
        pass

    def barrier(self, engines=ENGS, name=None):
        pass

    def sbuf(self, name, shape, dtype):
        if name in self._tcache:
            return self._tcache[name]
        t = _Dummy(name, shape, dtype)
        self._tcache[name] = t
        self.sbuf_log.append((name, list(shape), dtype))
        return t

    def stage_mark(self):
        self.sbuf_log.append(('__STAGE__', None, None))

    def psum(self, name, shape, dtype=F32):
        return _Dummy(name, shape, dtype)

    def dram(self, name, shape, dtype, kind):
        return _Dummy(name, shape, dtype)


class ArenaP(P):

    def __init__(self, layout, shared):
        super().__init__()
        self.layout = layout
        self.shared = shared
        self.arena = None

    def make_arena(self):
        self.arena = self.nc.alloc_sbuf_tensor("ARENA", [128, ARENA_F32], F32)

    def sbuf(self, name, shape, dtype):
        if name in self.layout:
            off, sh, dt = self.layout[name]
            assert dt == dtype and list(sh) == list(shape), name
            if dtype == F32:
                return self.arena[0:shape[0], off:off + shape[1]]
            elif dtype == BF16:
                return self.arena.bitcast(BF16)[0:shape[0], 2 * off:2 * off + shape[1]]
            else:
                raise ValueError(dtype)
        return super().sbuf(name, shape, dtype)


def _elems_f32(shape, dtype):
    n = shape[1] * (2 if dtype == F32 else 1)
    return (n + 1) // 2 * 2


def _stage_layouts(stage_logs):
    layout = {}
    for log in stage_logs:
        off = 0
        for name, shape, dtype in log:
            nf32 = shape[1] if dtype == F32 else (shape[1] + 1) // 2
            nf32 = (nf32 + 3) // 4 * 4
            layout[name] = (off, shape, dtype)
            off += nf32
        assert off <= ARENA_F32, f"stage overflow {off}"
    return layout


def emit_all_stages(p, t, upto=7):
    cx = common_alloc(p, t['ident'])
    lnx = p.sbuf('lnx', [128, 16 * TK], BF16)
    ve = emit_ln(p, cx, t['xT'], TK, lnx, 'ln0')
    p.wait('tensor', 'ln0_ve', ve)
    p.wait('vector', 'ln0_ve', ve)
    import os as _os
    _nh = int(_os.environ.get('S1NH', '16'))
    _ph = _os.environ.get('S1PH', 'ABCD')
    st = emit_l0_attn(p, cx, lnx, t['wqkv0T'], t['masks0'], t['attnT0'], 'a0',
                      nheads=_nh, phases=_ph)
    if st[0]:
        p.wait('gpsimd', 'a0_S0', st[0])
    if st[1]:
        p.wait('gpsimd', 'a0_S1', st[1])
    p.barrier()
    p.stage_mark()
    if upto < 2:
        return
    ares = p.sbuf('ares', [128, 16 * TG], BF16)
    v = load_resident(p, t['attnT0'], TG, 16, ares, 'ALR')
    p.wait('tensor', 'ALR', v)
    o0, o1 = emit_outproj_residual(p, cx, 's2', ares, t['wout0T'], t['xT'], 576,
                                   t['x1T'], TG, stats_hbm=t['stat1'])
    p.wait('gpsimd', 's2_O0', o0)
    p.wait('gpsimd', 's2_O1', o1)
    p.barrier()
    p.stage_mark()
    if upto < 3:
        return
    o0, o1, o2 = emit_ffn(p, cx, 'f', t['x1T'], t['w1T'], t['w3T'], t['w2T'],
                          t['wg0T'], t['sel8'], t['x2T'], stats_in=t['stat1'],
                          stats_out=t['stat2'])
    p.wait('gpsimd', 'f_O0', o0)
    p.wait('gpsimd', 'f_O1', o1)
    p.wait('gpsimd', 'f_O2', o2)
    p.barrier()
    p.stage_mark()
    if upto < 4:
        return
    o0, o1 = emit_l1qkv(p, cx, 's4', t['x2T'], t['wqkv1T'], t['bqkv'],
                        t['ropeC'], t['ropeS'], t['perm'], t['qkvT1'],
                        stats_in=t['stat2'])
    p.wait('gpsimd', 's4_O0', o0)
    p.wait('gpsimd', 's4_O1', o1)
    p.barrier()
    p.stage_mark()
    if upto < 5:
        return
    s0, s1 = emit_l1attn(p, cx, 's5', t['qkvT1'], t['masks1'], t['sinksT'],
                         t['sinks_aug'], t['attn1T'])
    p.wait('gpsimd', 's5_S0', s0)
    p.wait('gpsimd', 's5_S1', s1)
    p.barrier()
    p.stage_mark()
    if upto < 6:
        return
    ares6 = p.sbuf('ares6', [128, 16 * 2048], BF16)
    v = load_resident(p, t['attn1T'], 2048, 16, ares6, 'ALR6')
    p.wait('tensor', 'ALR6', v)
    bo_sb = p.sbuf('bo_sb', [128, 16], F32)
    p.op('gpsimd', lambda e: e.dma_start(out=bo_sb[:], in_=t['bo'][:]),
         sem='BO', n=16)
    p.wait('vector', 'BO', 16)
    o0, o1 = emit_outproj_residual(p, cx, 's6', ares6, t['wo1T'], t['x2T'], 64,
                                   t['x3T'], 2048, bias_sb=bo_sb,
                                   stats_hbm=t['stat3'])
    p.wait('gpsimd', 's6_O0', o0)
    p.wait('gpsimd', 's6_O1', o1)
    p.barrier()
    p.stage_mark()
    if upto < 7:
        return
    o0, o1 = emit_l1ffn(p, cx, 's7', t['x3T'], t['wd1T'], t['wg1T'],
                        t['ApackT'], t['Bpack'], t['sel16'], t['outT'],
                        stats_in=t['stat3'])
    p.wait('gpsimd', 's7_O0', o0)
    p.wait('gpsimd', 's7_O1', o1)




def _declare_drams(p, ext_scratch=False):
    t = {}
    SK = 'ExternalOutput' if ext_scratch else 'Internal'
    t['ident'] = p.dram('ident', [128, 128], F32, kind='ExternalInput')
    t['xT'] = p.dram('xT', [2048, TK], F32, kind='ExternalInput')
    t['wqkv0T'] = p.dram('wqkv0T', [2048, 6144], F32, kind='ExternalInput')
    t['masks0'] = p.dram('masks0', [17, 128, 512], BF16, kind='ExternalInput')
    t['wout0T'] = p.dram('wout0T', [2048, 2048], F32, kind='ExternalInput')
    t['w1T'] = p.dram('w1T', [9, 2048, 1024], F32, kind='ExternalInput')
    t['w3T'] = p.dram('w3T', [9, 2048, 1024], F32, kind='ExternalInput')
    t['w2T'] = p.dram('w2T', [9, 1024, 2048], F32, kind='ExternalInput')
    t['wg0T'] = p.dram('wg0T', [2048, 8], F32, kind='ExternalInput')
    t['sel8'] = p.dram('sel8', [8, 8, 128], F32, kind='ExternalInput')
    t['wqkv1T'] = p.dram('wqkv1T', [2048, 6144], F32, kind='ExternalInput')
    t['bqkv'] = p.dram('bqkv', [128, 48], F32, kind='ExternalInput')
    t['ropeC'] = p.dram('ropeC', [128, TG], F32, kind='ExternalInput')
    t['ropeS'] = p.dram('ropeS', [128, TG], F32, kind='ExternalInput')
    t['perm'] = p.dram('perm', [128, 128], BF16, kind='ExternalInput')
    t['masks1'] = p.dram('masks1', [7, 128, 512], BF16, kind='ExternalInput')
    t['sinksT'] = p.dram('sinksT', [32, 64, 16], F32, kind='ExternalInput')
    t['sinks_aug'] = p.dram('sinks_aug', [32, 16, 65], F32, kind='ExternalInput')
    t['wo1T'] = p.dram('wo1T', [2048, 2048], F32, kind='ExternalInput')
    t['bo'] = p.dram('bo', [128, 16], F32, kind='ExternalInput')
    t['wd1T'] = p.dram('wd1T', [2048, 2048], F32, kind='ExternalInput')
    t['wg1T'] = p.dram('wg1T', [2048, 8], F32, kind='ExternalInput')
    t['ApackT'] = p.dram('ApackT', [2048, 128], F32, kind='ExternalInput')
    t['Bpack'] = p.dram('Bpack', [128, 2048], F32, kind='ExternalInput')
    t['sel16'] = p.dram('sel16', [8, 128], F32, kind='ExternalInput')
    t['stat1'] = p.dram('stat1', [2, TG], F32, kind='Internal')
    t['stat2'] = p.dram('stat2', [2, TG], F32, kind='Internal')
    t['stat3'] = p.dram('stat3', [2, 2048], F32, kind='Internal')
    t['attnT0'] = p.dram('attnT0', [2048, TG], BF16, kind=SK)
    t['x1T'] = p.dram('x1T', [2048, TG], F32, kind=SK)
    t['x2T'] = p.dram('x2T', [2048, TG], F32, kind=SK)
    t['qkvT1'] = p.dram('qkvT1', [6144, TG], BF16, kind=SK)
    t['attn1T'] = p.dram('attn1T', [2048, 2048], BF16, kind=SK)
    t['x3T'] = p.dram('x3T', [2048, 2048], F32, kind=SK)
    t['outT'] = p.dram('outT', [2048, 2048], F32, kind='ExternalOutput')
    return t


def build_fused(upto=7, ext_scratch=False):
    dp = DryP()
    td = _declare_drams(dp)
    emit_all_stages(dp, td, upto)
    logs = [[]]
    shared = set()
    for name, shape, dtype in dp.sbuf_log:
        if name == '__STAGE__':
            logs.append([])
            continue
        if name.startswith('cx_') or name.startswith('lnsc_') or \
           name.startswith('gtsc_') or name == 'c_consts':
            shared.add(name)
            continue
        logs[-1].append((name, shape, dtype))
    layout = _stage_layouts(logs)
    p = ArenaP(layout, shared)
    init_gate(p)
    p.make_arena()
    t = _declare_drams(p, ext_scratch)
    emit_all_stages(p, t, upto)
    p.finalize_consts()
    return p


def kernel_fused(**inputs):
    ins = {k: np.asarray(v) for k, v in inputs.items()}
    sh, per = host_prep(ins)
    sel8, sel16 = _sel_consts()
    p = build_fused()
    nc = p.build()
    cores = list(range(8))
    in_maps = []
    for c in cores:
        m = {'ident': sh['ident'], 'xT': per[c]['xT'], 'wqkv0T': sh['wqkv0T'],
             'masks0': per[c]['masks0'], 'wout0T': sh['wout0T'],
             'w1T': sh['w1T'], 'w3T': sh['w3T'], 'w2T': sh['w2T'],
             'wg0T': sh['wg0T'], 'sel8': sel8, 'wqkv1T': sh['wqkv1T'],
             'bqkv': sh['bqkv'], 'ropeC': per[c]['ropeC'],
             'ropeS': per[c]['ropeS'], 'perm': per[c]['perm'],
             'masks1': per[c]['masks1'], 'sinksT': sh['sinksT'],
             'sinks_aug': sh['sinks_aug'], 'wo1T': sh['wo1T'], 'bo': sh['bo'],
             'wd1T': sh['wd1T'], 'wg1T': sh['wg1T'], 'ApackT': sh['ApackT'],
             'Bpack': sh['Bpack'], 'sel16': sel16}
        in_maps.append(m)
    res = run_bass_kernel_spmd(nc, in_maps, core_ids=cores)
    out = np.zeros((4, 4096, 2048), np.float32)
    for c in cores:
        b, h, s0, _, _ = core_geom(c)
        out[b, s0:s0 + 2048] = np.asarray(res.results[c]['outT'], np.float32).T
    return out


kernel = kernel_fused
